# version 03
# baseline (speedup 1.0000x reference)
import numpy as np
from contextlib import ExitStack

import concourse.bass as bass
import concourse.mybir as mybir
from concourse.bass_utils import run_bass_kernel_spmd

F32 = mybir.dt.float32
BF16 = mybir.dt.bfloat16
I32 = mybir.dt.int32
OP = mybir.AluOpType
AF = mybir.ActivationFunctionType
AX = mybir.AxisListType

NCORES = 8
D = 1024
HEADS = 16
BPC = 2
EPS = 1e-5
QMAX = 127.0
THRESH = 0.7

KVSLOT = [0, 65, 130, 195, 260, 325, 390, 512]
AOBANK = [(0, 7), (7, 14), (14, 16)]


class Ctx:
    def __init__(self, nc, semnames):
        self.nc = nc
        self.semnames = list(semnames)
        self.sems = {}
        self.c = {k: 0 for k in semnames}
        self.marks = {}
        self.mode = None
        self.eng = None
        self._waited = {}

    def snap(self):
        return (dict(self.c), dict(self.marks))

    def restore(self, s):
        self.c = dict(s[0])
        self.marks = dict(s[1])
        self._waited = {}

    def set_eng(self, mode, eng):
        self.mode = mode
        self.eng = eng
        self._waited = {}


class Px:
    def __init__(self, ctx, name, sem):
        self.ctx = ctx
        self.name = name
        self.sem = sem

    @property
    def live(self):
        return self.ctx.mode == self.name

    def w(self, label, extra=0):
        if LENIENT and label not in self.ctx.marks:
            return
        sem, val = self.ctx.marks[label]
        val += extra
        if val <= 0:
            return
        key = (self.name, sem)
        if self.ctx._waited.get(key, -1) >= val:
            return
        self.ctx._waited[key] = val
        if self.live:
            self.ctx.eng.wait_ge(self.ctx.sems[sem], val)

    def do(self, opname, *a, **k):
        if self.live:
            return getattr(self.ctx.eng, opname)(*a, **k)
        return None

    def doi(self, opname, *a, label=None, inc=1, **k):
        inst = self.do(opname, *a, **k)
        self.ctx.c[self.sem] += inc
        if label is not None:
            self.ctx.marks[label] = (self.sem, self.ctx.c[self.sem])
        if self.live:
            inst.then_inc(self.ctx.sems[self.sem], inc)
        return inst

    def dma(self, out, in_, label=None, sem=None):
        if sem is None:
            return self.doi("dma_start", out, in_, label=label, inc=16)
        old = self.sem
        self.sem = sem
        try:
            return self.doi("dma_start", out, in_, label=label, inc=16)
        finally:
            self.sem = old


STOP_AFTER = None
DEBUG_XOUT = False
DEBUG_DUMP = None
B_LIMIT = 99
Q_LIMIT = 99
LENIENT = False


def build_nc(S=2048):
    NT = S // 128
    SG = 512 if S >= 512 else S
    NG = S // SG
    SUBS = SG // 128

    nc = bass.Bass("TRN2", target_bir_lowering=False,
                   detect_race_conditions=False)

    h_d = nc.dram_tensor("h", [BPC, S, D], F32, kind="ExternalInput")
    wqkvT_d = nc.dram_tensor("wqkvT", [D, 3 * D], F32, kind="ExternalInput")
    bqkv_d = nc.dram_tensor("bqkv", [1, 3 * D], F32, kind="ExternalInput")
    wprojT_d = nc.dram_tensor("wprojT", [D, D], F32, kind="ExternalInput")
    bproj_d = nc.dram_tensor("bproj", [1, D], F32, kind="ExternalInput")
    g1_d = nc.dram_tensor("g1", [1, D], F32, kind="ExternalInput")
    be1_d = nc.dram_tensor("be1", [1, D], F32, kind="ExternalInput")
    wffnT_d = nc.dram_tensor("wffnT", [D, D], F32, kind="ExternalInput")
    g2_d = nc.dram_tensor("g2", [1, D], F32, kind="ExternalInput")
    be2_d = nc.dram_tensor("be2", [1, D], F32, kind="ExternalInput")
    idn_d = nc.dram_tensor("idn", [128, 128], F32, kind="ExternalInput")
    out_d = nc.dram_tensor("out", [BPC, S, D], F32, kind="ExternalOutput")
    xkind = "ExternalOutput" if DEBUG_XOUT else "Internal"
    mixs_d = nc.dram_tensor("mixs", [BPC * S, D], BF16)
    x1s_d = nc.dram_tensor("x1s", [BPC * S, D], BF16, kind=xkind)
    xns_d = nc.dram_tensor("xns", [BPC * S, D], BF16, kind=xkind)
    dbg_d = (nc.dram_tensor("dbg", [BPC * S, 4 * D], F32, kind="ExternalOutput")
             if DEBUG_DUMP else None)
    ccin_d = nc.dram_tensor("ccin", [1, 8], F32)
    ccout_d = nc.dram_tensor("ccout", [1, 8], F32)

    sems = ["dh", "dh0", "dh1", "dso", "ds0", "ds1",
            "dm0", "dm1", "dg", "pe", "ve", "se", "ge"]
    ctx = Ctx(nc, sems)
    SY = Px(ctx, "sync", "dh")
    TE = Px(ctx, "tensor", "pe")
    VE = Px(ctx, "vector", "ve")
    SE = Px(ctx, "scalar", "se")
    GE = Px(ctx, "gpsimd", "ge")
    SCD = Px(ctx, "scalar", "dsc")
    GED = Px(ctx, "gpsimd", "dg")

    def emit_block(body):
        snap = ctx.snap()
        with nc.Block() as blk:
            for engname, attr in [("sync", "sync"), ("tensor", "tensor"),
                                  ("vector", "vector"), ("scalar", "scalar"),
                                  ("gpsimd", "gpsimd")]:
                def run(eng, engname=engname):
                    ctx.restore(snap)
                    ctx.set_eng(engname, eng)
                    body()
                getattr(blk, attr)(run)

    st = ExitStack()
    with st:
        for name in sems:
            ctx.sems[name] = st.enter_context(nc.semaphore(name))

        def sbuf(stk, name, shape, dt=F32):
            return stk.enter_context(nc.sbuf_tensor(name, shape, dt))

        def psum(stk, name, shape, dt=F32):
            return stk.enter_context(nc.psum_tensor(name, shape, dt))

        wq = sbuf(st, "wq", [128, 8 * 3 * D], BF16)
        wp = sbuf(st, "wp", [128, 8 * D], BF16)
        bq_row = sbuf(st, "bq_row", [1, 3 * D], BF16)
        bp_row = sbuf(st, "bp_row", [1, D], BF16)
        g1bc = sbuf(st, "g1bc", [128, D], BF16)
        b1bc = sbuf(st, "b1bc", [128, D], BF16)
        g2bc = sbuf(st, "g2bc", [128, D], BF16)
        b2bc = sbuf(st, "b2bc", [128, D], BF16)
        idn32 = sbuf(st, "idn32", [128, 128], F32)
        idnbf = sbuf(st, "idnbf", [128, 128], BF16)
        ones_row_bf = sbuf(st, "ones_row_bf", [1, 128], BF16)
        ones_row_f = sbuf(st, "ones_row_f", [1, 128], F32)
        ones_col_f = sbuf(st, "ones_col_f", [128, 1], F32)
        ones11 = sbuf(st, "ones11", [1, 1], F32)
        negcol = sbuf(st, "negcol", [128, 1], F32)
        bqcol = sbuf(st, "bqcol", [128, 8], F32)
        bqncol = sbuf(st, "bqncol", [128, 8], F32)
        amax_cols = sbuf(st, "amax_cols", [128, BPC * NT], F32)
        inv_col = sbuf(st, "inv_col", [128, 1], F32)
        fs_col = sbuf(st, "fs_col", [128, 1], F32)
        cc_sc = sbuf(st, "cc_sc", [1, 8], F32)
        alpha11 = sbuf(st, "alpha11", [1, 1], F32)

        def prep_phase():
          with ExitStack() as pst:
            pp = psum(pst, "pp_prep", [128, 512], F32)
            pp2 = psum(pst, "pp_prep2", [128, 512], F32)
            bq_f32 = sbuf(pst, "bq_f32", [1, 3 * D], F32)
            vrow_f = sbuf(pst, "vrow_f", [1, 4 * D], F32)

            def prep():
                for d_ in range(8):
                    GED.dma(wq[:, d_ * 3 * D:(d_ + 1) * 3 * D],
                            wqkvT_d[d_ * 128:(d_ + 1) * 128, :],
                            label=f"p_wq{d_}")
                for d_ in range(8):
                    GED.dma(wp[:, d_ * D:(d_ + 1) * D],
                            wprojT_d[d_ * 128:(d_ + 1) * 128, :],
                            label=f"p_wp{d_}")
                GED.dma(bq_row[:, :], bqkv_d[:, :], label="p_bqr")
                GED.dma(bp_row[:, :], bproj_d[:, :], label="p_bpr")
                GED.dma(idnbf[:, :], idn_d[:, :], label="p_idnb")
                SY.dma(bq_f32[:, :], bqkv_d[:, :], label="p_bqf")
                SY.dma(idn32[:, :], idn_d[:, :], label="p_idn32")
                SY.dma(vrow_f[:, 0:D], g1_d[:, :], label="p_g1")
                SY.dma(vrow_f[:, D:2 * D], be1_d[:, :], label="p_b1")
                SY.dma(vrow_f[:, 2 * D:3 * D], g2_d[:, :], label="p_g2")
                SY.dma(vrow_f[:, 3 * D:4 * D], be2_d[:, :], label="p_b2")
                GE.doi("memset", ones_row_bf[:, :], 1.0)
                GE.doi("memset", ones_row_f[:, :], 1.0)
                GE.doi("memset", ones_col_f[:, :], 1.0)
                GE.doi("memset", ones11[:, :], 1.0)
                GE.doi("memset", negcol[:, :], -1.0)
                GE.doi("memset", cc_sc[:, :], 0.0)
                GE.doi("memset", amax_cols[:, :], 0.0, label="p_ms")

                if True:
                    VE.w("p_ms")
                    for _ in range(4):
                        VE.doi("reciprocal", inv_col[:, 0:1], ones_col_f[:, :])
                    TE.w("p_b2")
                    TE.w("p_idnb")
                    TE.w("p_ms")
                    for e in range(8):
                        TE.doi("matmul", pp[:, e:e + 1],
                               bq_f32[:, e * 128:(e + 1) * 128],
                               ones11[:, :], start=True, stop=True,
                               label=f"p_bqc{e}")
                    VE.w("p_bqc7")
                    VE.doi("tensor_copy", bqcol[:, :], pp[:, 0:8], label="p_bqcc")
                    VE.doi("tensor_scalar", bqncol[:, :], bqcol[:, :], -1.0, None,
                           OP.mult, label="p_bqnc")
                    dsts = [g1bc, b1bc, g2bc, b2bc]
                    labs = ["p_g1", "p_b1", "p_g2", "p_b2"]
                    for i in range(4):
                        TE.w(labs[i])
                        TE.w("p_bqcc" if i == 0 else f"p_bcc{i-1}_0")
                        TE.doi("matmul", pp[:, 0:512], ones_row_f[:, :],
                               vrow_f[:, i * D: i * D + 512],
                               start=True, stop=True, label=f"p_bc{i}_0")
                        if i > 0:
                            TE.w(f"p_bcc{i-1}_1")
                        TE.doi("matmul", pp2[:, 0:512], ones_row_f[:, :],
                               vrow_f[:, i * D + 512: i * D + 1024],
                               start=True, stop=True, label=f"p_bc{i}_1")
                        VE.w(f"p_bc{i}_0")
                        VE.doi("tensor_copy", dsts[i][:, 0:512], pp[:, 0:512],
                               label=f"p_bcc{i}_0")
                        VE.w(f"p_bc{i}_1")
                        VE.doi("tensor_copy", dsts[i][:, 512:1024], pp2[:, 0:512],
                               label=f"p_bcc{i}_1")

            emit_block(prep)
        prep_phase()

        stab = ExitStack()
        ht = sbuf(stab, "ht", [128, 8 * S], BF16)
        h32 = [sbuf(stab, f"h32_{i}", [128, D], F32) for i in range(2)]
        phik = [sbuf(stab, f"phik{i}", [128, D], BF16) for i in range(2)]
        vbf = [sbuf(stab, f"vbf{i}", [128, HEADS * 65], BF16) for i in range(2)]
        kvst = sbuf(stab, "kvst", [128, 1024], BF16)
        kvlo = sbuf(stab, "kvlo", [64, 1024], BF16)
        phiq = sbuf(stab, "phiq", [128, 8 * SG], BF16)
        phiqlo = sbuf(stab, "phiqlo", [64, 8 * SG], BF16)
        qex = [sbuf(stab, f"qex{i}", [128, SG], F32) for i in range(2)]
        mixb = [sbuf(stab, f"mixb{i}", [128, D], BF16) for i in range(2)]
        mixb2 = [sbuf(stab, f"mixb2_{i}", [128, D], BF16) for i in range(2)]
        mixT = [sbuf(stab, f"mixT{i}", [128, D], BF16) for i in range(2)]
        S0 = [sbuf(stab, f"S0_{i}", [128, D], F32) for i in range(2)]
        Z0 = [sbuf(stab, f"Z0_{i}", [128, D], F32) for i in range(2)]
        TRH = sbuf(stab, "TRH", [128, D], F32)
        X1b = [sbuf(stab, f"X1b{i}", [128, D], BF16) for i in range(2)]
        XNb = [sbuf(stab, f"XNb{i}", [128, D], BF16) for i in range(2)]
        dstg = sbuf(stab, "dstg", [128, 1024], F32)
        k1blk = sbuf(stab, "k1blk", [128, 16], BF16)
        qk1c = sbuf(stab, "qk1c", [128, HEADS], F32)
        recq = sbuf(stab, "recq", [128, HEADS], F32)
        colsc = sbuf(stab, "colsc", [128, 12], F32)

        def phase_a(b):
            with ExitStack() as pst:
                pT = psum(pst, f"pT_{b}", [128, 1024], F32)
                pK = psum(pst, f"pK_{b}", [128, 1024], F32)
                pV = psum(pst, f"pV_{b}", [128, 1024], F32)
                pacc = psum(pst, f"pacc_{b}", [128, 1024], F32)

                def body():
                    VE.doi("memset", pacc[:, :], 0.0, label=f"a_pz_{b}")
                    for i in range(2):
                        GE.doi("memset",
                               bass.AP(vbf[i], 64,
                                       [[HEADS * 65, 128], [65, HEADS], [1, 1]]),
                               1.0, label=f"a_vb1_{b}_{i}")

                    def load_h(t):
                        if t >= 2:
                            SY.w(f"a_tr_{b}_{t-2}")
                        SY.dma(h32[t % 2][:, :],
                               h_d[b, t * 128:(t + 1) * 128, :],
                               label=f"a_h_{b}_{t}", sem=f"dh{t % 2}")

                    def trans(t):
                        TE.w(f"a_h_{b}_{t}")
                        if t == 0:
                            TE.w("p_b2")
                        if t >= 1:
                            TE.w(f"a_evac_{b}_{t-1}")
                        for c in range(8):
                            TE.doi("transpose", pT[:, c * 128:(c + 1) * 128],
                                   h32[t % 2][:, c * 128:(c + 1) * 128],
                                   idn32[:, :],
                                   label=(f"a_tr_{b}_{t}" if c == 7 else None))

                    def evac(t):
                        VE.w(f"a_tr_{b}_{t}")
                        dst = bass.AP(ht, t * 128, [[8 * S, 128], [S, 8], [1, 128]])
                        src = bass.AP(pT, 0, [[1024, 128], [128, 8], [1, 128]])
                        VE.doi("tensor_copy", dst, src, label=f"a_evac_{b}_{t}")

                    def kvmm(t):
                        if t == 0:
                            TE.w("p_idnb")
                            TE.w("p_ms")
                        TE.w(f"a_evac_{b}_{t}")
                        if t >= 1:
                            TE.w(f"a_t1_{b}_{t-1}")
                            TE.w(f"a_vcopy_{b}_{t-1}")
                        for d_ in range(8):
                            lw = ht[:, d_ * S + t * 128: d_ * S + (t + 1) * 128]
                            for g_ in range(2):
                                TE.do("matmul", pK[:, g_ * 512:(g_ + 1) * 512], lw,
                                      wq[:, d_ * 3 * D + D + g_ * 512:
                                         d_ * 3 * D + D + (g_ + 1) * 512],
                                      start=(d_ == 0), stop=False)
                            for g_ in range(2):
                                TE.do("matmul", pV[:, g_ * 512:(g_ + 1) * 512], lw,
                                      wq[:, d_ * 3 * D + 2 * D + g_ * 512:
                                         d_ * 3 * D + 2 * D + (g_ + 1) * 512],
                                      start=(d_ == 0), stop=False)
                        for g_ in range(2):
                            TE.do("matmul", pK[:, g_ * 512:(g_ + 1) * 512],
                                  ones_row_bf[:, :],
                                  bq_row[:, D + g_ * 512: D + (g_ + 1) * 512],
                                  start=False, stop=True)
                        for g_ in range(2):
                            TE.doi("matmul", pV[:, g_ * 512:(g_ + 1) * 512],
                                   ones_row_bf[:, :],
                                   bq_row[:, 2 * D + g_ * 512:
                                          2 * D + (g_ + 1) * 512],
                                   start=False, stop=True,
                                   label=(f"a_kv_{b}_{t}" if g_ == 1 else None))

                    def phi(t):
                        if t == 0:
                            SE.w("p_ms")
                        SE.w(f"a_kv_{b}_{t}")
                        if t >= 2:
                            SE.w(f"a_acc_{b}_{t-2}")
                            SE.w(f"a_phi_{b}_{t-2}")
                        SE.do("activation", phik[t % 2][:, :], pK[:, :], AF.Relu)
                        SE.do("activation", TRH[:, :], pK[:, :], AF.Relu,
                              scale=negcol[:, 0:1])
                        SE.doi("activation", Z0[t % 2][:, :], TRH[:, :], AF.Exp,
                               scale=negcol[:, 0:1], label=f"a_t1_{b}_{t}")
                        VE.w(f"a_t1_{b}_{t}")
                        if t >= 2:
                            VE.w(f"a_acc_{b}_{t-2}")
                        VE.doi("tensor_tensor", phik[t % 2][:, :],
                               phik[t % 2][:, :], Z0[t % 2][:, :], OP.add,
                               label=f"a_phi_{b}_{t}")
                        VE.w(f"a_kv_{b}_{t}")
                        if t >= 2:
                            VE.w(f"a_acc_{b}_{t-2}")
                        if t == 0:
                            VE.w(f"a_vb1_{b}_1")
                        vdst = bass.AP(vbf[t % 2], 0,
                                       [[HEADS * 65, 128], [65, HEADS], [1, 64]])
                        vsrc = bass.AP(pV, 0, [[1024, 128], [64, HEADS], [1, 64]])
                        VE.doi("tensor_copy", vdst, vsrc, label=f"a_vcopy_{b}_{t}")

                    def kvacc(t):
                        TE.w(f"a_phi_{b}_{t}")
                        TE.w(f"a_vcopy_{b}_{t}")
                        if t == 0:
                            TE.w(f"a_pz_{b}")
                        for hh in range(HEADS):
                            par = hh % 2
                            slot = KVSLOT[hh // 2]
                            tp = (0, 64) if par else (0, 0)
                            TE.doi("matmul",
                                   pacc[par * 64:par * 64 + 64, slot:slot + 65],
                                   phik[t % 2][:, hh * 64:(hh + 1) * 64],
                                   vbf[t % 2][:, hh * 65:(hh + 1) * 65],
                                   start=False, stop=False,
                                   skip_group_check=True,
                                   tile_position=tp,
                                   label=(f"a_acc_{b}_{t}"
                                          if hh == HEADS - 1 else None))

                    load_h(0)
                    trans(0)
                    evac(0)
                    for t in range(NT):
                        if t + 1 < NT:
                            load_h(t + 1)
                        kvmm(t)
                        if t + 1 < NT:
                            trans(t + 1)
                            evac(t + 1)
                        phi(t)
                        if t >= 1:
                            kvacc(t - 1)
                    kvacc(NT - 1)
                    VE.w(f"a_acc_{b}_{NT-1}")
                    VE.doi("tensor_copy", kvst[:, :], pacc[:, :],
                           label=f"a_kvev_{b}")
                    VE.do("drain")
                    VE.do("memset", k1blk[:, :], 0.0)
                    VE.do("drain")
                    for e_ in range(8):
                        VE.do("tensor_copy",
                              k1blk[0:64, 2 * e_:2 * e_ + 1],
                              kvst[0:64, KVSLOT[e_] + 64:KVSLOT[e_] + 65])
                        VE.doi("tensor_copy",
                               k1blk[64:128, 2 * e_ + 1:2 * e_ + 2],
                               kvst[64:128, KVSLOT[e_] + 64:KVSLOT[e_] + 65],
                               label=(f"a_k1b_{b}" if e_ == 7 else None))
                    GE.w(f"a_kvev_{b}")
                    GED.dma(kvlo[:, :], kvst[64:128, :], label=f"a_kvlo_{b}")
                    if DEBUG_DUMP in ("kv", "all"):
                        GED.dma(dbg_d[512 + b * 128:512 + (b + 1) * 128, 0:1024],
                                kvst[:, :], label=f"dbgkv_{b}")

                emit_block(body)

        def phase_b(b):
            with ExitStack() as pst:
                pQ = [psum(pst, f"pQ{i}_{b}", [128, 512], F32) for i in range(2)]
                pO = [psum(pst, f"pO{i}_{b}", [128, 512], F32) for i in range(2)]
                pQk = psum(pst, f"pQk_{b}", [128, 16], F32)
                pT2 = psum(pst, f"pT2_{b}", [128, 512], BF16)
                pPr = psum(pst, f"pPr_{b}", [128, 1024], F32)

                def body():
                    def qproj(g):
                        TE.w("p_idnb")
                        SE.w("p_bqnc")
                        SE.w("p_ms")
                        for e in range(8):
                            if g > 0 or e >= 2:
                                pg, pb = (g, e - 2) if e >= 2 else (g - 1, e + 6)
                                TE.w(f"b_qphi_{b}_{pg}_{pb}")
                                TE.w(f"b_qt1_{b}_{pg}_{pb}")
                            for d_ in range(8):
                                TE.doi("matmul", pQ[e % 2][:, 0:SG],
                                       wq[:, d_ * 3 * D + e * 128:
                                          d_ * 3 * D + (e + 1) * 128],
                                       ht[:, d_ * S + g * SG:
                                          d_ * S + g * SG + SG],
                                       start=(d_ == 0), stop=(d_ == 7),
                                       label=(f"b_q_{b}_{g}_{e}"
                                              if d_ == 7 else None))
                            if Q_LIMIT < 2:
                                continue
                            VE.w(f"b_q_{b}_{g}_{e}")
                            if g > 0:
                                VE.w(f"b_ao_{b}_{g * SUBS - 1}")
                            VE.doi("tensor_scalar",
                                   phiq[:, e * SG:(e + 1) * SG],
                                   pQ[e % 2][:, 0:SG],
                                   bqcol[:, e:e + 1], 0.0, OP.add, OP.max,
                                   label=f"b_qu_{b}_{g}_{e}")
                            if Q_LIMIT < 3:
                                continue
                            SE.w(f"b_q_{b}_{g}_{e}")
                            SE.w(f"b_qu_{b}_{g}_{e}")
                            if g > 0 or e >= 2:
                                pg, pb = (g, e - 2) if e >= 2 else (g - 1, e + 6)
                                SE.w(f"b_qphi_{b}_{pg}_{pb}")
                            SE.do("activation", TRH[:, 0:SG], pQ[e % 2][:, 0:SG],
                                  AF.Relu, scale=negcol[:, 0:1],
                                  bias=bqncol[:, e:e + 1])
                            SE.doi("activation", qex[e % 2][:, 0:SG],
                                   TRH[:, 0:SG], AF.Exp, scale=negcol[:, 0:1],
                                   label=f"b_qt1_{b}_{g}_{e}")
                            if Q_LIMIT < 4:
                                continue
                            VE.w(f"b_qt1_{b}_{g}_{e}")
                            VE.doi("tensor_tensor",
                                   phiq[:, e * SG:(e + 1) * SG],
                                   phiq[:, e * SG:(e + 1) * SG],
                                   qex[e % 2][:, 0:SG], OP.add,
                                   label=f"b_qphi_{b}_{g}_{e}")
                        if Q_LIMIT >= 5:
                            GE.w(f"b_qphi_{b}_{g}_7")
                            if g > 0:
                                GE.w(f"b_ao_{b}_{g * SUBS - 1}")
                            GED.dma(phiqlo[:, :], phiq[64:128, :],
                                    label=f"b_qlo_{b}_{g}")
                            if DEBUG_DUMP in ("phiq", "all") and g == 0:
                                GED.dma(dbg_d[b * 256:b * 256 + 128, 0:8 * SG],
                                        phiq[:, :], label=f"dbgq_{b}")
                                GED.dma(dbg_d[b * 256 + 128:b * 256 + 192,
                                              0:8 * SG],
                                        phiqlo[:, :], label=f"dbgql_{b}")

                    def attnout(t):
                        g, s_ = divmod(t, SUBS)
                        TE.w(f"b_qlo_{b}_{g}")
                        TE.w(f"a_kvlo_{b}")
                        TE.w(f"a_kvev_{b}")
                        TE.w(f"a_k1b_{b}")
                        if t >= 1:
                            TE.w(f"b_div_{b}_{t-1}")
                            if DEBUG_DUMP == "post":
                                TE.w(f"pd_{b}_{t-1}")
                        for hh in range(HEADS):
                            e = hh // 2
                            src = phiq if hh % 2 == 0 else phiqlo
                            lw = src[0:64, e * SG + s_ * 128:
                                     e * SG + (s_ + 1) * 128]
                            kvsrc = kvst if hh % 2 == 0 else kvlo
                            rhs = kvsrc[0:64, KVSLOT[e]:KVSLOT[e] + 64]
                            bank, off = divmod(hh, 8)
                            TE.doi("matmul", pO[bank][:, off * 64:off * 64 + 64],
                                   lw, rhs, start=True, stop=True)
                        for e in range(8):
                            TE.doi("matmul", pQk[:, 2 * e:2 * e + 2],
                                   phiq[:, e * SG + s_ * 128:
                                        e * SG + (s_ + 1) * 128],
                                   k1blk[:, 2 * e:2 * e + 2],
                                   start=True, stop=True)
                        TE.doi("drain", label=f"b_ao_{b}_{t}")

                    def division(t):
                        VE.w(f"b_ao_{b}_{t}")
                        if t >= 2:
                            VE.w(f"b_ms_{b}_{t-2}")
                        VE.doi("tensor_scalar", qk1c[:, :], pQk[:, :],
                               1e-6, None, OP.max, label=f"rq1_{b}_{t}")
                        VE.w(f"rq1_{b}_{t}")
                        VE.doi("reciprocal", recq[:, :], qk1c[:, :],
                               label=f"rq2_{b}_{t}")
                        VE.w(f"rq2_{b}_{t}")
                        for bi in range(2):
                            src = bass.AP(pO[bi], 0, [[512, 128], [64, 8], [1, 64]])
                            rb = bass.AP(recq, bi * 8, [[HEADS, 128], [1, 8], [0, 64]])
                            dst = bass.AP(mixb[t % 2], bi * 512,
                                          [[D, 128], [64, 8], [1, 64]])
                            VE.doi("tensor_tensor", dst, src, rb, OP.mult,
                                   label=(f"b_div_{b}_{t}" if bi == 1 else None))
                        if DEBUG_DUMP == "post":
                            tgd = b * NT + t
                            VE.doi("tensor_copy", dstg[:, 0:16], pQk[:, :])
                            VE.doi("tensor_copy", dstg[:, 16:32], qk1c[:, :])
                            VE.doi("tensor_copy", dstg[:, 32:48], recq[:, :],
                                   label=f"pd_{b}_{t}")
                            SY.w(f"pd_{b}_{t}")
                            SY.dma(dbg_d[tgd * 128:(tgd + 1) * 128, 1024:1072],
                                   dstg[:, 0:48], label=f"pdd_{b}_{t}",
                                   sem="dso")
                        SY.w(f"b_div_{b}_{t}")
                        for hh in range(HEADS):
                            dstr = b * S + hh * (S // 16) + 8 * t
                            SY.dma(mixs_d[dstr:dstr + 8, :],
                                   mixb[t % 2][:, hh * 64:(hh + 1) * 64],
                                   label=(f"b_ms_{b}_{t}"
                                          if hh == HEADS - 1 else None),
                                   sem=f"dm{t % 2}")

                    def mixload(t):
                        if t >= 2:
                            SY.w(f"b_z_{b}_{t-2}")
                        SY.dma(h32[t % 2][:, :],
                               h_d[b, t * 128:(t + 1) * 128, :],
                               label=f"b_h_{b}_{t}", sem=f"dh{t % 2}")
                        if t >= 2:
                            SY.w(f"b_tr_{b}_{t-2}_1")
                        SY.w(f"b_ms_{b}_{NT-1}")
                        SY.w(f"b_ms_{b}_{NT-2}")
                        SY.dma(mixb2[t % 2][:, :],
                               mixs_d[b * S + t * 128:b * S + (t + 1) * 128, :],
                               label=f"b_ml_{b}_{t}", sem=f"dm{t % 2}")

                    def mixtrans(t):
                        if DEBUG_DUMP == "mix2":
                            tgd = b * NT + t
                            GE.w(f"b_ml_{b}_{t}")
                            GED.dma(dbg_d[tgd * 128:(tgd + 1) * 128, 0:1024],
                                    mixb2[t % 2][:, :], label=f"dmpm_{b}_{t}")
                        TE.w("p_idnb")
                        for r in range(2):
                            TE.w(f"b_ml_{b}_{t}")
                            if r == 1:
                                TE.w(f"b_mev_{b}_{t}_0")
                            elif t >= 1:
                                TE.w(f"b_mev_{b}_{t-1}_1")
                            for c in range(4):
                                cc = r * 4 + c
                                TE.doi("transpose",
                                       pT2[:, c * 128:(c + 1) * 128],
                                       mixb2[t % 2][:, cc * 128:(cc + 1) * 128],
                                       idnbf[:, :],
                                       label=(f"b_tr_{b}_{t}_{r}"
                                              if c == 3 else None))
                            px = VE if r == 0 else SE
                            px.w(f"b_tr_{b}_{t}_{r}")
                            dst = bass.AP(mixT[t % 2], r * 4 * 128,
                                          [[D, 128], [128, 4], [1, 128]])
                            src = bass.AP(pT2, 0, [[512, 128], [128, 4], [1, 128]])
                            px.doi("tensor_copy" if r == 0 else "copy", dst, src,
                                   label=f"b_mev_{b}_{t}_{r}")

                    def proj(t):
                        TE.w("p_idnb")
                        TE.w(f"b_mev_{b}_{t}_0")
                        TE.w(f"b_mev_{b}_{t}_1")
                        if t >= 1:
                            TE.w(f"b_pcopy_{b}_{t-1}")
                        pp = pPr
                        for d_ in range(8):
                            lw = mixT[t % 2][:, d_ * 128:(d_ + 1) * 128]
                            for g_ in range(2):
                                TE.do("matmul", pp[:, g_ * 512:(g_ + 1) * 512],
                                      lw,
                                      wp[:, d_ * D + g_ * 512:
                                         d_ * D + (g_ + 1) * 512],
                                      start=(d_ == 0), stop=False)
                        for g_ in range(2):
                            TE.doi("matmul", pp[:, g_ * 512:(g_ + 1) * 512],
                                   ones_row_bf[:, :],
                                   bp_row[:, g_ * 512:(g_ + 1) * 512],
                                   start=False, stop=True,
                                   label=(f"b_pr_{b}_{t}" if g_ == 1 else None))

                    def stats(prefix, t, mucol_from):
                        VE.do("tensor_scalar", colsc[:, 2:3],
                              colsc[:, mucol_from:mucol_from + 1],
                              1.0 / D, None, OP.mult)
                        VE.do("drain")
                        VE.do("tensor_tensor", colsc[:, 3:4], colsc[:, 2:3],
                              colsc[:, 2:3], OP.mult)
                        VE.do("tensor_scalar", colsc[:, 4:5], colsc[:, 1:2],
                              1.0 / D, None, OP.mult)
                        VE.do("drain")
                        VE.do("tensor_tensor", colsc[:, 4:5], colsc[:, 4:5],
                              colsc[:, 3:4], OP.subtract)
                        VE.do("drain")
                        VE.doi("tensor_scalar", colsc[:, 4:5], colsc[:, 4:5],
                               EPS, None, OP.add, label=f"{prefix}v_{b}_{t}")
                        SE.w(f"{prefix}v_{b}_{t}")
                        SE.doi("sqrt", colsc[:, 5:6], colsc[:, 4:5],
                               label=f"{prefix}sd_{b}_{t}")
                        VE.w(f"{prefix}sd_{b}_{t}")
                        VE.doi("reciprocal", colsc[:, 5:6], colsc[:, 5:6],
                               label=f"{prefix}rc_{b}_{t}")
                        VE.w(f"{prefix}rc_{b}_{t}")
                        VE.do("tensor_tensor", colsc[:, 6:7], colsc[:, 2:3],
                              colsc[:, 5:6], OP.mult)
                        VE.do("drain")
                        VE.doi("tensor_scalar", colsc[:, 6:7], colsc[:, 6:7],
                               -1.0, None, OP.mult, label=f"{prefix}mb_{b}_{t}")

                    def lnchain(t):
                        tgd = b * NT + t
                        if DEBUG_DUMP == "mix":
                            GE.w(f"b_div_{b}_{t}")
                            GED.dma(dbg_d[tgd * 128:(tgd + 1) * 128, :],
                                    mixb[t % 2][:, :], label=f"dbg_{b}_{t}")
                        pp = pPr
                        SE.w(f"b_pr_{b}_{t}")
                        if t >= 2:
                            SE.w(f"b_b2_{b}_{t-2}")
                        SE.doi("activation", S0[t % 2][:, :], pp[:, :], AF.Copy,
                               accum_out=colsc[:, 0:1], label=f"b_pcopy_{b}_{t}")
                        if DEBUG_DUMP == "proj2":
                            tgd2 = b * NT + t
                            SY.w(f"b_pcopy_{b}_{t}")
                            SY.dma(dbg_d[tgd2 * 128:(tgd2 + 1) * 128, 0:1024],
                                   S0[t % 2][:, :], label=f"dmpp_{b}_{t}")
                        SE.doi("activation", TRH[:, :], S0[t % 2][:, :],
                               AF.Square, accum_out=colsc[:, 1:2],
                               label=f"b_sq1_{b}_{t}")
                        VE.w(f"b_sq1_{b}_{t}")
                        stats("b_s1", t, 0)
                        SE.w(f"b_s1mb_{b}_{t}")
                        if t >= 2:
                            SE.w(f"b_g2_{b}_{t-2}")
                        SE.doi("activation", Z0[t % 2][:, :], S0[t % 2][:, :],
                               AF.Identity, scale=colsc[:, 5:6], bias=colsc[:, 6:7],
                               label=f"b_u1_{b}_{t}")
                        if DEBUG_DUMP == "u1":
                            SY.w(f"b_u1_{b}_{t}")
                            SY.dma(dbg_d[tgd * 128:(tgd + 1) * 128, :],
                                   Z0[t % 2][:, :], label=f"dbg_{b}_{t}")
                        VE.w(f"b_u1_{b}_{t}")
                        if DEBUG_DUMP == "proj2":
                            VE.w(f"dmpp_{b}_{t}")
                        VE.doi("tensor_tensor", S0[t % 2][:, :], Z0[t % 2][:, :],
                               g1bc[:, :], OP.mult, label=f"b_g1_{b}_{t}")
                        GE.w(f"b_g1_{b}_{t}")
                        GE.w("p_bcc3_1")
                        GE.doi("tensor_tensor", Z0[t % 2][:, :], S0[t % 2][:, :],
                               b1bc[:, :], OP.add, label=f"b_b1_{b}_{t}")
                        VE.w(f"b_b1_{b}_{t}")
                        VE.w(f"b_h_{b}_{t}")
                        VE.doi("scalar_tensor_tensor", S0[t % 2][:, :],
                               h32[t % 2][:, :], 2.0, Z0[t % 2][:, :],
                               OP.mult, OP.add, label=f"b_z_{b}_{t}")
                        if DEBUG_DUMP == "z":
                            SY.w(f"b_z_{b}_{t}")
                            SY.dma(dbg_d[tgd * 128:(tgd + 1) * 128, :],
                                   S0[t % 2][:, :], label=f"dbg_{b}_{t}")
                        SE.w(f"b_z_{b}_{t}")
                        SE.do("activation", TRH[:, :], S0[t % 2][:, :],
                              AF.Identity, accum_out=colsc[:, 0:1])
                        SE.doi("activation", TRH[:, :], S0[t % 2][:, :],
                               AF.Square, accum_out=colsc[:, 1:2],
                               label=f"b_sq2_{b}_{t}")
                        VE.w(f"b_sq2_{b}_{t}")
                        stats("b_s2", t, 0)
                        SE.w(f"b_s2mb_{b}_{t}")
                        if t >= 2:
                            SE.w(f"b_x1s_{b}_{t-2}")
                        elif b > 0:
                            SE.w(f"b_x1s_{b-1}_{NT + t - 2}")
                        SE.doi("activation", X1b[t % 2][:, :], S0[t % 2][:, :],
                               AF.Identity, scale=colsc[:, 5:6], bias=colsc[:, 6:7],
                               accum_out=colsc[:, 7:8], label=f"b_x1_{b}_{t}")
                        tg = b * NT + t
                        SY.w(f"b_x1_{b}_{t}")
                        SY.dma(x1s_d[tg * 128:(tg + 1) * 128, :],
                               X1b[t % 2][:, :], label=f"b_x1s_{b}_{t}",
                               sem="dso")
                        SE.doi("activation", TRH[:, :], X1b[t % 2][:, :],
                               AF.Square, accum_out=colsc[:, 1:2],
                               label=f"b_sq3_{b}_{t}")
                        VE.w(f"b_sq3_{b}_{t}")
                        stats("b_s3", t, 7)
                        SE.w(f"b_s3mb_{b}_{t}")
                        SE.doi("activation", Z0[t % 2][:, :], X1b[t % 2][:, :],
                               AF.Identity, scale=colsc[:, 5:6], bias=colsc[:, 6:7],
                               label=f"b_u3_{b}_{t}")
                        VE.w(f"b_u3_{b}_{t}")
                        VE.doi("tensor_tensor", S0[t % 2][:, :], Z0[t % 2][:, :],
                               g2bc[:, :], OP.mult, label=f"b_g2_{b}_{t}")
                        GE.w(f"b_g2_{b}_{t}")
                        if t >= 2:
                            GE.w(f"b_xns_{b}_{t-2}")
                            GE.w(f"b_am_{b}_{t-2}")
                        elif b > 0:
                            GE.w(f"b_xns_{b-1}_{NT + t - 2}")
                        GE.doi("tensor_tensor", XNb[t % 2][:, :], S0[t % 2][:, :],
                               b2bc[:, :], OP.add, label=f"b_b2_{b}_{t}")
                        SY.w(f"b_b2_{b}_{t}")
                        SY.dma(xns_d[tg * 128:(tg + 1) * 128, :],
                               XNb[t % 2][:, :], label=f"b_xns_{b}_{t}",
                               sem="dso")
                        VE.w(f"b_b2_{b}_{t}")
                        VE.doi("tensor_reduce", amax_cols[:, tg:tg + 1],
                               XNb[t % 2][:, :], AX.X, OP.max,
                               apply_absolute_value=True,
                               label=f"b_am_{b}_{t}")

                    for g in range(NG):
                        qproj(g)
                        for s_ in range(SUBS):
                            t = g * SUBS + s_
                            attnout(t)
                            division(t)
                    for t in range(NT):
                        mixload(t)
                        mixtrans(t)
                        proj(t)
                        lnchain(t)

                emit_block(body)

        if STOP_AFTER != "prep":
            for b in range(BPC):
                phase_a(b)
                if STOP_AFTER == f"a{b}":
                    break
                phase_b(b)
                if STOP_AFTER == f"b{b}":
                    break

        stab.close()

        stf = ExitStack()
        sg = sbuf(stf, "sg", [128, 8 * D], BF16)
        wfc = [sbuf(stf, f"wfc{i}", [128, D], F32) for i in range(2)]
        ab_buf = sbuf(stf, "ab_buf", [128, D], F32)
        m_buf = sbuf(stf, "m_buf", [128, D], F32)
        sgn_buf = sbuf(stf, "sgn_buf", [128, D], F32)
        trf = sbuf(stf, "trf", [128, D], F32)
        wcols = sbuf(stf, "wcols", [128, 28], F32)
        xnt = [sbuf(stf, f"xnt{i}", [128, D], BF16) for i in range(2)]
        x1t = [sbuf(stf, f"x1t{i}", [128, D], BF16) for i in range(2)]
        Tq = [sbuf(stf, f"Tq{i}", [128, D], F32) for i in range(2)]
        Iq = [sbuf(stf, f"Iq{i}", [128, D], I32) for i in range(2)]
        Fq = sbuf(stf, "Fq", [128, D], F32)
        qb = [sbuf(stf, f"qb{i}", [128, D], BF16) for i in range(2)]
        qbT = [sbuf(stf, f"qbT{i}", [128, D], BF16) for i in range(2)]
        Sf = [sbuf(stf, f"Sf{i}", [128, D], F32) for i in range(2)]
        outb = [sbuf(stf, f"outb{i}", [128, D], F32) for i in range(2)]
        fcol = sbuf(stf, "fcol", [128, 12], F32)

        def cc_phase():
            with ExitStack() as pst:
                pp = psum(pst, "pp_cc", [128, 512], F32)

                def body():
                    VE.doi("tensor_reduce", fcol[:, 0:1], amax_cols[:, :],
                           AX.X, OP.max, label="c_lm")
                    TE.w("c_lm")
                    TE.w("p_idn32")
                    TE.doi("matmul", pp[0:1, 32:160], fcol[:, 0:1], idn32[:, :],
                           start=True, stop=True, label="c_row")
                    VE.w("c_row")
                    VE.doi("tensor_reduce", cc_sc[0:1, 0:1], pp[0:1, 32:160],
                           AX.X, OP.max, label="c_m11")
                    GE.w("c_m11")
                    GED.dma(ccin_d[:, :], cc_sc[0:1, 0:8], label="c_cin")
                    GE.w("c_cin")
                    GE.doi("collective_compute", "AllReduce", OP.max,
                           replica_groups=[list(range(NCORES))],
                           ins=[ccin_d.ap().opt()],
                           outs=[ccout_d.ap().opt()],
                           label="c_cc")
                    GE.w("c_cc")
                    GED.dma(cc_sc[0:1, 0:8], ccout_d[:, :], label="c_cout")
                    VE.w("c_cout")
                    VE.doi("tensor_scalar", cc_sc[0:1, 1:2], cc_sc[0:1, 0:1],
                           1e-8, 1.0 / QMAX, OP.max, OP.mult, label="c_sc1")
                    VE.w("c_sc1")
                    VE.doi("reciprocal", cc_sc[0:1, 2:3], cc_sc[0:1, 1:2],
                           label="c_inv")
                    VE.w("c_inv")

                    SY.dma(wfc[0][:, :], wffnT_d[0:128, :], label="c_w0")
                    SY.dma(wfc[1][:, :], wffnT_d[128:256, :], label="c_w1")
                    for c in range(8):
                        VE.w(f"c_w{c}")
                        VE.doi("tensor_reduce", wcols[:, c:c + 1],
                               wfc[c % 2][:, :], AX.X, OP.add,
                               apply_absolute_value=True, label=f"c_aw{c}")
                        if c + 2 < 8:
                            SY.w(f"c_aw{c}")
                            SY.dma(wfc[c % 2][:, :],
                                   wffnT_d[(c + 2) * 128:(c + 3) * 128, :],
                                   label=f"c_w{c+2}")
                    VE.do("drain")
                    VE.doi("tensor_reduce", wcols[:, 8:9], wcols[:, 0:8],
                           AX.X, OP.add, label="c_awsum")
                    TE.w("c_awsum")
                    TE.doi("matmul", pp[0:1, 0:1], ones_col_f[:, :],
                           wcols[:, 8:9], start=True, stop=True, label="c_tot")
                    VE.w("c_tot")
                    VE.doi("tensor_scalar", cc_sc[0:1, 3:4], pp[0:1, 0:1],
                           THRESH / float(D * D), None, OP.mult, label="c_del")
                    TE.w("c_del")
                    TE.doi("matmul", pp[:, 4:5], ones_row_f[:, :],
                           cc_sc[0:1, 3:4], start=True, stop=True, label="c_delb")
                    VE.w("c_delb")
                    VE.doi("tensor_copy", fcol[:, 1:2], pp[:, 4:5],
                           label="c_delc")
                    SY.w("c_aw6")
                    SY.dma(wfc[0][:, :], wffnT_d[0:128, :], label="c_v0")
                    SY.w("c_aw7")
                    SY.dma(wfc[1][:, :], wffnT_d[128:256, :], label="c_v1")
                    for c in range(8):
                        SE.w(f"c_v{c}")
                        SE.w("c_delc")
                        if c >= 1:
                            SE.w(f"c_awm{c-1}")
                            SE.w(f"c_S{c-1}")
                        SE.doi("activation", ab_buf[:, :], wfc[c % 2][:, :],
                               AF.Abs, label=f"c_ab{c}")
                        SE.doi("activation", sgn_buf[:, :], wfc[c % 2][:, :],
                               AF.Sign, label=f"c_sg{c}")
                        if c + 2 < 8:
                            SY.w(f"c_sg{c}")
                            SY.dma(wfc[c % 2][:, :],
                                   wffnT_d[(c + 2) * 128:(c + 3) * 128, :],
                                   label=f"c_v{c+2}")
                        VE.w(f"c_ab{c}")
                        VE.doi("tensor_scalar", m_buf[:, :], ab_buf[:, :],
                               fcol[:, 1:2], None, OP.is_gt, label=f"c_mk{c}")
                        VE.do("tensor_tensor", trf[:, :], ab_buf[:, :],
                              m_buf[:, :], OP.mult)
                        VE.doi("tensor_reduce", wcols[:, 9 + c:10 + c],
                               trf[:, :], AX.X, OP.add, label=f"c_awm{c}")
                        VE.doi("tensor_reduce", wcols[:, 17 + c:18 + c],
                               m_buf[:, :], AX.X, OP.add, label=f"c_sm{c}")
                        VE.w(f"c_sg{c}")
                        VE.doi("tensor_tensor", sg[:, c * D:(c + 1) * D],
                               sgn_buf[:, :], m_buf[:, :], OP.mult,
                               label=f"c_S{c}")
                    VE.do("drain")
                    VE.doi("tensor_reduce", wcols[:, 25:26], wcols[:, 9:17],
                           AX.X, OP.add, label="c_awm_sum")
                    VE.doi("tensor_reduce", wcols[:, 26:27], wcols[:, 17:25],
                           AX.X, OP.add, label="c_sm_sum")
                    TE.w("c_awm_sum")
                    TE.w("c_delc")
                    TE.doi("matmul", pp[0:1, 8:9], ones_col_f[:, :],
                           wcols[:, 25:26], start=True, stop=True, label="c_t1")
                    TE.w("c_sm_sum")
                    TE.doi("matmul", pp[0:1, 12:13], ones_col_f[:, :],
                           wcols[:, 26:27], start=True, stop=True, label="c_t2")
                    VE.w("c_t1")
                    VE.w("c_t2")
                    VE.doi("tensor_scalar", cc_sc[0:1, 4:5], pp[0:1, 12:13],
                           1.0, None, OP.max, label="c_sc2")
                    VE.w("c_sc2")
                    VE.doi("reciprocal", cc_sc[0:1, 5:6], cc_sc[0:1, 4:5],
                           label="c_rc2")
                    VE.w("c_rc2")
                    VE.do("tensor_tensor", alpha11[0:1, 0:1], pp[0:1, 8:9],
                          cc_sc[0:1, 5:6], OP.mult)
                    VE.do("drain")
                    VE.doi("tensor_tensor", cc_sc[0:1, 6:7], cc_sc[0:1, 1:2],
                           alpha11[0:1, 0:1], OP.mult, label="c_fs")
                    TE.w("c_inv")
                    TE.w("c_fs")
                    TE.doi("matmul", pp[:, 16:17], ones_row_f[:, :],
                           cc_sc[0:1, 2:3], start=True, stop=True, label="c_bi")
                    TE.doi("matmul", pp[:, 20:21], ones_row_f[:, :],
                           cc_sc[0:1, 6:7], start=True, stop=True, label="c_bf")
                    VE.w("c_bi")
                    VE.w("c_bf")
                    VE.doi("tensor_copy", inv_col[:, :], pp[:, 16:17],
                           label="c_invc")
                    VE.doi("tensor_copy", fs_col[:, :], pp[:, 20:21],
                           label="c_fsc")
                    if DEBUG_DUMP == "ffn":
                        GE.w("c_fsc")
                        GED.dma(dbg_d[0:128, 0:4096], sg[:, 0:4096],
                                label="sgd1", sem="dso")
                        GED.dma(dbg_d[128:256, 0:4096], sg[:, 4096:8192],
                                label="sgd2", sem="dso")
                    if DEBUG_DUMP == "cc":
                        GE.w("c_fsc")
                        GED.dma(dbg_d[0:1, 0:8], cc_sc[0:1, :], label="ccd1",
                                sem="dso")
                        GED.dma(dbg_d[1:2, 0:1], alpha11[0:1, :], label="ccd2",
                                sem="dso")
                        GED.dma(dbg_d[2:130, 0:28], wcols[:, :], label="ccd3",
                                sem="dso")

                emit_block(body)

        if STOP_AFTER is None or STOP_AFTER == "cc":
            cc_phase()

        def ffn_phase():
            NTT = BPC * NT
            with ExitStack() as pst:
                pT3 = psum(pst, "pT3", [128, 1024], BF16)
                pF = [psum(pst, f"pF{i}", [128, 1024], F32) for i in range(2)]

                def body():
                    def loads(t):
                        if t >= 2:
                            SY.w(f"f_t2_{t-2}")
                        SY.w(f"b_xns_{BPC-1}_{NT-1}")
                        SY.dma(xnt[t % 2][:, :],
                               xns_d[t * 128:(t + 1) * 128, :],
                               label=f"f_xn_{t}", sem=f"dh{t % 2}")
                        if t >= 2:
                            SCD.w(f"f_s_{t-2}")
                        SCD.w(f"b_x1s_{BPC-1}_{NT-1}")
                        SCD.dma(x1t[t % 2][:, :],
                                x1s_d[t * 128:(t + 1) * 128, :],
                                label=f"f_x1_{t}", sem=f"ds{t % 2}")

                    def quant(t):
                        VE.w(f"f_xn_{t}")
                        VE.w("c_invc")
                        if t >= 2:
                            VE.w(f"f_tr_{t-2}")
                            if DEBUG_DUMP == "ffn" and (t - 2) in (4, 5):
                                VE.w(f"qbd_{t-2}")
                            VE.w(f"f_i_{t-2}")
                        VE.do("tensor_scalar", Fq[:, :], xnt[t % 2][:, :],
                              inv_col[:, 0:1], -QMAX, OP.mult, OP.max)
                        VE.doi("tensor_scalar", Tq[t % 2][:, :], Fq[:, :],
                               QMAX, 128.0, OP.min, OP.add, label=f"f_t2_{t}")
                        SE.w(f"f_t2_{t}")
                        if t >= 2:
                            SE.w(f"f_qb_{t-2}")
                        SE.doi("copy", Iq[t % 2][:, :], Tq[t % 2][:, :],
                               label=f"f_i_{t}")
                        VE.w(f"f_i_{t}")
                        VE.do("scalar_tensor_tensor", Fq[:, :],
                              Iq[t % 2][:, :], 0.5, Tq[t % 2][:, :],
                              OP.add, OP.is_le)
                        VE.doi("scalar_tensor_tensor", qb[t % 2][:, :],
                               Iq[t % 2][:, :], -128.0, Fq[:, :],
                               OP.add, OP.add, label=f"f_qb_{t}")
                        if DEBUG_DUMP == "ffn" and t in (4, 5):
                            GE.w(f"f_qb_{t}")
                            GED.dma(dbg_d[256 + (t - 4) * 128:
                                          256 + (t - 3) * 128, 0:1024],
                                    qb[t % 2][:, :], label=f"qbd_{t}",
                                    sem="dso")

                    def trq(t):
                        TE.w(f"f_qb_{t}")
                        TE.w("p_idnb")
                        if t >= 1:
                            TE.w(f"f_qev_{t-1}_0")
                            TE.w(f"f_qev_{t-1}_1")
                        for c in range(8):
                            TE.doi("transpose", pT3[:, c * 128:(c + 1) * 128],
                                   qb[t % 2][:, c * 128:(c + 1) * 128],
                                   idnbf[:, :],
                                   label=(f"f_tr_{t}" if c == 7 else None))
                        for r, px in ((0, VE), (1, SE)):
                            px.w(f"f_tr_{t}")
                            if r == 1:
                                px.w(f"f_qev_{t}_0")
                            px.doi("tensor_copy" if r == 0 else "copy",
                                   qbT[t % 2][:, r * 512:(r + 1) * 512],
                                   pT3[:, r * 512:(r + 1) * 512],
                                   label=f"f_qev_{t}_{r}")

                    def ffnmm(t):
                        TE.w(f"f_qev_{t}_0")
                        TE.w(f"f_qev_{t}_1")
                        if t >= 2:
                            TE.w(f"f_s_{t-2}")
                        pp = pF[t % 2]
                        for d_ in range(8):
                            lw = qbT[t % 2][:, d_ * 128:(d_ + 1) * 128]
                            for g_ in range(2):
                                TE.doi("matmul",
                                       pp[:, g_ * 512:(g_ + 1) * 512], lw,
                                       sg[:, d_ * D + g_ * 512:
                                          d_ * D + (g_ + 1) * 512],
                                       start=(d_ == 0), stop=(d_ == 7),
                                       label=(f"f_mm_{t}"
                                              if d_ == 7 and g_ == 1 else None))

                    def final(t):
                        pp = pF[t % 2]
                        VE.w(f"f_mm_{t}")
                        VE.w(f"f_x1_{t}")
                        VE.w("c_fsc")
                        if t >= 2:
                            VE.w(f"f_o_{t-2}")
                        VE.doi("scalar_tensor_tensor", Sf[t % 2][:, :], pp[:, :],
                               fs_col[:, 0:1], x1t[t % 2][:, :], OP.mult, OP.add,
                               label=f"f_s_{t}")
                        SE.w(f"f_s_{t}")
                        SE.do("activation", trf[:, :], Sf[t % 2][:, :],
                              AF.Identity, accum_out=fcol[:, 2:3])
                        SE.doi("activation", trf[:, :], Sf[t % 2][:, :],
                               AF.Square, accum_out=fcol[:, 3:4],
                               label=f"f_sq_{t}")
                        VE.w(f"f_sq_{t}")
                        VE.do("tensor_scalar", fcol[:, 4:5], fcol[:, 2:3],
                              1.0 / D, None, OP.mult)
                        VE.do("drain")
                        VE.do("tensor_tensor", fcol[:, 5:6], fcol[:, 4:5],
                              fcol[:, 4:5], OP.mult)
                        VE.do("tensor_scalar", fcol[:, 6:7], fcol[:, 3:4],
                              1.0 / D, None, OP.mult)
                        VE.do("drain")
                        VE.do("tensor_tensor", fcol[:, 6:7], fcol[:, 6:7],
                              fcol[:, 5:6], OP.subtract)
                        VE.do("drain")
                        VE.doi("tensor_scalar", fcol[:, 6:7], fcol[:, 6:7],
                               EPS, None, OP.add, label=f"f_var_{t}")
                        SE.w(f"f_var_{t}")
                        SE.doi("sqrt", fcol[:, 7:8], fcol[:, 6:7],
                               label=f"f_sd_{t}")
                        VE.w(f"f_sd_{t}")
                        VE.doi("reciprocal", fcol[:, 7:8], fcol[:, 7:8],
                               label=f"f_rc_{t}")
                        VE.w(f"f_rc_{t}")
                        VE.do("tensor_tensor", fcol[:, 8:9], fcol[:, 4:5],
                              fcol[:, 7:8], OP.mult)
                        VE.do("drain")
                        VE.doi("tensor_scalar", fcol[:, 8:9], fcol[:, 8:9],
                               -1.0, None, OP.mult, label=f"f_mb_{t}")
                        SE.w(f"f_mb_{t}")
                        if t >= 2:
                            SE.w(f"f_out_{t-2}")
                        SE.doi("activation", outb[t % 2][:, :], Sf[t % 2][:, :],
                               AF.Identity, scale=fcol[:, 7:8], bias=fcol[:, 8:9],
                               label=f"f_o_{t}")
                        b_, tt = divmod(t, NT)
                        SY.w(f"f_o_{t}")
                        SY.dma(out_d[b_, tt * 128:(tt + 1) * 128, :],
                               outb[t % 2][:, :], label=f"f_out_{t}",
                               sem="dso")

                    loads(0)
                    loads(1)
                    quant(0)
                    for t in range(NTT):
                        trq(t)
                        if t + 1 < NTT:
                            quant(t + 1)
                        ffnmm(t)
                        final(t)
                        if t + 2 < NTT:
                            loads(t + 2)
                    SY.w(f"f_out_{NTT-1}")
                    SY.w(f"f_out_{NTT-2}")

                emit_block(body)

        if STOP_AFTER is None:
            ffn_phase()
        stf.close()

    return nc


def kernel(L0_attn_H, L0_attn_W_qkv, L0_attn_b_qkv, L0_attn_W_proj,
           L0_attn_b_proj, L0_attn_gamma, L0_attn_beta, L0_ffn_weight,
           L0_ffn_gamma, L0_ffn_beta):
    B, S, D_ = L0_attn_H.shape
    nc = build_nc(S=S)
    shared = {
        "wqkvT": np.ascontiguousarray(np.asarray(L0_attn_W_qkv, np.float32).T),
        "bqkv": np.asarray(L0_attn_b_qkv, np.float32).reshape(1, -1),
        "wprojT": np.ascontiguousarray(np.asarray(L0_attn_W_proj, np.float32).T),
        "bproj": np.asarray(L0_attn_b_proj, np.float32).reshape(1, -1),
        "g1": np.asarray(L0_attn_gamma, np.float32).reshape(1, -1),
        "be1": np.asarray(L0_attn_beta, np.float32).reshape(1, -1),
        "wffnT": np.ascontiguousarray(np.asarray(L0_ffn_weight, np.float32).T),
        "g2": np.asarray(L0_ffn_gamma, np.float32).reshape(1, -1),
        "be2": np.asarray(L0_ffn_beta, np.float32).reshape(1, -1),
        "idn": np.eye(128, dtype=np.float32),
    }
    H = np.ascontiguousarray(np.asarray(L0_attn_H, np.float32))
    in_maps = []
    for c in range(NCORES):
        m = dict(shared)
        m["h"] = H[c * BPC:(c + 1) * BPC]
        in_maps.append(m)
    res = run_bass_kernel_spmd(nc, in_maps, core_ids=list(range(NCORES)))
    out = np.concatenate([res.results[c]["out"] for c in range(NCORES)], axis=0)
    return np.ascontiguousarray(out.astype(np.float32))


# version 04
# speedup vs baseline: 1.0773x; 1.0773x over previous
import numpy as np
from contextlib import ExitStack

import concourse.bass as bass
import concourse.mybir as mybir
from concourse.bass_utils import run_bass_kernel_spmd

F32 = mybir.dt.float32
BF16 = mybir.dt.bfloat16
I32 = mybir.dt.int32
OP = mybir.AluOpType
AF = mybir.ActivationFunctionType
AX = mybir.AxisListType

NCORES = 8
D = 1024
HEADS = 16
BPC = 2
EPS = 1e-5
QMAX = 127.0
THRESH = 0.7

KVSLOT = [0, 65, 130, 195, 260, 325, 390, 512]
AOBANK = [(0, 7), (7, 14), (14, 16)]


class Ctx:
    def __init__(self, nc, semnames):
        self.nc = nc
        self.semnames = list(semnames)
        self.sems = {}
        self.c = {k: 0 for k in semnames}
        self.marks = {}
        self.mode = None
        self.eng = None
        self._waited = {}

    def snap(self):
        return (dict(self.c), dict(self.marks))

    def restore(self, s):
        self.c = dict(s[0])
        self.marks = dict(s[1])
        self._waited = {}

    def set_eng(self, mode, eng):
        self.mode = mode
        self.eng = eng
        self._waited = {}


class Px:
    def __init__(self, ctx, name, sem):
        self.ctx = ctx
        self.name = name
        self.sem = sem

    @property
    def live(self):
        return self.ctx.mode == self.name

    def w(self, label, extra=0):
        if LENIENT and label not in self.ctx.marks:
            return
        sem, val = self.ctx.marks[label]
        val += extra
        if val <= 0:
            return
        key = (self.name, sem)
        if self.ctx._waited.get(key, -1) >= val:
            return
        self.ctx._waited[key] = val
        if self.live:
            self.ctx.eng.wait_ge(self.ctx.sems[sem], val)

    def do(self, opname, *a, **k):
        if self.live:
            return getattr(self.ctx.eng, opname)(*a, **k)
        return None

    def doi(self, opname, *a, label=None, inc=1, **k):
        inst = self.do(opname, *a, **k)
        self.ctx.c[self.sem] += inc
        if label is not None:
            self.ctx.marks[label] = (self.sem, self.ctx.c[self.sem])
        if self.live:
            inst.then_inc(self.ctx.sems[self.sem], inc)
        return inst

    def dma(self, out, in_, label=None, sem=None):
        if sem is None:
            return self.doi("dma_start", out, in_, label=label, inc=16)
        old = self.sem
        self.sem = sem
        try:
            return self.doi("dma_start", out, in_, label=label, inc=16)
        finally:
            self.sem = old


STOP_AFTER = None
DEBUG_XOUT = False
DEBUG_DUMP = None
B_LIMIT = 99
Q_LIMIT = 99
LENIENT = False


def build_nc(S=2048):
    NT = S // 128
    SG = 512 if S >= 512 else S
    NG = S // SG
    SUBS = SG // 128

    nc = bass.Bass("TRN2", target_bir_lowering=False,
                   detect_race_conditions=False)

    h_d = nc.dram_tensor("h", [BPC, S, D], F32, kind="ExternalInput")
    wqkvT_d = nc.dram_tensor("wqkvT", [D, 3 * D], F32, kind="ExternalInput")
    bqkv_d = nc.dram_tensor("bqkv", [1, 3 * D], F32, kind="ExternalInput")
    wprojT_d = nc.dram_tensor("wprojT", [D, D], F32, kind="ExternalInput")
    bproj_d = nc.dram_tensor("bproj", [1, D], F32, kind="ExternalInput")
    g1_d = nc.dram_tensor("g1", [1, D], F32, kind="ExternalInput")
    be1_d = nc.dram_tensor("be1", [1, D], F32, kind="ExternalInput")
    wffnT_d = nc.dram_tensor("wffnT", [D, D], F32, kind="ExternalInput")
    g2_d = nc.dram_tensor("g2", [1, D], F32, kind="ExternalInput")
    be2_d = nc.dram_tensor("be2", [1, D], F32, kind="ExternalInput")
    idn_d = nc.dram_tensor("idn", [128, 128], F32, kind="ExternalInput")
    out_d = nc.dram_tensor("out", [BPC, S, D], F32, kind="ExternalOutput")
    xkind = "ExternalOutput" if DEBUG_XOUT else "Internal"
    mixs_d = nc.dram_tensor("mixs", [BPC * S, D], BF16)
    x1s_d = nc.dram_tensor("x1s", [BPC * S, D], BF16, kind=xkind)
    xns_d = nc.dram_tensor("xns", [BPC * S, D], BF16, kind=xkind)
    dbg_d = (nc.dram_tensor("dbg", [BPC * S, 4 * D], F32, kind="ExternalOutput")
             if DEBUG_DUMP else None)
    ccin_d = nc.dram_tensor("ccin", [1, 8], F32)
    ccout_d = nc.dram_tensor("ccout", [1, 8], F32)

    sems = ["dh", "dh0", "dh1", "dso", "ds0", "ds1",
            "dm0", "dm1", "dg", "pe", "ve", "se", "ge"]
    ctx = Ctx(nc, sems)
    SY = Px(ctx, "sync", "dh")
    TE = Px(ctx, "tensor", "pe")
    VE = Px(ctx, "vector", "ve")
    SE = Px(ctx, "scalar", "se")
    GE = Px(ctx, "gpsimd", "ge")
    SCD = Px(ctx, "scalar", "dsc")
    GED = Px(ctx, "gpsimd", "dg")

    def emit_block(body):
        snap = ctx.snap()
        with nc.Block() as blk:
            for engname, attr in [("sync", "sync"), ("tensor", "tensor"),
                                  ("vector", "vector"), ("scalar", "scalar"),
                                  ("gpsimd", "gpsimd")]:
                def run(eng, engname=engname):
                    ctx.restore(snap)
                    ctx.set_eng(engname, eng)
                    body()
                getattr(blk, attr)(run)

    st = ExitStack()
    with st:
        for name in sems:
            ctx.sems[name] = st.enter_context(nc.semaphore(name))

        def sbuf(stk, name, shape, dt=F32):
            return stk.enter_context(nc.sbuf_tensor(name, shape, dt))

        def psum(stk, name, shape, dt=F32):
            return stk.enter_context(nc.psum_tensor(name, shape, dt))

        wq = sbuf(st, "wq", [128, 8 * 3 * D], BF16)
        wp = sbuf(st, "wp", [128, 8 * D], BF16)
        bq_row = sbuf(st, "bq_row", [1, 3 * D], BF16)
        bp_row = sbuf(st, "bp_row", [1, D], BF16)
        g1bc = sbuf(st, "g1bc", [128, D], BF16)
        b1bc = sbuf(st, "b1bc", [128, D], BF16)
        g2bc = sbuf(st, "g2bc", [128, D], BF16)
        b2bc = sbuf(st, "b2bc", [128, D], BF16)
        idn32 = sbuf(st, "idn32", [128, 128], F32)
        idnbf = sbuf(st, "idnbf", [128, 128], BF16)
        ones_row_bf = sbuf(st, "ones_row_bf", [1, 128], BF16)
        ones_row_f = sbuf(st, "ones_row_f", [1, 128], F32)
        ones_col_f = sbuf(st, "ones_col_f", [128, 1], F32)
        ones11 = sbuf(st, "ones11", [1, 1], F32)
        negcol = sbuf(st, "negcol", [128, 1], F32)
        epscol = sbuf(st, "epscol", [128, 1], F32)
        bqcol = sbuf(st, "bqcol", [128, 8], F32)
        bqncol = sbuf(st, "bqncol", [128, 8], F32)
        amax_cols = sbuf(st, "amax_cols", [128, BPC * NT], F32)
        inv_col = sbuf(st, "inv_col", [128, 1], F32)
        fs_col = sbuf(st, "fs_col", [128, 1], F32)
        cc_sc = sbuf(st, "cc_sc", [1, 8], F32)
        alpha11 = sbuf(st, "alpha11", [1, 1], F32)

        def prep_phase():
          with ExitStack() as pst:
            pp = psum(pst, "pp_prep", [128, 512], F32)
            pp2 = psum(pst, "pp_prep2", [128, 512], F32)
            bq_f32 = sbuf(pst, "bq_f32", [1, 3 * D], F32)
            vrow_f = sbuf(pst, "vrow_f", [1, 4 * D], F32)

            def prep():
                for d_ in range(8):
                    GED.dma(wq[:, d_ * 3 * D:(d_ + 1) * 3 * D],
                            wqkvT_d[d_ * 128:(d_ + 1) * 128, :],
                            label=f"p_wq{d_}")
                for d_ in range(8):
                    GED.dma(wp[:, d_ * D:(d_ + 1) * D],
                            wprojT_d[d_ * 128:(d_ + 1) * 128, :],
                            label=f"p_wp{d_}")
                GED.dma(bq_row[:, :], bqkv_d[:, :], label="p_bqr")
                GED.dma(bp_row[:, :], bproj_d[:, :], label="p_bpr")
                GED.dma(idnbf[:, :], idn_d[:, :], label="p_idnb")
                SY.dma(bq_f32[:, :], bqkv_d[:, :], label="p_bqf")
                SY.dma(idn32[:, :], idn_d[:, :], label="p_idn32")
                SY.dma(vrow_f[:, 0:D], g1_d[:, :], label="p_g1")
                SY.dma(vrow_f[:, D:2 * D], be1_d[:, :], label="p_b1")
                SY.dma(vrow_f[:, 2 * D:3 * D], g2_d[:, :], label="p_g2")
                SY.dma(vrow_f[:, 3 * D:4 * D], be2_d[:, :], label="p_b2")
                GE.doi("memset", ones_row_bf[:, :], 1.0)
                GE.doi("memset", ones_row_f[:, :], 1.0)
                GE.doi("memset", ones_col_f[:, :], 1.0)
                GE.doi("memset", ones11[:, :], 1.0)
                GE.doi("memset", negcol[:, :], -1.0)
                GE.doi("memset", epscol[:, :], EPS)
                GE.doi("memset", cc_sc[:, :], 0.0)
                GE.doi("memset", amax_cols[:, :], 0.0, label="p_ms")

                if True:
                    VE.w("p_ms")
                    for _ in range(4):
                        VE.doi("reciprocal", inv_col[:, 0:1], ones_col_f[:, :])
                    TE.w("p_b2")
                    TE.w("p_idnb")
                    TE.w("p_ms")
                    for e in range(8):
                        TE.doi("matmul", pp[:, e:e + 1],
                               bq_f32[:, e * 128:(e + 1) * 128],
                               ones11[:, :], start=True, stop=True,
                               label=f"p_bqc{e}")
                    VE.w("p_bqc7")
                    VE.doi("tensor_copy", bqcol[:, :], pp[:, 0:8], label="p_bqcc")
                    VE.doi("tensor_scalar", bqncol[:, :], bqcol[:, :], -1.0, None,
                           OP.mult, label="p_bqnc")
                    dsts = [g1bc, b1bc, g2bc, b2bc]
                    labs = ["p_g1", "p_b1", "p_g2", "p_b2"]
                    for i in range(4):
                        TE.w(labs[i])
                        TE.w("p_bqcc" if i == 0 else f"p_bcc{i-1}_0")
                        TE.doi("matmul", pp[:, 0:512], ones_row_f[:, :],
                               vrow_f[:, i * D: i * D + 512],
                               start=True, stop=True, label=f"p_bc{i}_0")
                        if i > 0:
                            TE.w(f"p_bcc{i-1}_1")
                        TE.doi("matmul", pp2[:, 0:512], ones_row_f[:, :],
                               vrow_f[:, i * D + 512: i * D + 1024],
                               start=True, stop=True, label=f"p_bc{i}_1")
                        VE.w(f"p_bc{i}_0")
                        VE.doi("tensor_copy", dsts[i][:, 0:512], pp[:, 0:512],
                               label=f"p_bcc{i}_0")
                        VE.w(f"p_bc{i}_1")
                        VE.doi("tensor_copy", dsts[i][:, 512:1024], pp2[:, 0:512],
                               label=f"p_bcc{i}_1")

            emit_block(prep)
        prep_phase()

        stab = ExitStack()
        ht = sbuf(stab, "ht", [128, 8 * S], BF16)
        h32 = [sbuf(stab, f"h32_{i}", [128, D], F32) for i in range(2)]
        phik = [sbuf(stab, f"phik{i}", [128, D], BF16) for i in range(2)]
        vbf = [sbuf(stab, f"vbf{i}", [128, HEADS * 65], BF16) for i in range(2)]
        kvst = sbuf(stab, "kvst", [128, 1024], BF16)
        kvlo = sbuf(stab, "kvlo", [64, 1024], BF16)
        phiq = sbuf(stab, "phiq", [128, 8 * SG], BF16)
        phiqlo = sbuf(stab, "phiqlo", [64, 8 * SG], BF16)
        qex = [sbuf(stab, f"qex{i}", [128, SG], F32) for i in range(2)]
        mixb = [sbuf(stab, f"mixb{i}", [128, D], BF16) for i in range(2)]
        mixb2 = [sbuf(stab, f"mixb2_{i}", [128, D], BF16) for i in range(2)]
        mixT = [sbuf(stab, f"mixT{i}", [128, D], BF16) for i in range(2)]
        S0 = [sbuf(stab, f"S0_{i}", [128, D], F32) for i in range(2)]
        Z0 = [sbuf(stab, f"Z0_{i}", [128, D], F32) for i in range(2)]
        TRH = sbuf(stab, "TRH", [128, D], F32)
        X1b = [sbuf(stab, f"X1b{i}", [128, D], BF16) for i in range(2)]
        XNb = [sbuf(stab, f"XNb{i}", [128, D], BF16) for i in range(2)]
        dstg = sbuf(stab, "dstg", [128, 1024], F32)
        k1blk = sbuf(stab, "k1blk", [128, 16], BF16)
        qk1c = sbuf(stab, "qk1c", [128, HEADS], F32)
        recq = sbuf(stab, "recq", [128, HEADS], F32)
        colsc = sbuf(stab, "colsc", [128, 12], F32)

        def phase_a(b):
            with ExitStack() as pst:
                pT = psum(pst, f"pT_{b}", [128, 1024], F32)
                pK = psum(pst, f"pK_{b}", [128, 1024], F32)
                pV = psum(pst, f"pV_{b}", [128, 1024], F32)
                pacc = psum(pst, f"pacc_{b}", [128, 1024], F32)

                def body():
                    VE.doi("memset", pacc[:, :], 0.0, label=f"a_pz_{b}")
                    for i in range(2):
                        GE.doi("memset",
                               bass.AP(vbf[i], 64,
                                       [[HEADS * 65, 128], [65, HEADS], [1, 1]]),
                               1.0, label=f"a_vb1_{b}_{i}")

                    def load_h(t):
                        if t >= 2:
                            SY.w(f"a_tr_{b}_{t-2}")
                        SY.dma(h32[t % 2][:, :],
                               h_d[b, t * 128:(t + 1) * 128, :],
                               label=f"a_h_{b}_{t}", sem=f"dh{t % 2}")

                    def trans(t):
                        TE.w(f"a_h_{b}_{t}")
                        if t == 0:
                            TE.w("p_b2")
                        if t >= 1:
                            TE.w(f"a_evac_{b}_{t-1}")
                        for c in range(8):
                            TE.doi("transpose", pT[:, c * 128:(c + 1) * 128],
                                   h32[t % 2][:, c * 128:(c + 1) * 128],
                                   idn32[:, :],
                                   label=(f"a_tr_{b}_{t}" if c == 7 else None))

                    def evac(t):
                        VE.w(f"a_tr_{b}_{t}")
                        dst = bass.AP(ht, t * 128, [[8 * S, 128], [S, 8], [1, 128]])
                        src = bass.AP(pT, 0, [[1024, 128], [128, 8], [1, 128]])
                        VE.doi("tensor_copy", dst, src, label=f"a_evac_{b}_{t}")

                    def kvmm(t):
                        if t == 0:
                            TE.w("p_idnb")
                            TE.w("p_ms")
                        TE.w(f"a_evac_{b}_{t}")
                        if t >= 1:
                            TE.w(f"a_t1_{b}_{t-1}")
                            TE.w(f"a_vcopy_{b}_{t-1}")
                        for d_ in range(8):
                            lw = ht[:, d_ * S + t * 128: d_ * S + (t + 1) * 128]
                            for g_ in range(2):
                                TE.do("matmul", pK[:, g_ * 512:(g_ + 1) * 512], lw,
                                      wq[:, d_ * 3 * D + D + g_ * 512:
                                         d_ * 3 * D + D + (g_ + 1) * 512],
                                      start=(d_ == 0), stop=False)
                            for g_ in range(2):
                                TE.do("matmul", pV[:, g_ * 512:(g_ + 1) * 512], lw,
                                      wq[:, d_ * 3 * D + 2 * D + g_ * 512:
                                         d_ * 3 * D + 2 * D + (g_ + 1) * 512],
                                      start=(d_ == 0), stop=False)
                        for g_ in range(2):
                            TE.do("matmul", pK[:, g_ * 512:(g_ + 1) * 512],
                                  ones_row_bf[:, :],
                                  bq_row[:, D + g_ * 512: D + (g_ + 1) * 512],
                                  start=False, stop=True)
                        for g_ in range(2):
                            TE.doi("matmul", pV[:, g_ * 512:(g_ + 1) * 512],
                                   ones_row_bf[:, :],
                                   bq_row[:, 2 * D + g_ * 512:
                                          2 * D + (g_ + 1) * 512],
                                   start=False, stop=True,
                                   label=(f"a_kv_{b}_{t}" if g_ == 1 else None))

                    def phi(t):
                        if t == 0:
                            SE.w("p_ms")
                        SE.w(f"a_kv_{b}_{t}")
                        if t >= 2:
                            SE.w(f"a_acc_{b}_{t-2}")
                            SE.w(f"a_phi_{b}_{t-2}")
                        SE.do("activation", phik[t % 2][:, :], pK[:, :], AF.Relu)
                        SE.do("activation", TRH[:, :], pK[:, :], AF.Relu,
                              scale=negcol[:, 0:1])
                        SE.doi("activation", Z0[t % 2][:, :], TRH[:, :], AF.Exp,
                               scale=negcol[:, 0:1], label=f"a_t1_{b}_{t}")
                        VE.w(f"a_t1_{b}_{t}")
                        if t >= 2:
                            VE.w(f"a_acc_{b}_{t-2}")
                        VE.doi("tensor_tensor", phik[t % 2][:, :],
                               phik[t % 2][:, :], Z0[t % 2][:, :], OP.add,
                               label=f"a_phi_{b}_{t}")
                        VE.w(f"a_kv_{b}_{t}")
                        if t >= 2:
                            VE.w(f"a_acc_{b}_{t-2}")
                        if t == 0:
                            VE.w(f"a_vb1_{b}_1")
                        vdst = bass.AP(vbf[t % 2], 0,
                                       [[HEADS * 65, 128], [65, HEADS], [1, 64]])
                        vsrc = bass.AP(pV, 0, [[1024, 128], [64, HEADS], [1, 64]])
                        VE.doi("tensor_copy", vdst, vsrc, label=f"a_vcopy_{b}_{t}")

                    def kvacc(t):
                        TE.w(f"a_phi_{b}_{t}")
                        TE.w(f"a_vcopy_{b}_{t}")
                        if t == 0:
                            TE.w(f"a_pz_{b}")
                        for hh in range(HEADS):
                            par = hh % 2
                            slot = KVSLOT[hh // 2]
                            tp = (0, 64) if par else (0, 0)
                            TE.doi("matmul",
                                   pacc[par * 64:par * 64 + 64, slot:slot + 65],
                                   phik[t % 2][:, hh * 64:(hh + 1) * 64],
                                   vbf[t % 2][:, hh * 65:(hh + 1) * 65],
                                   start=False, stop=False,
                                   skip_group_check=True,
                                   tile_position=tp,
                                   label=(f"a_acc_{b}_{t}"
                                          if hh == HEADS - 1 else None))

                    load_h(0)
                    trans(0)
                    evac(0)
                    for t in range(NT):
                        if t + 1 < NT:
                            load_h(t + 1)
                        kvmm(t)
                        if t + 1 < NT:
                            trans(t + 1)
                            evac(t + 1)
                        phi(t)
                        if t >= 1:
                            kvacc(t - 1)
                    kvacc(NT - 1)
                    VE.w(f"a_acc_{b}_{NT-1}")
                    VE.doi("tensor_copy", kvst[:, :], pacc[:, :],
                           label=f"a_kvev_{b}")
                    VE.do("drain")
                    VE.do("memset", k1blk[:, :], 0.0)
                    VE.do("drain")
                    for e_ in range(8):
                        VE.do("tensor_copy",
                              k1blk[0:64, 2 * e_:2 * e_ + 1],
                              kvst[0:64, KVSLOT[e_] + 64:KVSLOT[e_] + 65])
                        VE.doi("tensor_copy",
                               k1blk[64:128, 2 * e_ + 1:2 * e_ + 2],
                               kvst[64:128, KVSLOT[e_] + 64:KVSLOT[e_] + 65],
                               label=(f"a_k1b_{b}" if e_ == 7 else None))
                    GE.w(f"a_kvev_{b}")
                    GED.dma(kvlo[:, :], kvst[64:128, :], label=f"a_kvlo_{b}")
                    if DEBUG_DUMP in ("kv", "all"):
                        GED.dma(dbg_d[512 + b * 128:512 + (b + 1) * 128, 0:1024],
                                kvst[:, :], label=f"dbgkv_{b}")

                emit_block(body)

        def phase_b(b):
            with ExitStack() as pst:
                pQ = [psum(pst, f"pQ{i}_{b}", [128, 512], F32) for i in range(2)]
                pO = [psum(pst, f"pO{i}_{b}", [128, 512], F32) for i in range(2)]
                pQk = psum(pst, f"pQk_{b}", [128, 16], F32)
                pT2 = psum(pst, f"pT2_{b}", [128, 512], BF16)
                pPr = psum(pst, f"pPr_{b}", [128, 1024], F32)

                def body():
                    def qproj(g):
                        TE.w("p_idnb")
                        SE.w("p_bqnc")
                        SE.w("p_ms")
                        for e in range(8):
                            if g > 0 or e >= 2:
                                pg, pb = (g, e - 2) if e >= 2 else (g - 1, e + 6)
                                TE.w(f"b_qphi_{b}_{pg}_{pb}")
                                TE.w(f"b_qt1_{b}_{pg}_{pb}")
                            for d_ in range(8):
                                TE.doi("matmul", pQ[e % 2][:, 0:SG],
                                       wq[:, d_ * 3 * D + e * 128:
                                          d_ * 3 * D + (e + 1) * 128],
                                       ht[:, d_ * S + g * SG:
                                          d_ * S + g * SG + SG],
                                       start=(d_ == 0), stop=(d_ == 7),
                                       label=(f"b_q_{b}_{g}_{e}"
                                              if d_ == 7 else None))
                            if Q_LIMIT < 2:
                                continue
                            VE.w(f"b_q_{b}_{g}_{e}")
                            if g > 0:
                                VE.w(f"b_ao_{b}_{g * SUBS - 1}")
                            VE.doi("tensor_scalar",
                                   phiq[:, e * SG:(e + 1) * SG],
                                   pQ[e % 2][:, 0:SG],
                                   bqcol[:, e:e + 1], 0.0, OP.add, OP.max,
                                   label=f"b_qu_{b}_{g}_{e}")
                            if Q_LIMIT < 3:
                                continue
                            SE.w(f"b_q_{b}_{g}_{e}")
                            SE.w(f"b_qu_{b}_{g}_{e}")
                            if g > 0 or e >= 2:
                                pg, pb = (g, e - 2) if e >= 2 else (g - 1, e + 6)
                                SE.w(f"b_qphi_{b}_{pg}_{pb}")
                            SE.do("activation", TRH[:, 0:SG], pQ[e % 2][:, 0:SG],
                                  AF.Relu, scale=negcol[:, 0:1],
                                  bias=bqncol[:, e:e + 1])
                            SE.doi("activation", qex[e % 2][:, 0:SG],
                                   TRH[:, 0:SG], AF.Exp, scale=negcol[:, 0:1],
                                   label=f"b_qt1_{b}_{g}_{e}")
                            if Q_LIMIT < 4:
                                continue
                            VE.w(f"b_qt1_{b}_{g}_{e}")
                            VE.doi("tensor_tensor",
                                   phiq[:, e * SG:(e + 1) * SG],
                                   phiq[:, e * SG:(e + 1) * SG],
                                   qex[e % 2][:, 0:SG], OP.add,
                                   label=f"b_qphi_{b}_{g}_{e}")
                        if Q_LIMIT >= 5:
                            GE.w(f"b_qphi_{b}_{g}_7")
                            if g > 0:
                                GE.w(f"b_ao_{b}_{g * SUBS - 1}")
                            GED.dma(phiqlo[:, :], phiq[64:128, :],
                                    label=f"b_qlo_{b}_{g}")
                            if DEBUG_DUMP in ("phiq", "all") and g == 0:
                                GED.dma(dbg_d[b * 256:b * 256 + 128, 0:8 * SG],
                                        phiq[:, :], label=f"dbgq_{b}")
                                GED.dma(dbg_d[b * 256 + 128:b * 256 + 192,
                                              0:8 * SG],
                                        phiqlo[:, :], label=f"dbgql_{b}")

                    def attnout(t):
                        g, s_ = divmod(t, SUBS)
                        TE.w(f"b_qlo_{b}_{g}")
                        TE.w(f"a_kvlo_{b}")
                        TE.w(f"a_kvev_{b}")
                        TE.w(f"a_k1b_{b}")
                        if t >= 1:
                            TE.w(f"b_div_{b}_{t-1}")
                            if DEBUG_DUMP == "post":
                                TE.w(f"pd_{b}_{t-1}")
                        for hh in range(HEADS):
                            e = hh // 2
                            src = phiq if hh % 2 == 0 else phiqlo
                            lw = src[0:64, e * SG + s_ * 128:
                                     e * SG + (s_ + 1) * 128]
                            kvsrc = kvst if hh % 2 == 0 else kvlo
                            rhs = kvsrc[0:64, KVSLOT[e]:KVSLOT[e] + 64]
                            bank, off = divmod(hh, 8)
                            TE.doi("matmul", pO[bank][:, off * 64:off * 64 + 64],
                                   lw, rhs, start=True, stop=True)
                        for e in range(8):
                            TE.doi("matmul", pQk[:, 2 * e:2 * e + 2],
                                   phiq[:, e * SG + s_ * 128:
                                        e * SG + (s_ + 1) * 128],
                                   k1blk[:, 2 * e:2 * e + 2],
                                   start=True, stop=True)
                        TE.doi("drain", label=f"b_ao_{b}_{t}")

                    def division(t):
                        VE.w(f"b_ao_{b}_{t}")
                        if t >= 2:
                            VE.w(f"b_ms_{b}_{t-2}")
                        VE.doi("tensor_scalar", qk1c[:, :], pQk[:, :],
                               1e-6, None, OP.max, label=f"rq1_{b}_{t}")
                        VE.w(f"rq1_{b}_{t}")
                        VE.doi("reciprocal", recq[:, :], qk1c[:, :],
                               label=f"rq2_{b}_{t}")
                        VE.w(f"rq2_{b}_{t}")
                        for bi in range(2):
                            src = bass.AP(pO[bi], 0, [[512, 128], [64, 8], [1, 64]])
                            rb = bass.AP(recq, bi * 8, [[HEADS, 128], [1, 8], [0, 64]])
                            dst = bass.AP(mixb[t % 2], bi * 512,
                                          [[D, 128], [64, 8], [1, 64]])
                            VE.doi("tensor_tensor", dst, src, rb, OP.mult,
                                   label=(f"b_div_{b}_{t}" if bi == 1 else None))
                        if DEBUG_DUMP == "post":
                            tgd = b * NT + t
                            VE.doi("tensor_copy", dstg[:, 0:16], pQk[:, :])
                            VE.doi("tensor_copy", dstg[:, 16:32], qk1c[:, :])
                            VE.doi("tensor_copy", dstg[:, 32:48], recq[:, :],
                                   label=f"pd_{b}_{t}")
                            SY.w(f"pd_{b}_{t}")
                            SY.dma(dbg_d[tgd * 128:(tgd + 1) * 128, 1024:1072],
                                   dstg[:, 0:48], label=f"pdd_{b}_{t}",
                                   sem="dso")
                        SY.w(f"b_div_{b}_{t}")
                        for hh in range(HEADS):
                            dstr = b * S + hh * (S // 16) + 8 * t
                            SY.dma(mixs_d[dstr:dstr + 8, :],
                                   mixb[t % 2][:, hh * 64:(hh + 1) * 64],
                                   label=(f"b_ms_{b}_{t}"
                                          if hh == HEADS - 1 else None),
                                   sem=f"dm{t % 2}")

                    def mixload(t):
                        if t >= 2:
                            SY.w(f"b_z_{b}_{t-2}")
                        SY.dma(h32[t % 2][:, :],
                               h_d[b, t * 128:(t + 1) * 128, :],
                               label=f"b_h_{b}_{t}", sem=f"dh{t % 2}")
                        if t >= 2:
                            SY.w(f"b_tr_{b}_{t-2}_1")
                        SY.w(f"b_ms_{b}_{NT-1}")
                        SY.w(f"b_ms_{b}_{NT-2}")
                        SY.dma(mixb2[t % 2][:, :],
                               mixs_d[b * S + t * 128:b * S + (t + 1) * 128, :],
                               label=f"b_ml_{b}_{t}", sem=f"dm{t % 2}")

                    def mixtrans(t):
                        if DEBUG_DUMP == "mix2":
                            tgd = b * NT + t
                            GE.w(f"b_ml_{b}_{t}")
                            GED.dma(dbg_d[tgd * 128:(tgd + 1) * 128, 0:1024],
                                    mixb2[t % 2][:, :], label=f"dmpm_{b}_{t}")
                        TE.w("p_idnb")
                        for r in range(2):
                            TE.w(f"b_ml_{b}_{t}")
                            if r == 1:
                                TE.w(f"b_mev_{b}_{t}_0")
                            elif t >= 1:
                                TE.w(f"b_mev_{b}_{t-1}_1")
                            for c in range(4):
                                cc = r * 4 + c
                                TE.doi("transpose",
                                       pT2[:, c * 128:(c + 1) * 128],
                                       mixb2[t % 2][:, cc * 128:(cc + 1) * 128],
                                       idnbf[:, :],
                                       label=(f"b_tr_{b}_{t}_{r}"
                                              if c == 3 else None))
                            px = VE if r == 0 else SE
                            px.w(f"b_tr_{b}_{t}_{r}")
                            dst = bass.AP(mixT[t % 2], r * 4 * 128,
                                          [[D, 128], [128, 4], [1, 128]])
                            src = bass.AP(pT2, 0, [[512, 128], [128, 4], [1, 128]])
                            px.doi("tensor_copy" if r == 0 else "copy", dst, src,
                                   label=f"b_mev_{b}_{t}_{r}")

                    def proj(t):
                        TE.w("p_idnb")
                        TE.w(f"b_mev_{b}_{t}_0")
                        TE.w(f"b_mev_{b}_{t}_1")
                        if t >= 1:
                            TE.w(f"b_pcopy_{b}_{t-1}")
                        pp = pPr
                        for d_ in range(8):
                            lw = mixT[t % 2][:, d_ * 128:(d_ + 1) * 128]
                            for g_ in range(2):
                                TE.do("matmul", pp[:, g_ * 512:(g_ + 1) * 512],
                                      lw,
                                      wp[:, d_ * D + g_ * 512:
                                         d_ * D + (g_ + 1) * 512],
                                      start=(d_ == 0), stop=False)
                        for g_ in range(2):
                            TE.doi("matmul", pp[:, g_ * 512:(g_ + 1) * 512],
                                   ones_row_bf[:, :],
                                   bp_row[:, g_ * 512:(g_ + 1) * 512],
                                   start=False, stop=True,
                                   label=(f"b_pr_{b}_{t}" if g_ == 1 else None))

                    def stats(prefix, t, mucol_from):
                        VE.doi("tensor_scalar", colsc[:, 4:5], colsc[:, 1:2],
                               1.0 / D, EPS, OP.mult, OP.add,
                               label=f"{prefix}v0_{b}_{t}")
                        VE.doi("tensor_scalar", colsc[:, 2:3],
                               colsc[:, mucol_from:mucol_from + 1],
                               1.0 / D, None, OP.mult,
                               label=f"{prefix}u_{b}_{t}")
                        SE.w(f"{prefix}u_{b}_{t}")
                        SE.doi("activation", colsc[:, 3:4], colsc[:, 2:3],
                               AF.Square, label=f"{prefix}q2_{b}_{t}")
                        VE.w(f"{prefix}q2_{b}_{t}")
                        VE.w(f"{prefix}v0_{b}_{t}")
                        VE.doi("tensor_tensor", colsc[:, 4:5], colsc[:, 4:5],
                               colsc[:, 3:4], OP.subtract,
                               label=f"{prefix}v_{b}_{t}")
                        SE.w(f"{prefix}v_{b}_{t}")
                        SE.doi("sqrt", colsc[:, 5:6], colsc[:, 4:5],
                               label=f"{prefix}sd_{b}_{t}")
                        VE.w(f"{prefix}sd_{b}_{t}")
                        VE.doi("reciprocal", colsc[:, 5:6], colsc[:, 5:6],
                               label=f"{prefix}rc_{b}_{t}")
                        VE.w(f"{prefix}rc_{b}_{t}")
                        VE.doi("scalar_tensor_tensor", colsc[:, 6:7],
                               colsc[:, 2:3], -1.0, colsc[:, 5:6],
                               OP.mult, OP.mult,
                               label=f"{prefix}mb_{b}_{t}")

                    def ln_head(t):
                        SE.w(f"b_pr_{b}_{t}")
                        if t >= 2:
                            SE.w(f"b_b2_{b}_{t-2}")
                        SE.doi("activation", S0[t % 2][:, :], pPr[:, :], AF.Copy,
                               accum_out=colsc[:, 0:1], label=f"b_pcopy_{b}_{t}")

                    def lnchain(t):
                        tgd = b * NT + t
                        if DEBUG_DUMP == "mix":
                            GE.w(f"b_div_{b}_{t}")
                            GED.dma(dbg_d[tgd * 128:(tgd + 1) * 128, :],
                                    mixb[t % 2][:, :], label=f"dbg_{b}_{t}")
                        if DEBUG_DUMP == "proj2":
                            tgd2 = b * NT + t
                            SY.w(f"b_pcopy_{b}_{t}")
                            SY.dma(dbg_d[tgd2 * 128:(tgd2 + 1) * 128, 0:1024],
                                   S0[t % 2][:, :], label=f"dmpp_{b}_{t}")
                        SE.doi("activation", TRH[:, :], S0[t % 2][:, :],
                               AF.Square, accum_out=colsc[:, 1:2],
                               label=f"b_sq1_{b}_{t}")
                        VE.w(f"b_sq1_{b}_{t}")
                        stats("b_s1", t, 0)
                        SE.w(f"b_s1mb_{b}_{t}")
                        if t >= 2:
                            SE.w(f"b_g2_{b}_{t-2}")
                        SE.doi("activation", Z0[t % 2][:, :], S0[t % 2][:, :],
                               AF.Identity, scale=colsc[:, 5:6], bias=colsc[:, 6:7],
                               label=f"b_u1_{b}_{t}")
                        if DEBUG_DUMP == "u1":
                            SY.w(f"b_u1_{b}_{t}")
                            SY.dma(dbg_d[tgd * 128:(tgd + 1) * 128, :],
                                   Z0[t % 2][:, :], label=f"dbg_{b}_{t}")
                        VE.w(f"b_u1_{b}_{t}")
                        if DEBUG_DUMP == "proj2":
                            VE.w(f"dmpp_{b}_{t}")
                        VE.doi("tensor_tensor", S0[t % 2][:, :], Z0[t % 2][:, :],
                               g1bc[:, :], OP.mult, label=f"b_g1_{b}_{t}")
                        GE.w(f"b_g1_{b}_{t}")
                        GE.w("p_bcc3_1")
                        GE.doi("tensor_tensor", Z0[t % 2][:, :], S0[t % 2][:, :],
                               b1bc[:, :], OP.add, label=f"b_b1_{b}_{t}")
                        VE.w(f"b_b1_{b}_{t}")
                        VE.w(f"b_h_{b}_{t}")
                        VE.doi("scalar_tensor_tensor", S0[t % 2][:, :],
                               h32[t % 2][:, :], 2.0, Z0[t % 2][:, :],
                               OP.mult, OP.add, label=f"b_z_{b}_{t}")
                        if DEBUG_DUMP == "z":
                            SY.w(f"b_z_{b}_{t}")
                            SY.dma(dbg_d[tgd * 128:(tgd + 1) * 128, :],
                                   S0[t % 2][:, :], label=f"dbg_{b}_{t}")
                        SE.w(f"b_z_{b}_{t}")
                        SE.do("activation", TRH[:, :], S0[t % 2][:, :],
                              AF.Identity, accum_out=colsc[:, 0:1])
                        SE.doi("activation", TRH[:, :], S0[t % 2][:, :],
                               AF.Square, accum_out=colsc[:, 1:2],
                               label=f"b_sq2_{b}_{t}")
                        VE.w(f"b_sq2_{b}_{t}")
                        stats("b_s2", t, 0)
                        SE.w(f"b_s2mb_{b}_{t}")
                        if t >= 2:
                            SE.w(f"b_x1s_{b}_{t-2}")
                        elif b > 0:
                            SE.w(f"b_x1s_{b-1}_{NT + t - 2}")
                        SE.doi("activation", X1b[t % 2][:, :], S0[t % 2][:, :],
                               AF.Identity, scale=colsc[:, 5:6], bias=colsc[:, 6:7],
                               accum_out=colsc[:, 7:8], label=f"b_x1_{b}_{t}")
                        tg = b * NT + t
                        SY.w(f"b_x1_{b}_{t}")
                        SY.dma(x1s_d[tg * 128:(tg + 1) * 128, :],
                               X1b[t % 2][:, :], label=f"b_x1s_{b}_{t}",
                               sem="dso")
                        SE.doi("activation", TRH[:, :], X1b[t % 2][:, :],
                               AF.Square, accum_out=colsc[:, 1:2],
                               label=f"b_sq3_{b}_{t}")
                        VE.w(f"b_sq3_{b}_{t}")
                        stats("b_s3", t, 7)
                        SE.w(f"b_s3mb_{b}_{t}")
                        SE.doi("activation", Z0[t % 2][:, :], X1b[t % 2][:, :],
                               AF.Identity, scale=colsc[:, 5:6], bias=colsc[:, 6:7],
                               label=f"b_u3_{b}_{t}")
                        VE.w(f"b_u3_{b}_{t}")
                        VE.doi("tensor_tensor", S0[t % 2][:, :], Z0[t % 2][:, :],
                               g2bc[:, :], OP.mult, label=f"b_g2_{b}_{t}")
                        GE.w(f"b_g2_{b}_{t}")
                        if t >= 2:
                            GE.w(f"b_xns_{b}_{t-2}")
                            GE.w(f"b_am_{b}_{t-2}")
                        elif b > 0:
                            GE.w(f"b_xns_{b-1}_{NT + t - 2}")
                        GE.doi("tensor_tensor", XNb[t % 2][:, :], S0[t % 2][:, :],
                               b2bc[:, :], OP.add, label=f"b_b2_{b}_{t}")
                        SY.w(f"b_b2_{b}_{t}")
                        SY.dma(xns_d[tg * 128:(tg + 1) * 128, :],
                               XNb[t % 2][:, :], label=f"b_xns_{b}_{t}",
                               sem="dso")
                        VE.w(f"b_b2_{b}_{t}")
                        VE.doi("tensor_reduce", amax_cols[:, tg:tg + 1],
                               XNb[t % 2][:, :], AX.X, OP.max,
                               apply_absolute_value=True,
                               label=f"b_am_{b}_{t}")

                    for g in range(NG):
                        qproj(g)
                        for s_ in range(SUBS):
                            t = g * SUBS + s_
                            attnout(t)
                            division(t)
                    for t in range(NT):
                        mixload(t)
                        mixtrans(t)
                        if t >= 1:
                            ln_head(t - 1)
                        proj(t)
                        if t >= 1:
                            lnchain(t - 1)
                    ln_head(NT - 1)
                    lnchain(NT - 1)

                emit_block(body)

        if STOP_AFTER != "prep":
            for b in range(BPC):
                phase_a(b)
                if STOP_AFTER == f"a{b}":
                    break
                phase_b(b)
                if STOP_AFTER == f"b{b}":
                    break

        stab.close()

        stf = ExitStack()
        sg = sbuf(stf, "sg", [128, 8 * D], BF16)
        wfc = [sbuf(stf, f"wfc{i}", [128, D], F32) for i in range(2)]
        ab_buf = sbuf(stf, "ab_buf", [128, D], F32)
        m_buf = sbuf(stf, "m_buf", [128, D], F32)
        sgn_buf = sbuf(stf, "sgn_buf", [128, D], F32)
        trf = sbuf(stf, "trf", [128, D], F32)
        wcols = sbuf(stf, "wcols", [128, 28], F32)
        xnt = [sbuf(stf, f"xnt{i}", [128, D], BF16) for i in range(2)]
        x1t = [sbuf(stf, f"x1t{i}", [128, D], BF16) for i in range(2)]
        Tq = [sbuf(stf, f"Tq{i}", [128, D], F32) for i in range(2)]
        Iq = [sbuf(stf, f"Iq{i}", [128, D], I32) for i in range(2)]
        Fq = sbuf(stf, "Fq", [128, D], F32)
        qb = [sbuf(stf, f"qb{i}", [128, D], BF16) for i in range(2)]
        qbT = [sbuf(stf, f"qbT{i}", [128, D], BF16) for i in range(2)]
        Sf = [sbuf(stf, f"Sf{i}", [128, D], F32) for i in range(2)]
        outb = [sbuf(stf, f"outb{i}", [128, D], F32) for i in range(2)]
        fcol = sbuf(stf, "fcol", [128, 12], F32)

        def cc_phase():
            with ExitStack() as pst:
                pp = psum(pst, "pp_cc", [128, 512], F32)

                def body():
                    VE.doi("tensor_reduce", fcol[:, 0:1], amax_cols[:, :],
                           AX.X, OP.max, label="c_lm")
                    TE.w("c_lm")
                    TE.w("p_idn32")
                    TE.doi("matmul", pp[0:1, 32:160], fcol[:, 0:1], idn32[:, :],
                           start=True, stop=True, label="c_row")
                    VE.w("c_row")
                    VE.doi("tensor_reduce", cc_sc[0:1, 0:1], pp[0:1, 32:160],
                           AX.X, OP.max, label="c_m11")
                    GE.w("c_m11")
                    GED.dma(ccin_d[:, :], cc_sc[0:1, 0:8], label="c_cin")
                    GE.w("c_cin")
                    GE.doi("collective_compute", "AllReduce", OP.max,
                           replica_groups=[list(range(NCORES))],
                           ins=[ccin_d.ap().opt()],
                           outs=[ccout_d.ap().opt()],
                           label="c_cc")
                    GE.w("c_cc")
                    GED.dma(cc_sc[0:1, 0:8], ccout_d[:, :], label="c_cout")
                    VE.w("c_cout")
                    VE.doi("tensor_scalar", cc_sc[0:1, 1:2], cc_sc[0:1, 0:1],
                           1e-8, 1.0 / QMAX, OP.max, OP.mult, label="c_sc1")
                    VE.w("c_sc1")
                    VE.doi("reciprocal", cc_sc[0:1, 2:3], cc_sc[0:1, 1:2],
                           label="c_inv")
                    VE.w("c_inv")

                    SY.dma(wfc[0][:, :], wffnT_d[0:128, :], label="c_w0")
                    SY.dma(wfc[1][:, :], wffnT_d[128:256, :], label="c_w1")
                    for c in range(8):
                        VE.w(f"c_w{c}")
                        VE.doi("tensor_reduce", wcols[:, c:c + 1],
                               wfc[c % 2][:, :], AX.X, OP.add,
                               apply_absolute_value=True, label=f"c_aw{c}")
                        if c + 2 < 8:
                            SY.w(f"c_aw{c}")
                            SY.dma(wfc[c % 2][:, :],
                                   wffnT_d[(c + 2) * 128:(c + 3) * 128, :],
                                   label=f"c_w{c+2}")
                    VE.do("drain")
                    VE.doi("tensor_reduce", wcols[:, 8:9], wcols[:, 0:8],
                           AX.X, OP.add, label="c_awsum")
                    TE.w("c_awsum")
                    TE.doi("matmul", pp[0:1, 0:1], ones_col_f[:, :],
                           wcols[:, 8:9], start=True, stop=True, label="c_tot")
                    VE.w("c_tot")
                    VE.doi("tensor_scalar", cc_sc[0:1, 3:4], pp[0:1, 0:1],
                           THRESH / float(D * D), None, OP.mult, label="c_del")
                    TE.w("c_del")
                    TE.doi("matmul", pp[:, 4:5], ones_row_f[:, :],
                           cc_sc[0:1, 3:4], start=True, stop=True, label="c_delb")
                    VE.w("c_delb")
                    VE.doi("tensor_copy", fcol[:, 1:2], pp[:, 4:5],
                           label="c_delc")
                    SY.w("c_aw6")
                    SY.dma(wfc[0][:, :], wffnT_d[0:128, :], label="c_v0")
                    SY.w("c_aw7")
                    SY.dma(wfc[1][:, :], wffnT_d[128:256, :], label="c_v1")
                    for c in range(8):
                        SE.w(f"c_v{c}")
                        SE.w("c_delc")
                        if c >= 1:
                            SE.w(f"c_awm{c-1}")
                            SE.w(f"c_S{c-1}")
                        SE.doi("activation", ab_buf[:, :], wfc[c % 2][:, :],
                               AF.Abs, label=f"c_ab{c}")
                        SE.doi("activation", sgn_buf[:, :], wfc[c % 2][:, :],
                               AF.Sign, label=f"c_sg{c}")
                        if c + 2 < 8:
                            SY.w(f"c_sg{c}")
                            SY.dma(wfc[c % 2][:, :],
                                   wffnT_d[(c + 2) * 128:(c + 3) * 128, :],
                                   label=f"c_v{c+2}")
                        VE.w(f"c_ab{c}")
                        VE.doi("tensor_scalar", m_buf[:, :], ab_buf[:, :],
                               fcol[:, 1:2], None, OP.is_gt, label=f"c_mk{c}")
                        VE.do("tensor_tensor", trf[:, :], ab_buf[:, :],
                              m_buf[:, :], OP.mult)
                        VE.doi("tensor_reduce", wcols[:, 9 + c:10 + c],
                               trf[:, :], AX.X, OP.add, label=f"c_awm{c}")
                        VE.doi("tensor_reduce", wcols[:, 17 + c:18 + c],
                               m_buf[:, :], AX.X, OP.add, label=f"c_sm{c}")
                        VE.w(f"c_sg{c}")
                        VE.doi("tensor_tensor", sg[:, c * D:(c + 1) * D],
                               sgn_buf[:, :], m_buf[:, :], OP.mult,
                               label=f"c_S{c}")
                    VE.do("drain")
                    VE.doi("tensor_reduce", wcols[:, 25:26], wcols[:, 9:17],
                           AX.X, OP.add, label="c_awm_sum")
                    VE.doi("tensor_reduce", wcols[:, 26:27], wcols[:, 17:25],
                           AX.X, OP.add, label="c_sm_sum")
                    TE.w("c_awm_sum")
                    TE.w("c_delc")
                    TE.doi("matmul", pp[0:1, 8:9], ones_col_f[:, :],
                           wcols[:, 25:26], start=True, stop=True, label="c_t1")
                    TE.w("c_sm_sum")
                    TE.doi("matmul", pp[0:1, 12:13], ones_col_f[:, :],
                           wcols[:, 26:27], start=True, stop=True, label="c_t2")
                    VE.w("c_t1")
                    VE.w("c_t2")
                    VE.doi("tensor_scalar", cc_sc[0:1, 4:5], pp[0:1, 12:13],
                           1.0, None, OP.max, label="c_sc2")
                    VE.w("c_sc2")
                    VE.doi("reciprocal", cc_sc[0:1, 5:6], cc_sc[0:1, 4:5],
                           label="c_rc2")
                    VE.w("c_rc2")
                    VE.do("tensor_tensor", alpha11[0:1, 0:1], pp[0:1, 8:9],
                          cc_sc[0:1, 5:6], OP.mult)
                    VE.do("drain")
                    VE.doi("tensor_tensor", cc_sc[0:1, 6:7], cc_sc[0:1, 1:2],
                           alpha11[0:1, 0:1], OP.mult, label="c_fs")
                    TE.w("c_inv")
                    TE.w("c_fs")
                    TE.doi("matmul", pp[:, 16:17], ones_row_f[:, :],
                           cc_sc[0:1, 2:3], start=True, stop=True, label="c_bi")
                    TE.doi("matmul", pp[:, 20:21], ones_row_f[:, :],
                           cc_sc[0:1, 6:7], start=True, stop=True, label="c_bf")
                    VE.w("c_bi")
                    VE.w("c_bf")
                    VE.doi("tensor_copy", inv_col[:, :], pp[:, 16:17],
                           label="c_invc")
                    VE.doi("tensor_copy", fs_col[:, :], pp[:, 20:21],
                           label="c_fsc")
                    if DEBUG_DUMP == "ffn":
                        GE.w("c_fsc")
                        GED.dma(dbg_d[0:128, 0:4096], sg[:, 0:4096],
                                label="sgd1", sem="dso")
                        GED.dma(dbg_d[128:256, 0:4096], sg[:, 4096:8192],
                                label="sgd2", sem="dso")
                    if DEBUG_DUMP == "cc":
                        GE.w("c_fsc")
                        GED.dma(dbg_d[0:1, 0:8], cc_sc[0:1, :], label="ccd1",
                                sem="dso")
                        GED.dma(dbg_d[1:2, 0:1], alpha11[0:1, :], label="ccd2",
                                sem="dso")
                        GED.dma(dbg_d[2:130, 0:28], wcols[:, :], label="ccd3",
                                sem="dso")

                emit_block(body)

        if STOP_AFTER is None or STOP_AFTER == "cc":
            cc_phase()

        def ffn_phase():
            NTT = BPC * NT
            with ExitStack() as pst:
                pT3 = psum(pst, "pT3", [128, 1024], BF16)
                pF = [psum(pst, f"pF{i}", [128, 1024], F32) for i in range(2)]

                def body():
                    def loads(t):
                        if t >= 2:
                            SY.w(f"f_t2_{t-2}")
                        SY.w(f"b_xns_{BPC-1}_{NT-1}")
                        SY.dma(xnt[t % 2][:, :],
                               xns_d[t * 128:(t + 1) * 128, :],
                               label=f"f_xn_{t}", sem=f"dh{t % 2}")
                        if t >= 2:
                            SCD.w(f"f_s_{t-2}")
                        SCD.w(f"b_x1s_{BPC-1}_{NT-1}")
                        SCD.dma(x1t[t % 2][:, :],
                                x1s_d[t * 128:(t + 1) * 128, :],
                                label=f"f_x1_{t}", sem=f"ds{t % 2}")

                    def quant(t):
                        VE.w(f"f_xn_{t}")
                        VE.w("c_invc")
                        if t >= 2:
                            VE.w(f"f_tr_{t-2}")
                            if DEBUG_DUMP == "ffn" and (t - 2) in (4, 5):
                                VE.w(f"qbd_{t-2}")
                            VE.w(f"f_i_{t-2}")
                        VE.do("tensor_scalar", Fq[:, :], xnt[t % 2][:, :],
                              inv_col[:, 0:1], -QMAX, OP.mult, OP.max)
                        VE.doi("tensor_scalar", Tq[t % 2][:, :], Fq[:, :],
                               QMAX, 128.0, OP.min, OP.add, label=f"f_t2_{t}")
                        SE.w(f"f_t2_{t}")
                        if t >= 2:
                            SE.w(f"f_qb_{t-2}")
                        SE.doi("copy", Iq[t % 2][:, :], Tq[t % 2][:, :],
                               label=f"f_i_{t}")
                        VE.w(f"f_i_{t}")
                        VE.do("scalar_tensor_tensor", Fq[:, :],
                              Iq[t % 2][:, :], 0.5, Tq[t % 2][:, :],
                              OP.add, OP.is_le)
                        VE.doi("scalar_tensor_tensor", qb[t % 2][:, :],
                               Iq[t % 2][:, :], -128.0, Fq[:, :],
                               OP.add, OP.add, label=f"f_qb_{t}")
                        if DEBUG_DUMP == "ffn" and t in (4, 5):
                            GE.w(f"f_qb_{t}")
                            GED.dma(dbg_d[256 + (t - 4) * 128:
                                          256 + (t - 3) * 128, 0:1024],
                                    qb[t % 2][:, :], label=f"qbd_{t}",
                                    sem="dso")

                    def trq(t):
                        TE.w(f"f_qb_{t}")
                        TE.w("p_idnb")
                        if t >= 1:
                            TE.w(f"f_qev_{t-1}_0")
                            TE.w(f"f_qev_{t-1}_1")
                        for c in range(8):
                            TE.doi("transpose", pT3[:, c * 128:(c + 1) * 128],
                                   qb[t % 2][:, c * 128:(c + 1) * 128],
                                   idnbf[:, :],
                                   label=(f"f_tr_{t}" if c == 7 else None))
                        for r, px in ((0, VE), (1, SE)):
                            px.w(f"f_tr_{t}")
                            if r == 1:
                                px.w(f"f_qev_{t}_0")
                            px.doi("tensor_copy" if r == 0 else "copy",
                                   qbT[t % 2][:, r * 512:(r + 1) * 512],
                                   pT3[:, r * 512:(r + 1) * 512],
                                   label=f"f_qev_{t}_{r}")

                    def ffnmm(t):
                        TE.w(f"f_qev_{t}_0")
                        TE.w(f"f_qev_{t}_1")
                        if t >= 2:
                            TE.w(f"f_s_{t-2}")
                        pp = pF[t % 2]
                        for d_ in range(8):
                            lw = qbT[t % 2][:, d_ * 128:(d_ + 1) * 128]
                            for g_ in range(2):
                                TE.doi("matmul",
                                       pp[:, g_ * 512:(g_ + 1) * 512], lw,
                                       sg[:, d_ * D + g_ * 512:
                                          d_ * D + (g_ + 1) * 512],
                                       start=(d_ == 0), stop=(d_ == 7),
                                       label=(f"f_mm_{t}"
                                              if d_ == 7 and g_ == 1 else None))

                    def final(t):
                        pp = pF[t % 2]
                        VE.w(f"f_mm_{t}")
                        VE.w(f"f_x1_{t}")
                        VE.w("c_fsc")
                        if t >= 2:
                            VE.w(f"f_o_{t-2}")
                        VE.doi("scalar_tensor_tensor", Sf[t % 2][:, :], pp[:, :],
                               fs_col[:, 0:1], x1t[t % 2][:, :], OP.mult, OP.add,
                               label=f"f_s_{t}")
                        SE.w(f"f_s_{t}")
                        SE.do("activation", trf[:, :], Sf[t % 2][:, :],
                              AF.Identity, accum_out=fcol[:, 2:3])
                        SE.doi("activation", trf[:, :], Sf[t % 2][:, :],
                               AF.Square, accum_out=fcol[:, 3:4],
                               label=f"f_sq_{t}")
                        VE.w(f"f_sq_{t}")
                        VE.do("tensor_scalar", fcol[:, 4:5], fcol[:, 2:3],
                              1.0 / D, None, OP.mult)
                        VE.do("drain")
                        VE.do("tensor_tensor", fcol[:, 5:6], fcol[:, 4:5],
                              fcol[:, 4:5], OP.mult)
                        VE.do("tensor_scalar", fcol[:, 6:7], fcol[:, 3:4],
                              1.0 / D, None, OP.mult)
                        VE.do("drain")
                        VE.do("tensor_tensor", fcol[:, 6:7], fcol[:, 6:7],
                              fcol[:, 5:6], OP.subtract)
                        VE.do("drain")
                        VE.doi("tensor_scalar", fcol[:, 6:7], fcol[:, 6:7],
                               EPS, None, OP.add, label=f"f_var_{t}")
                        SE.w(f"f_var_{t}")
                        SE.doi("sqrt", fcol[:, 7:8], fcol[:, 6:7],
                               label=f"f_sd_{t}")
                        VE.w(f"f_sd_{t}")
                        VE.doi("reciprocal", fcol[:, 7:8], fcol[:, 7:8],
                               label=f"f_rc_{t}")
                        VE.w(f"f_rc_{t}")
                        VE.do("tensor_tensor", fcol[:, 8:9], fcol[:, 4:5],
                              fcol[:, 7:8], OP.mult)
                        VE.do("drain")
                        VE.doi("tensor_scalar", fcol[:, 8:9], fcol[:, 8:9],
                               -1.0, None, OP.mult, label=f"f_mb_{t}")
                        SE.w(f"f_mb_{t}")
                        if t >= 2:
                            SE.w(f"f_out_{t-2}")
                        SE.doi("activation", outb[t % 2][:, :], Sf[t % 2][:, :],
                               AF.Identity, scale=fcol[:, 7:8], bias=fcol[:, 8:9],
                               label=f"f_o_{t}")
                        b_, tt = divmod(t, NT)
                        SY.w(f"f_o_{t}")
                        SY.dma(out_d[b_, tt * 128:(tt + 1) * 128, :],
                               outb[t % 2][:, :], label=f"f_out_{t}",
                               sem="dso")

                    loads(0)
                    loads(1)
                    quant(0)
                    for t in range(NTT):
                        trq(t)
                        if t + 1 < NTT:
                            quant(t + 1)
                        ffnmm(t)
                        final(t)
                        if t + 2 < NTT:
                            loads(t + 2)
                    SY.w(f"f_out_{NTT-1}")
                    SY.w(f"f_out_{NTT-2}")

                emit_block(body)

        if STOP_AFTER is None:
            ffn_phase()
        stf.close()

    return nc


def kernel(L0_attn_H, L0_attn_W_qkv, L0_attn_b_qkv, L0_attn_W_proj,
           L0_attn_b_proj, L0_attn_gamma, L0_attn_beta, L0_ffn_weight,
           L0_ffn_gamma, L0_ffn_beta):
    B, S, D_ = L0_attn_H.shape
    nc = build_nc(S=S)
    shared = {
        "wqkvT": np.ascontiguousarray(np.asarray(L0_attn_W_qkv, np.float32).T),
        "bqkv": np.asarray(L0_attn_b_qkv, np.float32).reshape(1, -1),
        "wprojT": np.ascontiguousarray(np.asarray(L0_attn_W_proj, np.float32).T),
        "bproj": np.asarray(L0_attn_b_proj, np.float32).reshape(1, -1),
        "g1": np.asarray(L0_attn_gamma, np.float32).reshape(1, -1),
        "be1": np.asarray(L0_attn_beta, np.float32).reshape(1, -1),
        "wffnT": np.ascontiguousarray(np.asarray(L0_ffn_weight, np.float32).T),
        "g2": np.asarray(L0_ffn_gamma, np.float32).reshape(1, -1),
        "be2": np.asarray(L0_ffn_beta, np.float32).reshape(1, -1),
        "idn": np.eye(128, dtype=np.float32),
    }
    H = np.ascontiguousarray(np.asarray(L0_attn_H, np.float32))
    in_maps = []
    for c in range(NCORES):
        m = dict(shared)
        m["h"] = H[c * BPC:(c + 1) * BPC]
        in_maps.append(m)
    res = run_bass_kernel_spmd(nc, in_maps, core_ids=list(range(NCORES)))
    out = np.concatenate([res.results[c]["out"] for c in range(NCORES)], axis=0)
    return np.ascontiguousarray(out.astype(np.float32))


# version 05
# speedup vs baseline: 1.0864x; 1.0084x over previous
import numpy as np
from contextlib import ExitStack

import concourse.bass as bass
import concourse.mybir as mybir
from concourse.bass_utils import run_bass_kernel_spmd

F32 = mybir.dt.float32
BF16 = mybir.dt.bfloat16
I32 = mybir.dt.int32
OP = mybir.AluOpType
AF = mybir.ActivationFunctionType
AX = mybir.AxisListType

NCORES = 8
D = 1024
HEADS = 16
BPC = 2
EPS = 1e-5
QMAX = 127.0
THRESH = 0.7

KVSLOT = [0, 65, 130, 195, 260, 325, 390, 512]
AOBANK = [(0, 7), (7, 14), (14, 16)]


class Ctx:
    def __init__(self, nc, semnames):
        self.nc = nc
        self.semnames = list(semnames)
        self.sems = {}
        self.c = {k: 0 for k in semnames}
        self.marks = {}
        self.mode = None
        self.eng = None
        self._waited = {}

    def snap(self):
        return (dict(self.c), dict(self.marks))

    def restore(self, s):
        self.c = dict(s[0])
        self.marks = dict(s[1])
        self._waited = {}

    def set_eng(self, mode, eng):
        self.mode = mode
        self.eng = eng
        self._waited = {}


class Px:
    def __init__(self, ctx, name, sem):
        self.ctx = ctx
        self.name = name
        self.sem = sem

    @property
    def live(self):
        return self.ctx.mode == self.name

    def w(self, label, extra=0):
        if LENIENT and label not in self.ctx.marks:
            return
        sem, val = self.ctx.marks[label]
        val += extra
        if val <= 0:
            return
        key = (self.name, sem)
        if self.ctx._waited.get(key, -1) >= val:
            return
        self.ctx._waited[key] = val
        if self.live:
            self.ctx.eng.wait_ge(self.ctx.sems[sem], val)

    def do(self, opname, *a, **k):
        if self.live:
            return getattr(self.ctx.eng, opname)(*a, **k)
        return None

    def doi(self, opname, *a, label=None, inc=1, **k):
        inst = self.do(opname, *a, **k)
        self.ctx.c[self.sem] += inc
        if label is not None:
            self.ctx.marks[label] = (self.sem, self.ctx.c[self.sem])
        if self.live:
            inst.then_inc(self.ctx.sems[self.sem], inc)
        return inst

    def dma(self, out, in_, label=None, sem=None):
        if sem is None:
            return self.doi("dma_start", out, in_, label=label, inc=16)
        old = self.sem
        self.sem = sem
        try:
            return self.doi("dma_start", out, in_, label=label, inc=16)
        finally:
            self.sem = old


STOP_AFTER = None
DEBUG_XOUT = False
DEBUG_DUMP = None
B_LIMIT = 99
Q_LIMIT = 99
LENIENT = False


def build_nc(S=2048):
    NT = S // 128
    SG = 512 if S >= 512 else S
    NG = S // SG
    SUBS = SG // 128

    nc = bass.Bass("TRN2", target_bir_lowering=False,
                   detect_race_conditions=False)

    h_d = nc.dram_tensor("h", [BPC, S, D], F32, kind="ExternalInput")
    wqkvT_d = nc.dram_tensor("wqkvT", [D, 3 * D], F32, kind="ExternalInput")
    bqkv_d = nc.dram_tensor("bqkv", [1, 3 * D], F32, kind="ExternalInput")
    wprojT_d = nc.dram_tensor("wprojT", [D, D], F32, kind="ExternalInput")
    bproj_d = nc.dram_tensor("bproj", [1, D], F32, kind="ExternalInput")
    g1_d = nc.dram_tensor("g1", [1, D], F32, kind="ExternalInput")
    be1_d = nc.dram_tensor("be1", [1, D], F32, kind="ExternalInput")
    wffnT_d = nc.dram_tensor("wffnT", [D, D], F32, kind="ExternalInput")
    g2_d = nc.dram_tensor("g2", [1, D], F32, kind="ExternalInput")
    be2_d = nc.dram_tensor("be2", [1, D], F32, kind="ExternalInput")
    idn_d = nc.dram_tensor("idn", [128, 128], F32, kind="ExternalInput")
    out_d = nc.dram_tensor("out", [BPC, S, D], F32, kind="ExternalOutput")
    xkind = "ExternalOutput" if DEBUG_XOUT else "Internal"
    mixs_d = nc.dram_tensor("mixs", [BPC * S, D], BF16)
    x1s_d = nc.dram_tensor("x1s", [BPC * S, D], BF16, kind=xkind)
    xns_d = nc.dram_tensor("xns", [BPC * S, D], BF16, kind=xkind)
    dbg_d = (nc.dram_tensor("dbg", [BPC * S, 4 * D], F32, kind="ExternalOutput")
             if DEBUG_DUMP else None)
    ccin_d = nc.dram_tensor("ccin", [1, 8], F32)
    ccout_d = nc.dram_tensor("ccout", [1, 8], F32)

    sems = ["dh", "dh0", "dh1", "dso", "ds0", "ds1",
            "dm0", "dm1", "dg", "pe", "ve", "se", "ge"]
    ctx = Ctx(nc, sems)
    SY = Px(ctx, "sync", "dh")
    TE = Px(ctx, "tensor", "pe")
    VE = Px(ctx, "vector", "ve")
    SE = Px(ctx, "scalar", "se")
    GE = Px(ctx, "gpsimd", "ge")
    SCD = Px(ctx, "scalar", "dsc")
    GED = Px(ctx, "gpsimd", "dg")

    def emit_block(body):
        snap = ctx.snap()
        with nc.Block() as blk:
            for engname, attr in [("sync", "sync"), ("tensor", "tensor"),
                                  ("vector", "vector"), ("scalar", "scalar"),
                                  ("gpsimd", "gpsimd")]:
                def run(eng, engname=engname):
                    ctx.restore(snap)
                    ctx.set_eng(engname, eng)
                    body()
                getattr(blk, attr)(run)

    st = ExitStack()
    with st:
        for name in sems:
            ctx.sems[name] = st.enter_context(nc.semaphore(name))

        def sbuf(stk, name, shape, dt=F32):
            return stk.enter_context(nc.sbuf_tensor(name, shape, dt))

        def psum(stk, name, shape, dt=F32):
            return stk.enter_context(nc.psum_tensor(name, shape, dt))

        wq = sbuf(st, "wq", [128, 8 * 3 * D], BF16)
        wp = sbuf(st, "wp", [128, 8 * D], BF16)
        bq_row = sbuf(st, "bq_row", [1, 3 * D], BF16)
        bp_row = sbuf(st, "bp_row", [1, D], BF16)
        g1bc = sbuf(st, "g1bc", [128, D], BF16)
        b1bc = sbuf(st, "b1bc", [128, D], BF16)
        g2bc = sbuf(st, "g2bc", [128, D], BF16)
        b2bc = sbuf(st, "b2bc", [128, D], BF16)
        idn32 = sbuf(st, "idn32", [128, 128], F32)
        idnbf = sbuf(st, "idnbf", [128, 128], BF16)
        ones_row_bf = sbuf(st, "ones_row_bf", [1, 128], BF16)
        ones_row_f = sbuf(st, "ones_row_f", [1, 128], F32)
        ones_col_f = sbuf(st, "ones_col_f", [128, 1], F32)
        ones11 = sbuf(st, "ones11", [1, 1], F32)
        negcol = sbuf(st, "negcol", [128, 1], F32)
        epscol = sbuf(st, "epscol", [128, 1], F32)
        bqcol = sbuf(st, "bqcol", [128, 8], F32)
        bqncol = sbuf(st, "bqncol", [128, 8], F32)
        amax_cols = sbuf(st, "amax_cols", [128, BPC * NT], F32)
        inv_col = sbuf(st, "inv_col", [128, 1], F32)
        fs_col = sbuf(st, "fs_col", [128, 1], F32)
        cc_sc = sbuf(st, "cc_sc", [1, 8], F32)
        alpha11 = sbuf(st, "alpha11", [1, 1], F32)

        def prep_phase():
          with ExitStack() as pst:
            pp = psum(pst, "pp_prep", [128, 512], F32)
            pp2 = psum(pst, "pp_prep2", [128, 512], F32)
            bq_f32 = sbuf(pst, "bq_f32", [1, 3 * D], F32)
            vrow_f = sbuf(pst, "vrow_f", [1, 4 * D], F32)

            def prep():
                for d_ in range(8):
                    GED.dma(wq[:, d_ * 3 * D:(d_ + 1) * 3 * D],
                            wqkvT_d[d_ * 128:(d_ + 1) * 128, :],
                            label=f"p_wq{d_}")
                for d_ in range(8):
                    GED.dma(wp[:, d_ * D:(d_ + 1) * D],
                            wprojT_d[d_ * 128:(d_ + 1) * 128, :],
                            label=f"p_wp{d_}")
                GED.dma(bq_row[:, :], bqkv_d[:, :], label="p_bqr")
                GED.dma(bp_row[:, :], bproj_d[:, :], label="p_bpr")
                GED.dma(idnbf[:, :], idn_d[:, :], label="p_idnb")
                SY.dma(bq_f32[:, :], bqkv_d[:, :], label="p_bqf")
                SY.dma(idn32[:, :], idn_d[:, :], label="p_idn32")
                SY.dma(vrow_f[:, 0:D], g1_d[:, :], label="p_g1")
                SY.dma(vrow_f[:, D:2 * D], be1_d[:, :], label="p_b1")
                SY.dma(vrow_f[:, 2 * D:3 * D], g2_d[:, :], label="p_g2")
                SY.dma(vrow_f[:, 3 * D:4 * D], be2_d[:, :], label="p_b2")
                GE.doi("memset", ones_row_bf[:, :], 1.0)
                GE.doi("memset", ones_row_f[:, :], 1.0)
                GE.doi("memset", ones_col_f[:, :], 1.0)
                GE.doi("memset", ones11[:, :], 1.0)
                GE.doi("memset", negcol[:, :], -1.0)
                GE.doi("memset", epscol[:, :], EPS)
                GE.doi("memset", cc_sc[:, :], 0.0)
                GE.doi("memset", amax_cols[:, :], 0.0, label="p_ms")

                if True:
                    VE.w("p_ms")
                    for _ in range(4):
                        VE.doi("reciprocal", inv_col[:, 0:1], ones_col_f[:, :])
                    TE.w("p_b2")
                    TE.w("p_idnb")
                    TE.w("p_ms")
                    for e in range(8):
                        TE.doi("matmul", pp[:, e:e + 1],
                               bq_f32[:, e * 128:(e + 1) * 128],
                               ones11[:, :], start=True, stop=True,
                               label=f"p_bqc{e}")
                    VE.w("p_bqc7")
                    VE.doi("tensor_copy", bqcol[:, :], pp[:, 0:8], label="p_bqcc")
                    VE.doi("tensor_scalar", bqncol[:, :], bqcol[:, :], -1.0, None,
                           OP.mult, label="p_bqnc")
                    dsts = [g1bc, b1bc, g2bc, b2bc]
                    labs = ["p_g1", "p_b1", "p_g2", "p_b2"]
                    for i in range(4):
                        TE.w(labs[i])
                        TE.w("p_bqcc" if i == 0 else f"p_bcc{i-1}_0")
                        TE.doi("matmul", pp[:, 0:512], ones_row_f[:, :],
                               vrow_f[:, i * D: i * D + 512],
                               start=True, stop=True, label=f"p_bc{i}_0")
                        if i > 0:
                            TE.w(f"p_bcc{i-1}_1")
                        TE.doi("matmul", pp2[:, 0:512], ones_row_f[:, :],
                               vrow_f[:, i * D + 512: i * D + 1024],
                               start=True, stop=True, label=f"p_bc{i}_1")
                        VE.w(f"p_bc{i}_0")
                        VE.doi("tensor_copy", dsts[i][:, 0:512], pp[:, 0:512],
                               label=f"p_bcc{i}_0")
                        VE.w(f"p_bc{i}_1")
                        VE.doi("tensor_copy", dsts[i][:, 512:1024], pp2[:, 0:512],
                               label=f"p_bcc{i}_1")

            emit_block(prep)
        prep_phase()

        stab = ExitStack()
        ht = sbuf(stab, "ht", [128, 8 * S], BF16)
        h32 = [sbuf(stab, f"h32_{i}", [128, D], F32) for i in range(2)]
        phik = [sbuf(stab, f"phik{i}", [128, D], BF16) for i in range(2)]
        vbf = [sbuf(stab, f"vbf{i}", [128, HEADS * 65], BF16) for i in range(2)]
        kvst = sbuf(stab, "kvst", [128, 1024], BF16)
        kvlo = sbuf(stab, "kvlo", [64, 1024], BF16)
        phiq = sbuf(stab, "phiq", [128, 8 * SG], BF16)
        phiqlo = sbuf(stab, "phiqlo", [64, 8 * SG], BF16)
        qex = [sbuf(stab, f"qex{i}", [128, SG], F32) for i in range(2)]
        mixb = [sbuf(stab, f"mixb{i}", [128, D], BF16) for i in range(2)]
        mixb2 = [sbuf(stab, f"mixb2_{i}", [128, D], BF16) for i in range(2)]
        mixT = [sbuf(stab, f"mixT{i}", [128, D], BF16) for i in range(2)]
        S0 = [sbuf(stab, f"S0_{i}", [128, D], F32) for i in range(2)]
        Z0 = [sbuf(stab, f"Z0_{i}", [128, D], F32) for i in range(2)]
        TRH = sbuf(stab, "TRH", [128, D], F32)
        X1b = [sbuf(stab, f"X1b{i}", [128, D], BF16) for i in range(2)]
        XNb = [sbuf(stab, f"XNb{i}", [128, D], BF16) for i in range(2)]
        dstg = sbuf(stab, "dstg", [128, 1024], F32)
        k1blk = sbuf(stab, "k1blk", [128, 16], BF16)
        qk1c = sbuf(stab, "qk1c", [128, HEADS], F32)
        recq = sbuf(stab, "recq", [128, HEADS], F32)
        colsc = sbuf(stab, "colsc", [128, 12], F32)

        def phase_a(b):
            with ExitStack() as pst:
                pT = psum(pst, f"pT_{b}", [128, 1024], F32)
                pK = psum(pst, f"pK_{b}", [128, 1024], F32)
                pV = psum(pst, f"pV_{b}", [128, 1024], F32)
                pacc = psum(pst, f"pacc_{b}", [128, 1024], F32)

                def body():
                    VE.doi("memset", pacc[:, :], 0.0, label=f"a_pz_{b}")
                    for i in range(2):
                        GE.doi("memset",
                               bass.AP(vbf[i], 64,
                                       [[HEADS * 65, 128], [65, HEADS], [1, 1]]),
                               1.0, label=f"a_vb1_{b}_{i}")

                    def load_h(t):
                        if t >= 2:
                            SY.w(f"a_tr_{b}_{t-2}")
                        SY.dma(h32[t % 2][:, :],
                               h_d[b, t * 128:(t + 1) * 128, :],
                               label=f"a_h_{b}_{t}", sem=f"dh{t % 2}")

                    def trans(t):
                        TE.w(f"a_h_{b}_{t}")
                        if t == 0:
                            TE.w("p_b2")
                        if t >= 1:
                            TE.w(f"a_evac_{b}_{t-1}")
                        for c in range(8):
                            TE.doi("transpose", pT[:, c * 128:(c + 1) * 128],
                                   h32[t % 2][:, c * 128:(c + 1) * 128],
                                   idn32[:, :],
                                   label=(f"a_tr_{b}_{t}" if c == 7 else None))

                    def evac(t):
                        VE.w(f"a_tr_{b}_{t}")
                        dst = bass.AP(ht, t * 128, [[8 * S, 128], [S, 8], [1, 128]])
                        src = bass.AP(pT, 0, [[1024, 128], [128, 8], [1, 128]])
                        VE.doi("tensor_copy", dst, src, label=f"a_evac_{b}_{t}")

                    def kvmm(t):
                        if t == 0:
                            TE.w("p_idnb")
                            TE.w("p_ms")
                        TE.w(f"a_evac_{b}_{t}")
                        if t >= 1:
                            TE.w(f"a_t1_{b}_{t-1}")
                            TE.w(f"a_vcopy_{b}_{t-1}")
                        for d_ in range(8):
                            lw = ht[:, d_ * S + t * 128: d_ * S + (t + 1) * 128]
                            for g_ in range(2):
                                TE.do("matmul", pK[:, g_ * 512:(g_ + 1) * 512], lw,
                                      wq[:, d_ * 3 * D + D + g_ * 512:
                                         d_ * 3 * D + D + (g_ + 1) * 512],
                                      start=(d_ == 0), stop=False)
                            for g_ in range(2):
                                TE.do("matmul", pV[:, g_ * 512:(g_ + 1) * 512], lw,
                                      wq[:, d_ * 3 * D + 2 * D + g_ * 512:
                                         d_ * 3 * D + 2 * D + (g_ + 1) * 512],
                                      start=(d_ == 0), stop=False)
                        for g_ in range(2):
                            TE.do("matmul", pK[:, g_ * 512:(g_ + 1) * 512],
                                  ones_row_bf[:, :],
                                  bq_row[:, D + g_ * 512: D + (g_ + 1) * 512],
                                  start=False, stop=True)
                        for g_ in range(2):
                            TE.doi("matmul", pV[:, g_ * 512:(g_ + 1) * 512],
                                   ones_row_bf[:, :],
                                   bq_row[:, 2 * D + g_ * 512:
                                          2 * D + (g_ + 1) * 512],
                                   start=False, stop=True,
                                   label=(f"a_kv_{b}_{t}" if g_ == 1 else None))

                    def phi(t):
                        if t == 0:
                            SE.w("p_ms")
                        SE.w(f"a_kv_{b}_{t}")
                        if t >= 2:
                            SE.w(f"a_acc_{b}_{t-2}")
                            SE.w(f"a_phi_{b}_{t-2}")
                        SE.do("activation", phik[t % 2][:, :], pK[:, :], AF.Relu)
                        SE.do("activation", TRH[:, :], pK[:, :], AF.Relu,
                              scale=negcol[:, 0:1])
                        SE.doi("activation", Z0[t % 2][:, :], TRH[:, :], AF.Exp,
                               scale=negcol[:, 0:1], label=f"a_t1_{b}_{t}")
                        VE.w(f"a_t1_{b}_{t}")
                        if t >= 2:
                            VE.w(f"a_acc_{b}_{t-2}")
                        VE.doi("tensor_tensor", phik[t % 2][:, :],
                               phik[t % 2][:, :], Z0[t % 2][:, :], OP.add,
                               label=f"a_phi_{b}_{t}")
                        VE.w(f"a_kv_{b}_{t}")
                        if t >= 2:
                            VE.w(f"a_acc_{b}_{t-2}")
                        if t == 0:
                            VE.w(f"a_vb1_{b}_1")
                        vdst = bass.AP(vbf[t % 2], 0,
                                       [[HEADS * 65, 128], [65, HEADS], [1, 64]])
                        vsrc = bass.AP(pV, 0, [[1024, 128], [64, HEADS], [1, 64]])
                        VE.doi("tensor_copy", vdst, vsrc, label=f"a_vcopy_{b}_{t}")

                    def kvacc(t):
                        TE.w(f"a_phi_{b}_{t}")
                        TE.w(f"a_vcopy_{b}_{t}")
                        if t == 0:
                            TE.w(f"a_pz_{b}")
                        for hh in range(HEADS):
                            par = hh % 2
                            slot = KVSLOT[hh // 2]
                            tp = (0, 64) if par else (0, 0)
                            TE.doi("matmul",
                                   pacc[par * 64:par * 64 + 64, slot:slot + 65],
                                   phik[t % 2][:, hh * 64:(hh + 1) * 64],
                                   vbf[t % 2][:, hh * 65:(hh + 1) * 65],
                                   start=False, stop=False,
                                   skip_group_check=True,
                                   tile_position=tp,
                                   label=(f"a_acc_{b}_{t}"
                                          if hh == HEADS - 1 else None))

                    load_h(0)
                    trans(0)
                    evac(0)
                    for t in range(NT):
                        if t + 1 < NT:
                            load_h(t + 1)
                        kvmm(t)
                        if t + 1 < NT:
                            trans(t + 1)
                            evac(t + 1)
                        phi(t)
                        if t >= 1:
                            kvacc(t - 1)
                    kvacc(NT - 1)
                    VE.w(f"a_acc_{b}_{NT-1}")
                    VE.doi("tensor_copy", kvst[:, :], pacc[:, :],
                           label=f"a_kvev_{b}")
                    VE.do("drain")
                    VE.do("memset", k1blk[:, :], 0.0)
                    VE.do("drain")
                    for e_ in range(8):
                        VE.do("tensor_copy",
                              k1blk[0:64, 2 * e_:2 * e_ + 1],
                              kvst[0:64, KVSLOT[e_] + 64:KVSLOT[e_] + 65])
                        VE.doi("tensor_copy",
                               k1blk[64:128, 2 * e_ + 1:2 * e_ + 2],
                               kvst[64:128, KVSLOT[e_] + 64:KVSLOT[e_] + 65],
                               label=(f"a_k1b_{b}" if e_ == 7 else None))
                    GE.w(f"a_kvev_{b}")
                    GED.dma(kvlo[:, :], kvst[64:128, :], label=f"a_kvlo_{b}")
                    if DEBUG_DUMP in ("kv", "all"):
                        GED.dma(dbg_d[512 + b * 128:512 + (b + 1) * 128, 0:1024],
                                kvst[:, :], label=f"dbgkv_{b}")

                emit_block(body)

        def phase_b(b):
            with ExitStack() as pst:
                pQ = [psum(pst, f"pQ{i}_{b}", [128, 512], F32) for i in range(2)]
                pO = [psum(pst, f"pO{i}_{b}", [128, 512], F32) for i in range(2)]
                pQk = psum(pst, f"pQk_{b}", [128, 16], F32)
                pT2 = psum(pst, f"pT2_{b}", [128, 512], BF16)
                pPr = psum(pst, f"pPr_{b}", [128, 1024], F32)

                def body():
                    def qproj(g):
                        TE.w("p_idnb")
                        SE.w("p_bqnc")
                        SE.w("p_ms")
                        for e in range(8):
                            if g > 0 or e >= 2:
                                pg, pb = (g, e - 2) if e >= 2 else (g - 1, e + 6)
                                TE.w(f"b_qphi_{b}_{pg}_{pb}")
                                TE.w(f"b_qt1_{b}_{pg}_{pb}")
                            for d_ in range(8):
                                TE.doi("matmul", pQ[e % 2][:, 0:SG],
                                       wq[:, d_ * 3 * D + e * 128:
                                          d_ * 3 * D + (e + 1) * 128],
                                       ht[:, d_ * S + g * SG:
                                          d_ * S + g * SG + SG],
                                       start=(d_ == 0), stop=(d_ == 7),
                                       label=(f"b_q_{b}_{g}_{e}"
                                              if d_ == 7 else None))
                            if Q_LIMIT < 2:
                                continue
                            VE.w(f"b_q_{b}_{g}_{e}")
                            if g > 0:
                                VE.w(f"b_ao_{b}_{g * SUBS - 1}")
                            VE.doi("tensor_scalar",
                                   phiq[:, e * SG:(e + 1) * SG],
                                   pQ[e % 2][:, 0:SG],
                                   bqcol[:, e:e + 1], 0.0, OP.add, OP.max,
                                   label=f"b_qu_{b}_{g}_{e}")
                            if Q_LIMIT < 3:
                                continue
                            SE.w(f"b_q_{b}_{g}_{e}")
                            SE.w(f"b_qu_{b}_{g}_{e}")
                            if g > 0 or e >= 2:
                                pg, pb = (g, e - 2) if e >= 2 else (g - 1, e + 6)
                                SE.w(f"b_qphi_{b}_{pg}_{pb}")
                            SE.do("activation", TRH[:, 0:SG], pQ[e % 2][:, 0:SG],
                                  AF.Relu, scale=negcol[:, 0:1],
                                  bias=bqncol[:, e:e + 1])
                            SE.doi("activation", qex[e % 2][:, 0:SG],
                                   TRH[:, 0:SG], AF.Exp, scale=negcol[:, 0:1],
                                   label=f"b_qt1_{b}_{g}_{e}")
                            if Q_LIMIT < 4:
                                continue
                            VE.w(f"b_qt1_{b}_{g}_{e}")
                            VE.doi("tensor_tensor",
                                   phiq[:, e * SG:(e + 1) * SG],
                                   phiq[:, e * SG:(e + 1) * SG],
                                   qex[e % 2][:, 0:SG], OP.add,
                                   label=f"b_qphi_{b}_{g}_{e}")
                        if Q_LIMIT >= 5:
                            GE.w(f"b_qphi_{b}_{g}_7")
                            if g > 0:
                                GE.w(f"b_ao_{b}_{g * SUBS - 1}")
                            GED.dma(phiqlo[:, :], phiq[64:128, :],
                                    label=f"b_qlo_{b}_{g}")
                            if DEBUG_DUMP in ("phiq", "all") and g == 0:
                                GED.dma(dbg_d[b * 256:b * 256 + 128, 0:8 * SG],
                                        phiq[:, :], label=f"dbgq_{b}")
                                GED.dma(dbg_d[b * 256 + 128:b * 256 + 192,
                                              0:8 * SG],
                                        phiqlo[:, :], label=f"dbgql_{b}")

                    def attnout(t):
                        g, s_ = divmod(t, SUBS)
                        TE.w(f"b_qlo_{b}_{g}")
                        TE.w(f"a_kvlo_{b}")
                        TE.w(f"a_kvev_{b}")
                        TE.w(f"a_k1b_{b}")
                        if t >= 1:
                            TE.w(f"b_div_{b}_{t-1}")
                            if DEBUG_DUMP == "post":
                                TE.w(f"pd_{b}_{t-1}")
                        for hh in range(HEADS):
                            e = hh // 2
                            src = phiq if hh % 2 == 0 else phiqlo
                            lw = src[0:64, e * SG + s_ * 128:
                                     e * SG + (s_ + 1) * 128]
                            kvsrc = kvst if hh % 2 == 0 else kvlo
                            rhs = kvsrc[0:64, KVSLOT[e]:KVSLOT[e] + 64]
                            bank, off = divmod(hh, 8)
                            TE.doi("matmul", pO[bank][:, off * 64:off * 64 + 64],
                                   lw, rhs, start=True, stop=True)
                        for e in range(8):
                            TE.doi("matmul", pQk[:, 2 * e:2 * e + 2],
                                   phiq[:, e * SG + s_ * 128:
                                        e * SG + (s_ + 1) * 128],
                                   k1blk[:, 2 * e:2 * e + 2],
                                   start=True, stop=True)
                        TE.doi("drain", label=f"b_ao_{b}_{t}")

                    def division(t):
                        VE.w(f"b_ao_{b}_{t}")
                        if t >= 2:
                            VE.w(f"b_ms_{b}_{t-2}")
                        VE.doi("tensor_scalar", qk1c[:, :], pQk[:, :],
                               1e-6, None, OP.max, label=f"rq1_{b}_{t}")
                        VE.w(f"rq1_{b}_{t}")
                        VE.doi("reciprocal", recq[:, :], qk1c[:, :],
                               label=f"rq2_{b}_{t}")
                        VE.w(f"rq2_{b}_{t}")
                        for bi in range(2):
                            src = bass.AP(pO[bi], 0, [[512, 128], [64, 8], [1, 64]])
                            rb = bass.AP(recq, bi * 8, [[HEADS, 128], [1, 8], [0, 64]])
                            dst = bass.AP(mixb[t % 2], bi * 512,
                                          [[D, 128], [64, 8], [1, 64]])
                            VE.doi("tensor_tensor", dst, src, rb, OP.mult,
                                   label=(f"b_div_{b}_{t}" if bi == 1 else None))
                        if DEBUG_DUMP == "post":
                            tgd = b * NT + t
                            VE.doi("tensor_copy", dstg[:, 0:16], pQk[:, :])
                            VE.doi("tensor_copy", dstg[:, 16:32], qk1c[:, :])
                            VE.doi("tensor_copy", dstg[:, 32:48], recq[:, :],
                                   label=f"pd_{b}_{t}")
                            SY.w(f"pd_{b}_{t}")
                            SY.dma(dbg_d[tgd * 128:(tgd + 1) * 128, 1024:1072],
                                   dstg[:, 0:48], label=f"pdd_{b}_{t}",
                                   sem="dso")
                        SY.w(f"b_div_{b}_{t}")
                        for hh in range(HEADS):
                            dstr = b * S + hh * (S // 16) + 8 * t
                            SY.dma(mixs_d[dstr:dstr + 8, :],
                                   mixb[t % 2][:, hh * 64:(hh + 1) * 64],
                                   label=(f"b_ms_{b}_{t}"
                                          if hh == HEADS - 1 else None),
                                   sem=f"dm{t % 2}")

                    def mixload(t):
                        if t >= 2:
                            SY.w(f"b_z_{b}_{t-2}")
                        SY.dma(h32[t % 2][:, :],
                               h_d[b, t * 128:(t + 1) * 128, :],
                               label=f"b_h_{b}_{t}", sem=f"dh{t % 2}")
                        if t >= 2:
                            SY.w(f"b_tr_{b}_{t-2}_1")
                        SY.w(f"b_ms_{b}_{NT-1}")
                        SY.w(f"b_ms_{b}_{NT-2}")
                        SY.dma(mixb2[t % 2][:, :],
                               mixs_d[b * S + t * 128:b * S + (t + 1) * 128, :],
                               label=f"b_ml_{b}_{t}", sem=f"dm{t % 2}")

                    def mixtrans(t):
                        if DEBUG_DUMP == "mix2":
                            tgd = b * NT + t
                            GE.w(f"b_ml_{b}_{t}")
                            GED.dma(dbg_d[tgd * 128:(tgd + 1) * 128, 0:1024],
                                    mixb2[t % 2][:, :], label=f"dmpm_{b}_{t}")
                        TE.w("p_idnb")
                        for r in range(2):
                            TE.w(f"b_ml_{b}_{t}")
                            if r == 1:
                                TE.w(f"b_mev_{b}_{t}_0")
                            elif t >= 1:
                                TE.w(f"b_mev_{b}_{t-1}_1")
                            for c in range(4):
                                cc = r * 4 + c
                                TE.doi("transpose",
                                       pT2[:, c * 128:(c + 1) * 128],
                                       mixb2[t % 2][:, cc * 128:(cc + 1) * 128],
                                       idnbf[:, :],
                                       label=(f"b_tr_{b}_{t}_{r}"
                                              if c == 3 else None))
                            px = VE if r == 0 else SE
                            px.w(f"b_tr_{b}_{t}_{r}")
                            dst = bass.AP(mixT[t % 2], r * 4 * 128,
                                          [[D, 128], [128, 4], [1, 128]])
                            src = bass.AP(pT2, 0, [[512, 128], [128, 4], [1, 128]])
                            px.doi("tensor_copy" if r == 0 else "copy", dst, src,
                                   label=f"b_mev_{b}_{t}_{r}")

                    def proj(t):
                        TE.w("p_idnb")
                        TE.w(f"b_mev_{b}_{t}_0")
                        TE.w(f"b_mev_{b}_{t}_1")
                        if t >= 1:
                            TE.w(f"b_pcopy_{b}_{t-1}")
                        pp = pPr
                        for d_ in range(8):
                            lw = mixT[t % 2][:, d_ * 128:(d_ + 1) * 128]
                            for g_ in range(2):
                                TE.do("matmul", pp[:, g_ * 512:(g_ + 1) * 512],
                                      lw,
                                      wp[:, d_ * D + g_ * 512:
                                         d_ * D + (g_ + 1) * 512],
                                      start=(d_ == 0), stop=False)
                        for g_ in range(2):
                            TE.doi("matmul", pp[:, g_ * 512:(g_ + 1) * 512],
                                   ones_row_bf[:, :],
                                   bp_row[:, g_ * 512:(g_ + 1) * 512],
                                   start=False, stop=True,
                                   label=(f"b_pr_{b}_{t}" if g_ == 1 else None))

                    def stats(prefix, t, mucol_from):
                        VE.doi("tensor_scalar", colsc[:, 4:5], colsc[:, 1:2],
                               1.0 / D, EPS, OP.mult, OP.add,
                               label=f"{prefix}v0_{b}_{t}")
                        VE.doi("tensor_scalar", colsc[:, 2:3],
                               colsc[:, mucol_from:mucol_from + 1],
                               1.0 / D, None, OP.mult,
                               label=f"{prefix}u_{b}_{t}")
                        SE.w(f"{prefix}u_{b}_{t}")
                        SE.doi("activation", colsc[:, 3:4], colsc[:, 2:3],
                               AF.Square, label=f"{prefix}q2_{b}_{t}")
                        VE.w(f"{prefix}q2_{b}_{t}")
                        VE.w(f"{prefix}v0_{b}_{t}")
                        VE.doi("tensor_tensor", colsc[:, 4:5], colsc[:, 4:5],
                               colsc[:, 3:4], OP.subtract,
                               label=f"{prefix}v_{b}_{t}")
                        SE.w(f"{prefix}v_{b}_{t}")
                        SE.doi("sqrt", colsc[:, 5:6], colsc[:, 4:5],
                               label=f"{prefix}sd_{b}_{t}")
                        VE.w(f"{prefix}sd_{b}_{t}")
                        VE.doi("reciprocal", colsc[:, 5:6], colsc[:, 5:6],
                               label=f"{prefix}rc_{b}_{t}")
                        VE.w(f"{prefix}rc_{b}_{t}")
                        VE.doi("scalar_tensor_tensor", colsc[:, 6:7],
                               colsc[:, 2:3], -1.0, colsc[:, 5:6],
                               OP.mult, OP.mult,
                               label=f"{prefix}mb_{b}_{t}")

                    def ln_head(t):
                        SE.w(f"b_pr_{b}_{t}")
                        if t >= 2:
                            SE.w(f"b_b2_{b}_{t-2}")
                        SE.doi("activation", S0[t % 2][:, :], pPr[:, :], AF.Copy,
                               accum_out=colsc[:, 0:1], label=f"b_pcopy_{b}_{t}")

                    def lnchain(t):
                        tgd = b * NT + t
                        if DEBUG_DUMP == "mix":
                            GE.w(f"b_div_{b}_{t}")
                            GED.dma(dbg_d[tgd * 128:(tgd + 1) * 128, :],
                                    mixb[t % 2][:, :], label=f"dbg_{b}_{t}")
                        if DEBUG_DUMP == "proj2":
                            tgd2 = b * NT + t
                            SY.w(f"b_pcopy_{b}_{t}")
                            SY.dma(dbg_d[tgd2 * 128:(tgd2 + 1) * 128, 0:1024],
                                   S0[t % 2][:, :], label=f"dmpp_{b}_{t}")
                        SE.doi("activation", TRH[:, :], S0[t % 2][:, :],
                               AF.Square, accum_out=colsc[:, 1:2],
                               label=f"b_sq1_{b}_{t}")
                        VE.w(f"b_sq1_{b}_{t}")
                        stats("b_s1", t, 0)
                        SE.w(f"b_s1mb_{b}_{t}")
                        if t >= 2:
                            SE.w(f"b_g2_{b}_{t-2}")
                        SE.doi("activation", Z0[t % 2][:, :], S0[t % 2][:, :],
                               AF.Identity, scale=colsc[:, 5:6], bias=colsc[:, 6:7],
                               label=f"b_u1_{b}_{t}")
                        if DEBUG_DUMP == "u1":
                            SY.w(f"b_u1_{b}_{t}")
                            SY.dma(dbg_d[tgd * 128:(tgd + 1) * 128, :],
                                   Z0[t % 2][:, :], label=f"dbg_{b}_{t}")
                        VE.w(f"b_u1_{b}_{t}")
                        if DEBUG_DUMP == "proj2":
                            VE.w(f"dmpp_{b}_{t}")
                        VE.doi("tensor_tensor", S0[t % 2][:, :], Z0[t % 2][:, :],
                               g1bc[:, :], OP.mult, label=f"b_g1_{b}_{t}")
                        GE.w(f"b_g1_{b}_{t}")
                        GE.w("p_bcc3_1")
                        GE.doi("tensor_tensor", Z0[t % 2][:, :], S0[t % 2][:, :],
                               b1bc[:, :], OP.add, label=f"b_b1_{b}_{t}")
                        VE.w(f"b_b1_{b}_{t}")
                        VE.w(f"b_h_{b}_{t}")
                        VE.doi("scalar_tensor_tensor", S0[t % 2][:, :],
                               h32[t % 2][:, :], 2.0, Z0[t % 2][:, :],
                               OP.mult, OP.add, label=f"b_z_{b}_{t}")
                        if DEBUG_DUMP == "z":
                            SY.w(f"b_z_{b}_{t}")
                            SY.dma(dbg_d[tgd * 128:(tgd + 1) * 128, :],
                                   S0[t % 2][:, :], label=f"dbg_{b}_{t}")
                        SE.w(f"b_z_{b}_{t}")
                        SE.do("activation", TRH[:, :], S0[t % 2][:, :],
                              AF.Identity, accum_out=colsc[:, 0:1])
                        SE.doi("activation", TRH[:, :], S0[t % 2][:, :],
                               AF.Square, accum_out=colsc[:, 1:2],
                               label=f"b_sq2_{b}_{t}")
                        VE.w(f"b_sq2_{b}_{t}")
                        stats("b_s2", t, 0)
                        SE.w(f"b_s2mb_{b}_{t}")
                        if t >= 2:
                            SE.w(f"b_x1s_{b}_{t-2}")
                        elif b > 0:
                            SE.w(f"b_x1s_{b-1}_{NT + t - 2}")
                        SE.doi("activation", X1b[t % 2][:, :], S0[t % 2][:, :],
                               AF.Identity, scale=colsc[:, 5:6], bias=colsc[:, 6:7],
                               accum_out=colsc[:, 7:8], label=f"b_x1_{b}_{t}")
                        tg = b * NT + t
                        SY.w(f"b_x1_{b}_{t}")
                        SY.dma(x1s_d[tg * 128:(tg + 1) * 128, :],
                               X1b[t % 2][:, :], label=f"b_x1s_{b}_{t}",
                               sem="dso")
                        SE.doi("activation", TRH[:, :], X1b[t % 2][:, :],
                               AF.Square, accum_out=colsc[:, 1:2],
                               label=f"b_sq3_{b}_{t}")
                        VE.w(f"b_sq3_{b}_{t}")
                        stats("b_s3", t, 7)
                        SE.w(f"b_s3mb_{b}_{t}")
                        SE.doi("activation", Z0[t % 2][:, :], X1b[t % 2][:, :],
                               AF.Identity, scale=colsc[:, 5:6], bias=colsc[:, 6:7],
                               label=f"b_u3_{b}_{t}")
                        VE.w(f"b_u3_{b}_{t}")
                        VE.doi("tensor_tensor", S0[t % 2][:, :], Z0[t % 2][:, :],
                               g2bc[:, :], OP.mult, label=f"b_g2_{b}_{t}")
                        GE.w(f"b_g2_{b}_{t}")
                        if t >= 2:
                            GE.w(f"b_xns_{b}_{t-2}")
                            GE.w(f"b_am_{b}_{t-2}")
                        elif b > 0:
                            GE.w(f"b_xns_{b-1}_{NT + t - 2}")
                        GE.doi("tensor_tensor", XNb[t % 2][:, :], S0[t % 2][:, :],
                               b2bc[:, :], OP.add, label=f"b_b2_{b}_{t}")
                        SY.w(f"b_b2_{b}_{t}")
                        SY.dma(xns_d[tg * 128:(tg + 1) * 128, :],
                               XNb[t % 2][:, :], label=f"b_xns_{b}_{t}",
                               sem="dso")
                        VE.w(f"b_b2_{b}_{t}")
                        VE.doi("tensor_reduce", amax_cols[:, tg:tg + 1],
                               XNb[t % 2][:, :], AX.X, OP.max,
                               apply_absolute_value=True,
                               label=f"b_am_{b}_{t}")

                    for g in range(NG):
                        qproj(g)
                        for s_ in range(SUBS):
                            t = g * SUBS + s_
                            attnout(t)
                            division(t)
                    for t in range(NT):
                        mixload(t)
                        mixtrans(t)
                        if t >= 1:
                            ln_head(t - 1)
                        proj(t)
                        if t >= 1:
                            lnchain(t - 1)
                    ln_head(NT - 1)
                    lnchain(NT - 1)

                emit_block(body)

        if STOP_AFTER != "prep":
            for b in range(BPC):
                phase_a(b)
                if STOP_AFTER == f"a{b}":
                    break
                phase_b(b)
                if STOP_AFTER == f"b{b}":
                    break

        stab.close()

        stf = ExitStack()
        sg = sbuf(stf, "sg", [128, 8 * D], BF16)
        wfc = [sbuf(stf, f"wfc{i}", [128, D], F32) for i in range(2)]
        ab_buf = sbuf(stf, "ab_buf", [128, D], F32)
        m_buf = sbuf(stf, "m_buf", [128, D], F32)
        sgn_buf = sbuf(stf, "sgn_buf", [128, D], F32)
        trf = sbuf(stf, "trf", [128, D], F32)
        wcols = sbuf(stf, "wcols", [128, 28], F32)
        xnt = [sbuf(stf, f"xnt{i}", [128, D], BF16) for i in range(2)]
        x1t = [sbuf(stf, f"x1t{i}", [128, D], BF16) for i in range(2)]
        Tq = [sbuf(stf, f"Tq{i}", [128, D], F32) for i in range(2)]
        Iq = [sbuf(stf, f"Iq{i}", [128, D], I32) for i in range(2)]
        Fq = sbuf(stf, "Fq", [128, D], F32)
        qb = [sbuf(stf, f"qb{i}", [128, D], BF16) for i in range(2)]
        qbT = [sbuf(stf, f"qbT{i}", [128, D], BF16) for i in range(2)]
        Sf = [sbuf(stf, f"Sf{i}", [128, D], F32) for i in range(2)]
        outb = [sbuf(stf, f"outb{i}", [128, D], F32) for i in range(2)]
        fcol = sbuf(stf, "fcol", [128, 12], F32)

        def cc_phase():
            with ExitStack() as pst:
                pp = psum(pst, "pp_cc", [128, 512], F32)

                def body():
                    VE.doi("tensor_reduce", fcol[:, 0:1], amax_cols[:, :],
                           AX.X, OP.max, label="c_lm")
                    TE.w("c_lm")
                    TE.w("p_idn32")
                    TE.doi("matmul", pp[0:1, 32:160], fcol[:, 0:1], idn32[:, :],
                           start=True, stop=True, label="c_row")
                    VE.w("c_row")
                    VE.doi("tensor_reduce", cc_sc[0:1, 0:1], pp[0:1, 32:160],
                           AX.X, OP.max, label="c_m11")
                    GE.w("c_m11")
                    GED.dma(ccin_d[:, :], cc_sc[0:1, 0:8], label="c_cin")
                    GE.w("c_cin")
                    GE.doi("collective_compute", "AllReduce", OP.max,
                           replica_groups=[list(range(NCORES))],
                           ins=[ccin_d.ap().opt()],
                           outs=[ccout_d.ap().opt()],
                           label="c_cc")
                    GE.w("c_cc")
                    GED.dma(cc_sc[0:1, 0:8], ccout_d[:, :], label="c_cout")
                    VE.w("c_cout")
                    VE.doi("tensor_scalar", cc_sc[0:1, 1:2], cc_sc[0:1, 0:1],
                           1e-8, 1.0 / QMAX, OP.max, OP.mult, label="c_sc1")
                    VE.w("c_sc1")
                    VE.doi("reciprocal", cc_sc[0:1, 2:3], cc_sc[0:1, 1:2],
                           label="c_inv")
                    VE.w("c_inv")

                    SY.dma(wfc[0][:, :], wffnT_d[0:128, :], label="c_w0")
                    SY.dma(wfc[1][:, :], wffnT_d[128:256, :], label="c_w1")
                    for c in range(8):
                        VE.w(f"c_w{c}")
                        VE.doi("tensor_reduce", wcols[:, c:c + 1],
                               wfc[c % 2][:, :], AX.X, OP.add,
                               apply_absolute_value=True, label=f"c_aw{c}")
                        if c + 2 < 8:
                            SY.w(f"c_aw{c}")
                            SY.dma(wfc[c % 2][:, :],
                                   wffnT_d[(c + 2) * 128:(c + 3) * 128, :],
                                   label=f"c_w{c+2}")
                    VE.do("drain")
                    VE.doi("tensor_reduce", wcols[:, 8:9], wcols[:, 0:8],
                           AX.X, OP.add, label="c_awsum")
                    TE.w("c_awsum")
                    TE.doi("matmul", pp[0:1, 0:1], ones_col_f[:, :],
                           wcols[:, 8:9], start=True, stop=True, label="c_tot")
                    VE.w("c_tot")
                    VE.doi("tensor_scalar", cc_sc[0:1, 3:4], pp[0:1, 0:1],
                           THRESH / float(D * D), None, OP.mult, label="c_del")
                    TE.w("c_del")
                    TE.doi("matmul", pp[:, 4:5], ones_row_f[:, :],
                           cc_sc[0:1, 3:4], start=True, stop=True, label="c_delb")
                    VE.w("c_delb")
                    VE.doi("tensor_copy", fcol[:, 1:2], pp[:, 4:5],
                           label="c_delc")
                    SY.w("c_aw6")
                    SY.dma(wfc[0][:, :], wffnT_d[0:128, :], label="c_v0")
                    SY.w("c_aw7")
                    SY.dma(wfc[1][:, :], wffnT_d[128:256, :], label="c_v1")
                    for c in range(8):
                        SE.w(f"c_v{c}")
                        SE.w("c_delc")
                        if c >= 1:
                            SE.w(f"c_awm{c-1}")
                            SE.w(f"c_S{c-1}")
                        SE.doi("activation", ab_buf[:, :], wfc[c % 2][:, :],
                               AF.Abs, label=f"c_ab{c}")
                        SE.doi("activation", sgn_buf[:, :], wfc[c % 2][:, :],
                               AF.Sign, label=f"c_sg{c}")
                        if c + 2 < 8:
                            SY.w(f"c_sg{c}")
                            SY.dma(wfc[c % 2][:, :],
                                   wffnT_d[(c + 2) * 128:(c + 3) * 128, :],
                                   label=f"c_v{c+2}")
                        VE.w(f"c_ab{c}")
                        VE.doi("tensor_scalar", m_buf[:, :], ab_buf[:, :],
                               fcol[:, 1:2], None, OP.is_gt, label=f"c_mk{c}")
                        VE.do("tensor_tensor", trf[:, :], ab_buf[:, :],
                              m_buf[:, :], OP.mult)
                        VE.doi("tensor_reduce", wcols[:, 9 + c:10 + c],
                               trf[:, :], AX.X, OP.add, label=f"c_awm{c}")
                        VE.doi("tensor_reduce", wcols[:, 17 + c:18 + c],
                               m_buf[:, :], AX.X, OP.add, label=f"c_sm{c}")
                        VE.w(f"c_sg{c}")
                        VE.doi("tensor_tensor", sg[:, c * D:(c + 1) * D],
                               sgn_buf[:, :], m_buf[:, :], OP.mult,
                               label=f"c_S{c}")
                    VE.do("drain")
                    VE.doi("tensor_reduce", wcols[:, 25:26], wcols[:, 9:17],
                           AX.X, OP.add, label="c_awm_sum")
                    VE.doi("tensor_reduce", wcols[:, 26:27], wcols[:, 17:25],
                           AX.X, OP.add, label="c_sm_sum")
                    TE.w("c_awm_sum")
                    TE.w("c_delc")
                    TE.doi("matmul", pp[0:1, 8:9], ones_col_f[:, :],
                           wcols[:, 25:26], start=True, stop=True, label="c_t1")
                    TE.w("c_sm_sum")
                    TE.doi("matmul", pp[0:1, 12:13], ones_col_f[:, :],
                           wcols[:, 26:27], start=True, stop=True, label="c_t2")
                    VE.w("c_t1")
                    VE.w("c_t2")
                    VE.doi("tensor_scalar", cc_sc[0:1, 4:5], pp[0:1, 12:13],
                           1.0, None, OP.max, label="c_sc2")
                    VE.w("c_sc2")
                    VE.doi("reciprocal", cc_sc[0:1, 5:6], cc_sc[0:1, 4:5],
                           label="c_rc2")
                    VE.w("c_rc2")
                    VE.do("tensor_tensor", alpha11[0:1, 0:1], pp[0:1, 8:9],
                          cc_sc[0:1, 5:6], OP.mult)
                    VE.do("drain")
                    VE.doi("tensor_tensor", cc_sc[0:1, 6:7], cc_sc[0:1, 1:2],
                           alpha11[0:1, 0:1], OP.mult, label="c_fs")
                    TE.w("c_inv")
                    TE.w("c_fs")
                    TE.doi("matmul", pp[:, 16:17], ones_row_f[:, :],
                           cc_sc[0:1, 2:3], start=True, stop=True, label="c_bi")
                    TE.doi("matmul", pp[:, 20:21], ones_row_f[:, :],
                           cc_sc[0:1, 6:7], start=True, stop=True, label="c_bf")
                    VE.w("c_bi")
                    VE.w("c_bf")
                    VE.doi("tensor_copy", inv_col[:, :], pp[:, 16:17],
                           label="c_invc")
                    VE.doi("tensor_copy", fs_col[:, :], pp[:, 20:21],
                           label="c_fsc")
                    if DEBUG_DUMP == "ffn":
                        GE.w("c_fsc")
                        GED.dma(dbg_d[0:128, 0:4096], sg[:, 0:4096],
                                label="sgd1", sem="dso")
                        GED.dma(dbg_d[128:256, 0:4096], sg[:, 4096:8192],
                                label="sgd2", sem="dso")
                    if DEBUG_DUMP == "cc":
                        GE.w("c_fsc")
                        GED.dma(dbg_d[0:1, 0:8], cc_sc[0:1, :], label="ccd1",
                                sem="dso")
                        GED.dma(dbg_d[1:2, 0:1], alpha11[0:1, :], label="ccd2",
                                sem="dso")
                        GED.dma(dbg_d[2:130, 0:28], wcols[:, :], label="ccd3",
                                sem="dso")

                emit_block(body)

        if STOP_AFTER is None or STOP_AFTER == "cc":
            cc_phase()

        def ffn_phase():
            NTT = BPC * NT
            with ExitStack() as pst:
                pT3 = psum(pst, "pT3", [128, 1024], BF16)
                pF = [psum(pst, f"pF{i}", [128, 1024], F32) for i in range(2)]

                def body():
                    def loads(t):
                        if t >= 2:
                            SY.w(f"f_t2_{t-2}")
                        SY.w(f"b_xns_{BPC-1}_{NT-1}")
                        SY.dma(xnt[t % 2][:, :],
                               xns_d[t * 128:(t + 1) * 128, :],
                               label=f"f_xn_{t}", sem=f"dh{t % 2}")
                        if t >= 2:
                            SCD.w(f"f_s_{t-2}")
                        SCD.w(f"b_x1s_{BPC-1}_{NT-1}")
                        SCD.dma(x1t[t % 2][:, :],
                                x1s_d[t * 128:(t + 1) * 128, :],
                                label=f"f_x1_{t}", sem=f"ds{t % 2}")

                    def quant(t):
                        VE.w(f"f_xn_{t}")
                        VE.w("c_invc")
                        if t >= 2:
                            VE.w(f"f_tr_{t-2}")
                            if DEBUG_DUMP == "ffn" and (t - 2) in (4, 5):
                                VE.w(f"qbd_{t-2}")
                            VE.w(f"f_i_{t-2}")
                        VE.do("tensor_scalar", Fq[:, :], xnt[t % 2][:, :],
                              inv_col[:, 0:1], -QMAX, OP.mult, OP.max)
                        VE.doi("tensor_scalar", Tq[t % 2][:, :], Fq[:, :],
                               QMAX, 128.0, OP.min, OP.add, label=f"f_t2_{t}")
                        SE.w(f"f_t2_{t}")
                        if t >= 2:
                            SE.w(f"f_qb_{t-2}")
                        SE.doi("copy", Iq[t % 2][:, :], Tq[t % 2][:, :],
                               label=f"f_i_{t}")
                        VE.w(f"f_i_{t}")
                        VE.do("scalar_tensor_tensor", Fq[:, :],
                              Iq[t % 2][:, :], 0.5, Tq[t % 2][:, :],
                              OP.add, OP.is_le)
                        VE.doi("scalar_tensor_tensor", qb[t % 2][:, :],
                               Iq[t % 2][:, :], -128.0, Fq[:, :],
                               OP.add, OP.add, label=f"f_qb_{t}")
                        if DEBUG_DUMP == "ffn" and t in (4, 5):
                            GE.w(f"f_qb_{t}")
                            GED.dma(dbg_d[256 + (t - 4) * 128:
                                          256 + (t - 3) * 128, 0:1024],
                                    qb[t % 2][:, :], label=f"qbd_{t}",
                                    sem="dso")

                    def trq(t):
                        TE.w(f"f_qb_{t}")
                        TE.w("p_idnb")
                        if t >= 1:
                            TE.w(f"f_qev_{t-1}_0")
                            TE.w(f"f_qev_{t-1}_1")
                        for c in range(8):
                            TE.doi("transpose", pT3[:, c * 128:(c + 1) * 128],
                                   qb[t % 2][:, c * 128:(c + 1) * 128],
                                   idnbf[:, :],
                                   label=(f"f_tr_{t}" if c == 7 else None))
                        for r, px in ((0, VE), (1, SE)):
                            px.w(f"f_tr_{t}")
                            if r == 1:
                                px.w(f"f_qev_{t}_0")
                            px.doi("tensor_copy" if r == 0 else "copy",
                                   qbT[t % 2][:, r * 512:(r + 1) * 512],
                                   pT3[:, r * 512:(r + 1) * 512],
                                   label=f"f_qev_{t}_{r}")

                    def ffnmm(t):
                        TE.w(f"f_qev_{t}_0")
                        TE.w(f"f_qev_{t}_1")
                        if t >= 2:
                            TE.w(f"f_s_{t-2}")
                        pp = pF[t % 2]
                        for d_ in range(8):
                            lw = qbT[t % 2][:, d_ * 128:(d_ + 1) * 128]
                            for g_ in range(2):
                                TE.doi("matmul",
                                       pp[:, g_ * 512:(g_ + 1) * 512], lw,
                                       sg[:, d_ * D + g_ * 512:
                                          d_ * D + (g_ + 1) * 512],
                                       start=(d_ == 0), stop=(d_ == 7),
                                       label=(f"f_mm_{t}"
                                              if d_ == 7 and g_ == 1 else None))

                    def final(t):
                        pp = pF[t % 2]
                        VE.w(f"f_mm_{t}")
                        VE.w(f"f_x1_{t}")
                        VE.w("c_fsc")
                        if t >= 2:
                            VE.w(f"f_o_{t-2}")
                        VE.doi("scalar_tensor_tensor", Sf[t % 2][:, :], pp[:, :],
                               fs_col[:, 0:1], x1t[t % 2][:, :], OP.mult, OP.add,
                               label=f"f_s_{t}")
                        SE.w(f"f_s_{t}")
                        SE.do("activation", trf[:, :], Sf[t % 2][:, :],
                              AF.Identity, accum_out=fcol[:, 2:3])
                        SE.doi("activation", trf[:, :], Sf[t % 2][:, :],
                               AF.Square, accum_out=fcol[:, 3:4],
                               label=f"f_sq_{t}")
                        VE.w(f"f_sq_{t}")
                        VE.doi("tensor_scalar", fcol[:, 6:7], fcol[:, 3:4],
                               1.0 / D, EPS, OP.mult, OP.add,
                               label=f"f_v0_{t}")
                        VE.doi("tensor_scalar", fcol[:, 4:5], fcol[:, 2:3],
                               1.0 / D, None, OP.mult, label=f"f_u_{t}")
                        SE.w(f"f_u_{t}")
                        SE.doi("activation", fcol[:, 5:6], fcol[:, 4:5],
                               AF.Square, label=f"f_q2_{t}")
                        VE.w(f"f_q2_{t}")
                        VE.w(f"f_v0_{t}")
                        VE.doi("tensor_tensor", fcol[:, 6:7], fcol[:, 6:7],
                               fcol[:, 5:6], OP.subtract, label=f"f_var_{t}")
                        SE.w(f"f_var_{t}")
                        SE.doi("sqrt", fcol[:, 7:8], fcol[:, 6:7],
                               label=f"f_sd_{t}")
                        VE.w(f"f_sd_{t}")
                        VE.doi("reciprocal", fcol[:, 7:8], fcol[:, 7:8],
                               label=f"f_rc_{t}")
                        VE.w(f"f_rc_{t}")
                        VE.doi("scalar_tensor_tensor", fcol[:, 8:9],
                               fcol[:, 4:5], -1.0, fcol[:, 7:8],
                               OP.mult, OP.mult, label=f"f_mb_{t}")
                        SE.w(f"f_mb_{t}")
                        if t >= 2:
                            SE.w(f"f_out_{t-2}")
                        SE.doi("activation", outb[t % 2][:, :], Sf[t % 2][:, :],
                               AF.Identity, scale=fcol[:, 7:8], bias=fcol[:, 8:9],
                               label=f"f_o_{t}")
                        b_, tt = divmod(t, NT)
                        SY.w(f"f_o_{t}")
                        SY.dma(out_d[b_, tt * 128:(tt + 1) * 128, :],
                               outb[t % 2][:, :], label=f"f_out_{t}",
                               sem="dso")

                    loads(0)
                    loads(1)
                    quant(0)
                    for t in range(NTT):
                        trq(t)
                        if t + 1 < NTT:
                            quant(t + 1)
                        ffnmm(t)
                        final(t)
                        if t + 2 < NTT:
                            loads(t + 2)
                    SY.w(f"f_out_{NTT-1}")
                    SY.w(f"f_out_{NTT-2}")

                emit_block(body)

        if STOP_AFTER is None:
            ffn_phase()
        stf.close()

    return nc


def kernel(L0_attn_H, L0_attn_W_qkv, L0_attn_b_qkv, L0_attn_W_proj,
           L0_attn_b_proj, L0_attn_gamma, L0_attn_beta, L0_ffn_weight,
           L0_ffn_gamma, L0_ffn_beta):
    B, S, D_ = L0_attn_H.shape
    nc = build_nc(S=S)
    shared = {
        "wqkvT": np.ascontiguousarray(np.asarray(L0_attn_W_qkv, np.float32).T),
        "bqkv": np.asarray(L0_attn_b_qkv, np.float32).reshape(1, -1),
        "wprojT": np.ascontiguousarray(np.asarray(L0_attn_W_proj, np.float32).T),
        "bproj": np.asarray(L0_attn_b_proj, np.float32).reshape(1, -1),
        "g1": np.asarray(L0_attn_gamma, np.float32).reshape(1, -1),
        "be1": np.asarray(L0_attn_beta, np.float32).reshape(1, -1),
        "wffnT": np.ascontiguousarray(np.asarray(L0_ffn_weight, np.float32).T),
        "g2": np.asarray(L0_ffn_gamma, np.float32).reshape(1, -1),
        "be2": np.asarray(L0_ffn_beta, np.float32).reshape(1, -1),
        "idn": np.eye(128, dtype=np.float32),
    }
    H = np.ascontiguousarray(np.asarray(L0_attn_H, np.float32))
    in_maps = []
    for c in range(NCORES):
        m = dict(shared)
        m["h"] = H[c * BPC:(c + 1) * BPC]
        in_maps.append(m)
    res = run_bass_kernel_spmd(nc, in_maps, core_ids=list(range(NCORES)))
    out = np.concatenate([res.results[c]["out"] for c in range(NCORES)], axis=0)
    return np.ascontiguousarray(out.astype(np.float32))


# version 07 (final)
# speedup vs baseline: 1.1023x; 1.0147x over previous
import numpy as np
from contextlib import ExitStack

import concourse.bass as bass
import concourse.mybir as mybir
from concourse.bass_utils import run_bass_kernel_spmd

F32 = mybir.dt.float32
BF16 = mybir.dt.bfloat16
I32 = mybir.dt.int32
OP = mybir.AluOpType
AF = mybir.ActivationFunctionType
AX = mybir.AxisListType

NCORES = 8
D = 1024
HEADS = 16
BPC = 2
EPS = 1e-5
QMAX = 127.0
THRESH = 0.7

KVSLOT = [0, 65, 130, 195, 260, 325, 390, 512]
AOBANK = [(0, 7), (7, 14), (14, 16)]


class Ctx:
    def __init__(self, nc, semnames):
        self.nc = nc
        self.semnames = list(semnames)
        self.sems = {}
        self.c = {k: 0 for k in semnames}
        self.marks = {}
        self.mode = None
        self.eng = None
        self._waited = {}

    def snap(self):
        return (dict(self.c), dict(self.marks))

    def restore(self, s):
        self.c = dict(s[0])
        self.marks = dict(s[1])
        self._waited = {}

    def set_eng(self, mode, eng):
        self.mode = mode
        self.eng = eng
        self._waited = {}


class Px:
    def __init__(self, ctx, name, sem):
        self.ctx = ctx
        self.name = name
        self.sem = sem

    @property
    def live(self):
        return self.ctx.mode == self.name

    def w(self, label, extra=0):
        if LENIENT and label not in self.ctx.marks:
            return
        sem, val = self.ctx.marks[label]
        val += extra
        if val <= 0:
            return
        key = (self.name, sem)
        if self.ctx._waited.get(key, -1) >= val:
            return
        self.ctx._waited[key] = val
        if self.live:
            self.ctx.eng.wait_ge(self.ctx.sems[sem], val)

    def do(self, opname, *a, **k):
        if self.live:
            return getattr(self.ctx.eng, opname)(*a, **k)
        return None

    def doi(self, opname, *a, label=None, inc=1, **k):
        inst = self.do(opname, *a, **k)
        self.ctx.c[self.sem] += inc
        if label is not None:
            self.ctx.marks[label] = (self.sem, self.ctx.c[self.sem])
        if self.live:
            inst.then_inc(self.ctx.sems[self.sem], inc)
        return inst

    def dma(self, out, in_, label=None, sem=None):
        if sem is None:
            return self.doi("dma_start", out, in_, label=label, inc=16)
        old = self.sem
        self.sem = sem
        try:
            return self.doi("dma_start", out, in_, label=label, inc=16)
        finally:
            self.sem = old


STOP_AFTER = None
DEBUG_XOUT = False
DEBUG_DUMP = None
B_LIMIT = 99
Q_LIMIT = 99
LENIENT = False


def build_nc(S=2048):
    NT = S // 128
    SG = 512 if S >= 512 else S
    NG = S // SG
    SUBS = SG // 128

    nc = bass.Bass("TRN2", target_bir_lowering=False,
                   detect_race_conditions=False)

    h_d = nc.dram_tensor("h", [BPC, S, D], F32, kind="ExternalInput")
    wqkvT_d = nc.dram_tensor("wqkvT", [D, 3 * D], F32, kind="ExternalInput")
    bqkv_d = nc.dram_tensor("bqkv", [1, 3 * D], F32, kind="ExternalInput")
    wprojT_d = nc.dram_tensor("wprojT", [D, D], F32, kind="ExternalInput")
    bproj_d = nc.dram_tensor("bproj", [1, D], F32, kind="ExternalInput")
    g1_d = nc.dram_tensor("g1", [1, D], F32, kind="ExternalInput")
    be1_d = nc.dram_tensor("be1", [1, D], F32, kind="ExternalInput")
    wffnT_d = nc.dram_tensor("wffnT", [D, D], F32, kind="ExternalInput")
    g2_d = nc.dram_tensor("g2", [1, D], F32, kind="ExternalInput")
    be2_d = nc.dram_tensor("be2", [1, D], F32, kind="ExternalInput")
    idn_d = nc.dram_tensor("idn", [128, 128], F32, kind="ExternalInput")
    out_d = nc.dram_tensor("out", [BPC, S, D], F32, kind="ExternalOutput")
    xkind = "ExternalOutput" if DEBUG_XOUT else "Internal"
    mixs_d = nc.dram_tensor("mixs", [BPC * S, D], BF16)
    x1s_d = nc.dram_tensor("x1s", [BPC * S, D], BF16, kind=xkind)
    xns_d = nc.dram_tensor("xns", [BPC * S, D], BF16, kind=xkind)
    dbg_d = (nc.dram_tensor("dbg", [BPC * S, 4 * D], F32, kind="ExternalOutput")
             if DEBUG_DUMP else None)
    ccin_d = nc.dram_tensor("ccin", [1, 8], F32)
    ccout_d = nc.dram_tensor("ccout", [1, 8], F32)

    sems = ["dh", "dh0", "dh1", "dh2", "dh3", "dso", "ds0", "ds1",
            "dm0", "dm1", "dm2", "dm3", "dg", "pe", "ve", "se", "ge"]
    ctx = Ctx(nc, sems)
    SY = Px(ctx, "sync", "dh")
    TE = Px(ctx, "tensor", "pe")
    VE = Px(ctx, "vector", "ve")
    SE = Px(ctx, "scalar", "se")
    GE = Px(ctx, "gpsimd", "ge")
    SCD = Px(ctx, "scalar", "dsc")
    GED = Px(ctx, "gpsimd", "dg")

    def emit_block(body):
        snap = ctx.snap()
        with nc.Block() as blk:
            for engname, attr in [("sync", "sync"), ("tensor", "tensor"),
                                  ("vector", "vector"), ("scalar", "scalar"),
                                  ("gpsimd", "gpsimd")]:
                def run(eng, engname=engname):
                    ctx.restore(snap)
                    ctx.set_eng(engname, eng)
                    body()
                getattr(blk, attr)(run)

    st = ExitStack()
    with st:
        for name in sems:
            ctx.sems[name] = st.enter_context(nc.semaphore(name))

        def sbuf(stk, name, shape, dt=F32):
            return stk.enter_context(nc.sbuf_tensor(name, shape, dt))

        def psum(stk, name, shape, dt=F32):
            return stk.enter_context(nc.psum_tensor(name, shape, dt))

        wq = sbuf(st, "wq", [128, 8 * 3 * D], BF16)
        wp = sbuf(st, "wp", [128, 8 * D], BF16)
        bq_row = sbuf(st, "bq_row", [1, 3 * D], BF16)
        bp_row = sbuf(st, "bp_row", [1, D], BF16)
        g1bc = sbuf(st, "g1bc", [128, D], BF16)
        b1bc = sbuf(st, "b1bc", [128, D], BF16)
        g2bc = sbuf(st, "g2bc", [128, D], BF16)
        b2bc = sbuf(st, "b2bc", [128, D], BF16)
        idn32 = sbuf(st, "idn32", [128, 128], F32)
        idnbf = sbuf(st, "idnbf", [128, 128], BF16)
        ones_row_bf = sbuf(st, "ones_row_bf", [1, 128], BF16)
        ones_row_f = sbuf(st, "ones_row_f", [1, 128], F32)
        ones_col_f = sbuf(st, "ones_col_f", [128, 1], F32)
        ones11 = sbuf(st, "ones11", [1, 1], F32)
        negcol = sbuf(st, "negcol", [128, 1], F32)
        epscol = sbuf(st, "epscol", [128, 1], F32)
        bqcol = sbuf(st, "bqcol", [128, 8], F32)
        bqncol = sbuf(st, "bqncol", [128, 8], F32)
        amax_cols = sbuf(st, "amax_cols", [128, BPC * NT], F32)
        inv_col = sbuf(st, "inv_col", [128, 1], F32)
        fs_col = sbuf(st, "fs_col", [128, 1], F32)
        cc_sc = sbuf(st, "cc_sc", [1, 8], F32)
        alpha11 = sbuf(st, "alpha11", [1, 1], F32)

        def prep_phase():
          with ExitStack() as pst:
            pp = psum(pst, "pp_prep", [128, 512], F32)
            pp2 = psum(pst, "pp_prep2", [128, 512], F32)
            bq_f32 = sbuf(pst, "bq_f32", [1, 3 * D], F32)
            vrow_f = sbuf(pst, "vrow_f", [1, 4 * D], F32)

            def prep():
                for d_ in range(8):
                    GED.dma(wq[:, d_ * 3 * D:(d_ + 1) * 3 * D],
                            wqkvT_d[d_ * 128:(d_ + 1) * 128, :],
                            label=f"p_wq{d_}")
                for d_ in range(8):
                    GED.dma(wp[:, d_ * D:(d_ + 1) * D],
                            wprojT_d[d_ * 128:(d_ + 1) * 128, :],
                            label=f"p_wp{d_}")
                GED.dma(bq_row[:, :], bqkv_d[:, :], label="p_bqr")
                GED.dma(bp_row[:, :], bproj_d[:, :], label="p_bpr")
                GED.dma(idnbf[:, :], idn_d[:, :], label="p_idnb")
                SY.dma(bq_f32[:, :], bqkv_d[:, :], label="p_bqf")
                SY.dma(idn32[:, :], idn_d[:, :], label="p_idn32")
                SY.dma(vrow_f[:, 0:D], g1_d[:, :], label="p_g1")
                SY.dma(vrow_f[:, D:2 * D], be1_d[:, :], label="p_b1")
                SY.dma(vrow_f[:, 2 * D:3 * D], g2_d[:, :], label="p_g2")
                SY.dma(vrow_f[:, 3 * D:4 * D], be2_d[:, :], label="p_b2")
                GE.doi("memset", ones_row_bf[:, :], 1.0)
                GE.doi("memset", ones_row_f[:, :], 1.0)
                GE.doi("memset", ones_col_f[:, :], 1.0)
                GE.doi("memset", ones11[:, :], 1.0)
                GE.doi("memset", negcol[:, :], -1.0)
                GE.doi("memset", epscol[:, :], EPS)
                GE.doi("memset", cc_sc[:, :], 0.0)
                GE.doi("memset", amax_cols[:, :], 0.0, label="p_ms")

                if True:
                    VE.w("p_ms")
                    for _ in range(4):
                        VE.doi("reciprocal", inv_col[:, 0:1], ones_col_f[:, :])
                    TE.w("p_b2")
                    TE.w("p_idnb")
                    TE.w("p_ms")
                    for e in range(8):
                        TE.doi("matmul", pp[:, e:e + 1],
                               bq_f32[:, e * 128:(e + 1) * 128],
                               ones11[:, :], start=True, stop=True,
                               label=f"p_bqc{e}")
                    VE.w("p_bqc7")
                    VE.doi("tensor_copy", bqcol[:, :], pp[:, 0:8], label="p_bqcc")
                    VE.doi("tensor_scalar", bqncol[:, :], bqcol[:, :], -1.0, None,
                           OP.mult, label="p_bqnc")
                    dsts = [g1bc, b1bc, g2bc, b2bc]
                    labs = ["p_g1", "p_b1", "p_g2", "p_b2"]
                    for i in range(4):
                        TE.w(labs[i])
                        TE.w("p_bqcc" if i == 0 else f"p_bcc{i-1}_0")
                        TE.doi("matmul", pp[:, 0:512], ones_row_f[:, :],
                               vrow_f[:, i * D: i * D + 512],
                               start=True, stop=True, label=f"p_bc{i}_0")
                        if i > 0:
                            TE.w(f"p_bcc{i-1}_1")
                        TE.doi("matmul", pp2[:, 0:512], ones_row_f[:, :],
                               vrow_f[:, i * D + 512: i * D + 1024],
                               start=True, stop=True, label=f"p_bc{i}_1")
                        VE.w(f"p_bc{i}_0")
                        VE.doi("tensor_copy", dsts[i][:, 0:512], pp[:, 0:512],
                               label=f"p_bcc{i}_0")
                        VE.w(f"p_bc{i}_1")
                        VE.doi("tensor_copy", dsts[i][:, 512:1024], pp2[:, 0:512],
                               label=f"p_bcc{i}_1")

            emit_block(prep)
        prep_phase()

        stab = ExitStack()
        ht = sbuf(stab, "ht", [128, 8 * S], BF16)
        h32 = [sbuf(stab, f"h32_{i}", [128, D], F32) for i in range(2)]
        phik = [sbuf(stab, f"phik{i}", [128, D], BF16) for i in range(2)]
        vbf = [sbuf(stab, f"vbf{i}", [128, HEADS * 65], BF16) for i in range(2)]
        kvst = sbuf(stab, "kvst", [128, 1024], BF16)
        kvlo = sbuf(stab, "kvlo", [64, 1024], BF16)
        phiq = sbuf(stab, "phiq", [128, 8 * SG], BF16)
        phiqlo = sbuf(stab, "phiqlo", [64, 8 * SG], BF16)
        qex = [sbuf(stab, f"qex{i}", [128, SG], F32) for i in range(2)]
        mixb = [sbuf(stab, f"mixb{i}", [128, D], BF16) for i in range(2)]
        mixb2 = [sbuf(stab, f"mixb2_{i}", [128, D], BF16) for i in range(4)]
        mixT = [sbuf(stab, f"mixT{i}", [128, D], BF16) for i in range(2)]
        S0 = [sbuf(stab, f"S0_{i}", [128, D], F32) for i in range(2)]
        Z0 = [sbuf(stab, f"Z0_{i}", [128, D], F32) for i in range(2)]
        TRH = sbuf(stab, "TRH", [128, D], F32)
        X1b = [sbuf(stab, f"X1b{i}", [128, D], BF16) for i in range(2)]
        XNb = [sbuf(stab, f"XNb{i}", [128, D], BF16) for i in range(2)]
        dstg = sbuf(stab, "dstg", [128, 1024], F32)
        k1blk = sbuf(stab, "k1blk", [128, 16], BF16)
        qk1c = sbuf(stab, "qk1c", [128, HEADS], F32)
        recq = sbuf(stab, "recq", [128, HEADS], F32)
        colsc = sbuf(stab, "colsc", [128, 12], F32)

        def phase_a(b):
            with ExitStack() as pst:
                pT = psum(pst, f"pT_{b}", [128, 1024], F32)
                pK = psum(pst, f"pK_{b}", [128, 1024], F32)
                pV = psum(pst, f"pV_{b}", [128, 1024], F32)
                pacc = psum(pst, f"pacc_{b}", [128, 1024], F32)

                def body():
                    VE.doi("memset", pacc[:, :], 0.0, label=f"a_pz_{b}")
                    for i in range(2):
                        GE.doi("memset",
                               bass.AP(vbf[i], 64,
                                       [[HEADS * 65, 128], [65, HEADS], [1, 1]]),
                               1.0, label=f"a_vb1_{b}_{i}")

                    def load_h(t):
                        if t >= 2:
                            SY.w(f"a_tr_{b}_{t-2}")
                        SY.dma(h32[t % 2][:, :],
                               h_d[b, t * 128:(t + 1) * 128, :],
                               label=f"a_h_{b}_{t}", sem=f"dh{t % 2}")

                    def trans(t):
                        TE.w(f"a_h_{b}_{t}")
                        if t == 0:
                            TE.w("p_b2")
                        if t >= 1:
                            TE.w(f"a_evac_{b}_{t-1}")
                        for c in range(8):
                            TE.doi("transpose", pT[:, c * 128:(c + 1) * 128],
                                   h32[t % 2][:, c * 128:(c + 1) * 128],
                                   idn32[:, :],
                                   label=(f"a_tr_{b}_{t}" if c == 7 else None))

                    def evac(t):
                        VE.w(f"a_tr_{b}_{t}")
                        dst = bass.AP(ht, t * 128, [[8 * S, 128], [S, 8], [1, 128]])
                        src = bass.AP(pT, 0, [[1024, 128], [128, 8], [1, 128]])
                        VE.doi("tensor_copy", dst, src, label=f"a_evac_{b}_{t}")

                    def kvmm(t):
                        if t == 0:
                            TE.w("p_idnb")
                            TE.w("p_ms")
                        TE.w(f"a_evac_{b}_{t}")
                        if t >= 1:
                            TE.w(f"a_t1_{b}_{t-1}")
                            TE.w(f"a_vcopy_{b}_{t-1}")
                        for d_ in range(8):
                            lw = ht[:, d_ * S + t * 128: d_ * S + (t + 1) * 128]
                            for g_ in range(2):
                                TE.do("matmul", pK[:, g_ * 512:(g_ + 1) * 512], lw,
                                      wq[:, d_ * 3 * D + D + g_ * 512:
                                         d_ * 3 * D + D + (g_ + 1) * 512],
                                      start=(d_ == 0), stop=False)
                            for g_ in range(2):
                                TE.do("matmul", pV[:, g_ * 512:(g_ + 1) * 512], lw,
                                      wq[:, d_ * 3 * D + 2 * D + g_ * 512:
                                         d_ * 3 * D + 2 * D + (g_ + 1) * 512],
                                      start=(d_ == 0), stop=False)
                        for g_ in range(2):
                            TE.do("matmul", pK[:, g_ * 512:(g_ + 1) * 512],
                                  ones_row_bf[:, :],
                                  bq_row[:, D + g_ * 512: D + (g_ + 1) * 512],
                                  start=False, stop=True)
                        for g_ in range(2):
                            TE.doi("matmul", pV[:, g_ * 512:(g_ + 1) * 512],
                                   ones_row_bf[:, :],
                                   bq_row[:, 2 * D + g_ * 512:
                                          2 * D + (g_ + 1) * 512],
                                   start=False, stop=True,
                                   label=(f"a_kv_{b}_{t}" if g_ == 1 else None))

                    def phi(t):
                        if t == 0:
                            SE.w("p_ms")
                        SE.w(f"a_kv_{b}_{t}")
                        if t >= 2:
                            SE.w(f"a_acc_{b}_{t-2}")
                            SE.w(f"a_phi_{b}_{t-2}")
                        SE.do("activation", phik[t % 2][:, :], pK[:, :], AF.Relu)
                        SE.do("activation", TRH[:, :], pK[:, :], AF.Relu,
                              scale=negcol[:, 0:1])
                        SE.doi("activation", Z0[t % 2][:, :], TRH[:, :], AF.Exp,
                               scale=negcol[:, 0:1], label=f"a_t1_{b}_{t}")
                        VE.w(f"a_t1_{b}_{t}")
                        if t >= 2:
                            VE.w(f"a_acc_{b}_{t-2}")
                        VE.doi("tensor_tensor", phik[t % 2][:, :],
                               phik[t % 2][:, :], Z0[t % 2][:, :], OP.add,
                               label=f"a_phi_{b}_{t}")
                        VE.w(f"a_kv_{b}_{t}")
                        if t >= 2:
                            VE.w(f"a_acc_{b}_{t-2}")
                        if t == 0:
                            VE.w(f"a_vb1_{b}_1")
                        vdst = bass.AP(vbf[t % 2], 0,
                                       [[HEADS * 65, 128], [65, HEADS], [1, 64]])
                        vsrc = bass.AP(pV, 0, [[1024, 128], [64, HEADS], [1, 64]])
                        VE.doi("tensor_copy", vdst, vsrc, label=f"a_vcopy_{b}_{t}")

                    def kvacc(t):
                        TE.w(f"a_phi_{b}_{t}")
                        TE.w(f"a_vcopy_{b}_{t}")
                        if t == 0:
                            TE.w(f"a_pz_{b}")
                        for hh in range(HEADS):
                            par = hh % 2
                            slot = KVSLOT[hh // 2]
                            tp = (0, 64) if par else (0, 0)
                            TE.doi("matmul",
                                   pacc[par * 64:par * 64 + 64, slot:slot + 65],
                                   phik[t % 2][:, hh * 64:(hh + 1) * 64],
                                   vbf[t % 2][:, hh * 65:(hh + 1) * 65],
                                   start=False, stop=False,
                                   skip_group_check=True,
                                   tile_position=tp,
                                   label=(f"a_acc_{b}_{t}"
                                          if hh == HEADS - 1 else None))

                    load_h(0)
                    trans(0)
                    evac(0)
                    for t in range(NT):
                        if t + 1 < NT:
                            load_h(t + 1)
                        kvmm(t)
                        if t + 1 < NT:
                            trans(t + 1)
                            evac(t + 1)
                        phi(t)
                        if t >= 1:
                            kvacc(t - 1)
                    kvacc(NT - 1)
                    VE.w(f"a_acc_{b}_{NT-1}")
                    VE.doi("tensor_copy", kvst[:, :], pacc[:, :],
                           label=f"a_kvev_{b}")
                    VE.do("drain")
                    VE.do("memset", k1blk[:, :], 0.0)
                    VE.do("drain")
                    for e_ in range(8):
                        VE.do("tensor_copy",
                              k1blk[0:64, 2 * e_:2 * e_ + 1],
                              kvst[0:64, KVSLOT[e_] + 64:KVSLOT[e_] + 65])
                        VE.doi("tensor_copy",
                               k1blk[64:128, 2 * e_ + 1:2 * e_ + 2],
                               kvst[64:128, KVSLOT[e_] + 64:KVSLOT[e_] + 65],
                               label=(f"a_k1b_{b}" if e_ == 7 else None))
                    GE.w(f"a_kvev_{b}")
                    GED.dma(kvlo[:, :], kvst[64:128, :], label=f"a_kvlo_{b}")
                    if DEBUG_DUMP in ("kv", "all"):
                        GED.dma(dbg_d[512 + b * 128:512 + (b + 1) * 128, 0:1024],
                                kvst[:, :], label=f"dbgkv_{b}")

                emit_block(body)

        def phase_b(b):
            with ExitStack() as pst:
                pQ = [psum(pst, f"pQ{i}_{b}", [128, 512], F32) for i in range(2)]
                pO = [psum(pst, f"pO{i}_{b}", [128, 512], F32) for i in range(2)]
                pQk = psum(pst, f"pQk_{b}", [128, 16], F32)
                pT2 = psum(pst, f"pT2_{b}", [128, 512], BF16)
                pPr = psum(pst, f"pPr_{b}", [128, 1024], F32)

                def body():
                    def qproj(g):
                        TE.w("p_idnb")
                        SE.w("p_bqnc")
                        SE.w("p_ms")
                        for e in range(8):
                            if g > 0 or e >= 2:
                                pg, pb = (g, e - 2) if e >= 2 else (g - 1, e + 6)
                                TE.w(f"b_qphi_{b}_{pg}_{pb}")
                                TE.w(f"b_qt1_{b}_{pg}_{pb}")
                            for d_ in range(8):
                                TE.doi("matmul", pQ[e % 2][:, 0:SG],
                                       wq[:, d_ * 3 * D + e * 128:
                                          d_ * 3 * D + (e + 1) * 128],
                                       ht[:, d_ * S + g * SG:
                                          d_ * S + g * SG + SG],
                                       start=(d_ == 0), stop=(d_ == 7),
                                       label=(f"b_q_{b}_{g}_{e}"
                                              if d_ == 7 else None))
                            if Q_LIMIT < 2:
                                continue
                            VE.w(f"b_q_{b}_{g}_{e}")
                            if g > 0:
                                VE.w(f"b_ao_{b}_{g * SUBS - 1}")
                            VE.doi("tensor_scalar",
                                   phiq[:, e * SG:(e + 1) * SG],
                                   pQ[e % 2][:, 0:SG],
                                   bqcol[:, e:e + 1], 0.0, OP.add, OP.max,
                                   label=f"b_qu_{b}_{g}_{e}")
                            if Q_LIMIT < 3:
                                continue
                            SE.w(f"b_q_{b}_{g}_{e}")
                            SE.w(f"b_qu_{b}_{g}_{e}")
                            if g > 0 or e >= 2:
                                pg, pb = (g, e - 2) if e >= 2 else (g - 1, e + 6)
                                SE.w(f"b_qphi_{b}_{pg}_{pb}")
                            SE.do("activation", TRH[:, 0:SG], pQ[e % 2][:, 0:SG],
                                  AF.Relu, scale=negcol[:, 0:1],
                                  bias=bqncol[:, e:e + 1])
                            SE.doi("activation", qex[e % 2][:, 0:SG],
                                   TRH[:, 0:SG], AF.Exp, scale=negcol[:, 0:1],
                                   label=f"b_qt1_{b}_{g}_{e}")
                            if Q_LIMIT < 4:
                                continue
                            VE.w(f"b_qt1_{b}_{g}_{e}")
                            VE.doi("tensor_tensor",
                                   phiq[:, e * SG:(e + 1) * SG],
                                   phiq[:, e * SG:(e + 1) * SG],
                                   qex[e % 2][:, 0:SG], OP.add,
                                   label=f"b_qphi_{b}_{g}_{e}")
                        if Q_LIMIT >= 5:
                            GE.w(f"b_qphi_{b}_{g}_7")
                            if g > 0:
                                GE.w(f"b_ao_{b}_{g * SUBS - 1}")
                            GED.dma(phiqlo[:, :], phiq[64:128, :],
                                    label=f"b_qlo_{b}_{g}")
                            if DEBUG_DUMP in ("phiq", "all") and g == 0:
                                GED.dma(dbg_d[b * 256:b * 256 + 128, 0:8 * SG],
                                        phiq[:, :], label=f"dbgq_{b}")
                                GED.dma(dbg_d[b * 256 + 128:b * 256 + 192,
                                              0:8 * SG],
                                        phiqlo[:, :], label=f"dbgql_{b}")

                    def attnout(t):
                        g, s_ = divmod(t, SUBS)
                        TE.w(f"b_qlo_{b}_{g}")
                        TE.w(f"a_kvlo_{b}")
                        TE.w(f"a_kvev_{b}")
                        TE.w(f"a_k1b_{b}")
                        if t >= 1:
                            TE.w(f"b_div_{b}_{t-1}")
                            if DEBUG_DUMP == "post":
                                TE.w(f"pd_{b}_{t-1}")
                        for hh in range(HEADS):
                            e = hh // 2
                            src = phiq if hh % 2 == 0 else phiqlo
                            lw = src[0:64, e * SG + s_ * 128:
                                     e * SG + (s_ + 1) * 128]
                            kvsrc = kvst if hh % 2 == 0 else kvlo
                            rhs = kvsrc[0:64, KVSLOT[e]:KVSLOT[e] + 64]
                            bank, off = divmod(hh, 8)
                            TE.doi("matmul", pO[bank][:, off * 64:off * 64 + 64],
                                   lw, rhs, start=True, stop=True)
                        for e in range(8):
                            TE.doi("matmul", pQk[:, 2 * e:2 * e + 2],
                                   phiq[:, e * SG + s_ * 128:
                                        e * SG + (s_ + 1) * 128],
                                   k1blk[:, 2 * e:2 * e + 2],
                                   start=True, stop=True)
                        TE.doi("drain", label=f"b_ao_{b}_{t}")

                    def division(t):
                        VE.w(f"b_ao_{b}_{t}")
                        if t >= 2:
                            VE.w(f"b_ms_{b}_{t-2}")
                        VE.doi("tensor_scalar", qk1c[:, :], pQk[:, :],
                               1e-6, None, OP.max, label=f"rq1_{b}_{t}")
                        VE.w(f"rq1_{b}_{t}")
                        VE.doi("reciprocal", recq[:, :], qk1c[:, :],
                               label=f"rq2_{b}_{t}")
                        VE.w(f"rq2_{b}_{t}")
                        for bi in range(2):
                            src = bass.AP(pO[bi], 0, [[512, 128], [64, 8], [1, 64]])
                            rb = bass.AP(recq, bi * 8, [[HEADS, 128], [1, 8], [0, 64]])
                            dst = bass.AP(mixb[t % 2], bi * 512,
                                          [[D, 128], [64, 8], [1, 64]])
                            VE.doi("tensor_tensor", dst, src, rb, OP.mult,
                                   label=(f"b_div_{b}_{t}" if bi == 1 else None))
                        if DEBUG_DUMP == "post":
                            tgd = b * NT + t
                            VE.doi("tensor_copy", dstg[:, 0:16], pQk[:, :])
                            VE.doi("tensor_copy", dstg[:, 16:32], qk1c[:, :])
                            VE.doi("tensor_copy", dstg[:, 32:48], recq[:, :],
                                   label=f"pd_{b}_{t}")
                            SY.w(f"pd_{b}_{t}")
                            SY.dma(dbg_d[tgd * 128:(tgd + 1) * 128, 1024:1072],
                                   dstg[:, 0:48], label=f"pdd_{b}_{t}",
                                   sem="dso")
                        SY.w(f"b_div_{b}_{t}")
                        for hh in range(HEADS):
                            dstr = b * S + hh * (S // 16) + 8 * t
                            SY.dma(mixs_d[dstr:dstr + 8, :],
                                   mixb[t % 2][:, hh * 64:(hh + 1) * 64],
                                   label=(f"b_ms_{b}_{t}"
                                          if hh == HEADS - 1 else None),
                                   sem=f"dm{t % 2}")

                    def mixload(t):
                        if t >= 4:
                            SY.w(f"b_tr_{b}_{t-4}_1")
                        SY.w(f"b_ms_{b}_{NT-1}")
                        SY.w(f"b_ms_{b}_{NT-2}")
                        SY.dma(mixb2[t % 4][:, :],
                               mixs_d[b * S + t * 128:b * S + (t + 1) * 128, :],
                               label=f"b_ml_{b}_{t}", sem=f"dm{t % 4}")
                        if t >= 2:
                            SY.w(f"b_z_{b}_{t-2}")
                        SY.dma(h32[t % 2][:, :],
                               h_d[b, t * 128:(t + 1) * 128, :],
                               label=f"b_h_{b}_{t}", sem=f"dh{t % 2}")

                    def mixtrans(t):
                        if DEBUG_DUMP == "mix2":
                            tgd = b * NT + t
                            GE.w(f"b_ml_{b}_{t}")
                            GED.dma(dbg_d[tgd * 128:(tgd + 1) * 128, 0:1024],
                                    mixb2[t % 4][:, :], label=f"dmpm_{b}_{t}")
                        TE.w("p_idnb")
                        for r in range(2):
                            TE.w(f"b_ml_{b}_{t}")
                            if r == 1:
                                TE.w(f"b_mev_{b}_{t}_0")
                            elif t >= 1:
                                TE.w(f"b_mev_{b}_{t-1}_1")
                            for c in range(4):
                                cc = r * 4 + c
                                TE.doi("transpose",
                                       pT2[:, c * 128:(c + 1) * 128],
                                       mixb2[t % 4][:, cc * 128:(cc + 1) * 128],
                                       idnbf[:, :],
                                       label=(f"b_tr_{b}_{t}_{r}"
                                              if c == 3 else None))
                            px = VE if r == 0 else SE
                            px.w(f"b_tr_{b}_{t}_{r}")
                            dst = bass.AP(mixT[t % 2], r * 4 * 128,
                                          [[D, 128], [128, 4], [1, 128]])
                            src = bass.AP(pT2, 0, [[512, 128], [128, 4], [1, 128]])
                            px.doi("tensor_copy" if r == 0 else "copy", dst, src,
                                   label=f"b_mev_{b}_{t}_{r}")

                    def proj(t):
                        TE.w("p_idnb")
                        TE.w(f"b_mev_{b}_{t}_0")
                        TE.w(f"b_mev_{b}_{t}_1")
                        if t >= 1:
                            TE.w(f"b_pcopy_{b}_{t-1}")
                        pp = pPr
                        for d_ in range(8):
                            lw = mixT[t % 2][:, d_ * 128:(d_ + 1) * 128]
                            for g_ in range(2):
                                TE.do("matmul", pp[:, g_ * 512:(g_ + 1) * 512],
                                      lw,
                                      wp[:, d_ * D + g_ * 512:
                                         d_ * D + (g_ + 1) * 512],
                                      start=(d_ == 0), stop=False)
                        for g_ in range(2):
                            TE.doi("matmul", pp[:, g_ * 512:(g_ + 1) * 512],
                                   ones_row_bf[:, :],
                                   bp_row[:, g_ * 512:(g_ + 1) * 512],
                                   start=False, stop=True,
                                   label=(f"b_pr_{b}_{t}" if g_ == 1 else None))

                    def stats(prefix, t, mucol_from):
                        VE.doi("tensor_scalar", colsc[:, 4:5], colsc[:, 1:2],
                               1.0 / D, EPS, OP.mult, OP.add,
                               label=f"{prefix}v0_{b}_{t}")
                        VE.doi("tensor_scalar", colsc[:, 2:3],
                               colsc[:, mucol_from:mucol_from + 1],
                               1.0 / D, None, OP.mult,
                               label=f"{prefix}u_{b}_{t}")
                        SE.w(f"{prefix}u_{b}_{t}")
                        SE.doi("activation", colsc[:, 3:4], colsc[:, 2:3],
                               AF.Square, label=f"{prefix}q2_{b}_{t}")
                        VE.w(f"{prefix}q2_{b}_{t}")
                        VE.w(f"{prefix}v0_{b}_{t}")
                        VE.doi("tensor_tensor", colsc[:, 4:5], colsc[:, 4:5],
                               colsc[:, 3:4], OP.subtract,
                               label=f"{prefix}v_{b}_{t}")
                        SE.w(f"{prefix}v_{b}_{t}")
                        SE.doi("sqrt", colsc[:, 5:6], colsc[:, 4:5],
                               label=f"{prefix}sd_{b}_{t}")
                        VE.w(f"{prefix}sd_{b}_{t}")
                        VE.doi("reciprocal", colsc[:, 5:6], colsc[:, 5:6],
                               label=f"{prefix}rc_{b}_{t}")
                        VE.w(f"{prefix}rc_{b}_{t}")
                        VE.doi("scalar_tensor_tensor", colsc[:, 6:7],
                               colsc[:, 2:3], -1.0, colsc[:, 5:6],
                               OP.mult, OP.mult,
                               label=f"{prefix}mb_{b}_{t}")

                    def ln_head(t):
                        SE.w(f"b_pr_{b}_{t}")
                        if t >= 2:
                            SE.w(f"b_b2_{b}_{t-2}")
                        SE.doi("activation", S0[t % 2][:, :], pPr[:, :], AF.Copy,
                               accum_out=colsc[:, 0:1], label=f"b_pcopy_{b}_{t}")

                    def lnchain(t):
                        tgd = b * NT + t
                        if DEBUG_DUMP == "mix":
                            GE.w(f"b_div_{b}_{t}")
                            GED.dma(dbg_d[tgd * 128:(tgd + 1) * 128, :],
                                    mixb[t % 2][:, :], label=f"dbg_{b}_{t}")
                        if DEBUG_DUMP == "proj2":
                            tgd2 = b * NT + t
                            SY.w(f"b_pcopy_{b}_{t}")
                            SY.dma(dbg_d[tgd2 * 128:(tgd2 + 1) * 128, 0:1024],
                                   S0[t % 2][:, :], label=f"dmpp_{b}_{t}")
                        SE.doi("activation", TRH[:, :], S0[t % 2][:, :],
                               AF.Square, accum_out=colsc[:, 1:2],
                               label=f"b_sq1_{b}_{t}")
                        VE.w(f"b_sq1_{b}_{t}")
                        stats("b_s1", t, 0)
                        SE.w(f"b_s1mb_{b}_{t}")
                        if t >= 2:
                            SE.w(f"b_g2_{b}_{t-2}")
                        SE.doi("activation", Z0[t % 2][:, :], S0[t % 2][:, :],
                               AF.Identity, scale=colsc[:, 5:6], bias=colsc[:, 6:7],
                               label=f"b_u1_{b}_{t}")
                        if DEBUG_DUMP == "u1":
                            SY.w(f"b_u1_{b}_{t}")
                            SY.dma(dbg_d[tgd * 128:(tgd + 1) * 128, :],
                                   Z0[t % 2][:, :], label=f"dbg_{b}_{t}")
                        VE.w(f"b_u1_{b}_{t}")
                        if DEBUG_DUMP == "proj2":
                            VE.w(f"dmpp_{b}_{t}")
                        VE.doi("tensor_tensor", S0[t % 2][:, :], Z0[t % 2][:, :],
                               g1bc[:, :], OP.mult, label=f"b_g1_{b}_{t}")
                        GE.w(f"b_g1_{b}_{t}")
                        GE.w("p_bcc3_1")
                        GE.doi("tensor_tensor", Z0[t % 2][:, :], S0[t % 2][:, :],
                               b1bc[:, :], OP.add, label=f"b_b1_{b}_{t}")
                        VE.w(f"b_b1_{b}_{t}")
                        VE.w(f"b_h_{b}_{t}")
                        VE.doi("scalar_tensor_tensor", S0[t % 2][:, :],
                               h32[t % 2][:, :], 2.0, Z0[t % 2][:, :],
                               OP.mult, OP.add, label=f"b_z_{b}_{t}")
                        if DEBUG_DUMP == "z":
                            SY.w(f"b_z_{b}_{t}")
                            SY.dma(dbg_d[tgd * 128:(tgd + 1) * 128, :],
                                   S0[t % 2][:, :], label=f"dbg_{b}_{t}")
                        SE.w(f"b_z_{b}_{t}")
                        SE.do("activation", TRH[:, :], S0[t % 2][:, :],
                              AF.Identity, accum_out=colsc[:, 0:1])
                        SE.doi("activation", TRH[:, :], S0[t % 2][:, :],
                               AF.Square, accum_out=colsc[:, 1:2],
                               label=f"b_sq2_{b}_{t}")
                        VE.w(f"b_sq2_{b}_{t}")
                        stats("b_s2", t, 0)
                        SE.w(f"b_s2mb_{b}_{t}")
                        if t >= 2:
                            SE.w(f"b_x1s_{b}_{t-2}")
                        elif b > 0:
                            SE.w(f"b_x1s_{b-1}_{NT + t - 2}")
                        SE.doi("activation", X1b[t % 2][:, :], S0[t % 2][:, :],
                               AF.Identity, scale=colsc[:, 5:6], bias=colsc[:, 6:7],
                               accum_out=colsc[:, 7:8], label=f"b_x1_{b}_{t}")
                        tg = b * NT + t
                        SY.w(f"b_x1_{b}_{t}")
                        SY.dma(x1s_d[tg * 128:(tg + 1) * 128, :],
                               X1b[t % 2][:, :], label=f"b_x1s_{b}_{t}",
                               sem="dso")
                        SE.doi("activation", TRH[:, :], X1b[t % 2][:, :],
                               AF.Square, accum_out=colsc[:, 1:2],
                               label=f"b_sq3_{b}_{t}")
                        VE.w(f"b_sq3_{b}_{t}")
                        stats("b_s3", t, 7)
                        SE.w(f"b_s3mb_{b}_{t}")
                        SE.doi("activation", Z0[t % 2][:, :], X1b[t % 2][:, :],
                               AF.Identity, scale=colsc[:, 5:6], bias=colsc[:, 6:7],
                               label=f"b_u3_{b}_{t}")
                        VE.w(f"b_u3_{b}_{t}")
                        VE.doi("tensor_tensor", S0[t % 2][:, :], Z0[t % 2][:, :],
                               g2bc[:, :], OP.mult, label=f"b_g2_{b}_{t}")
                        GE.w(f"b_g2_{b}_{t}")
                        if t >= 2:
                            GE.w(f"b_xns_{b}_{t-2}")
                            GE.w(f"b_am_{b}_{t-2}")
                        elif b > 0:
                            GE.w(f"b_xns_{b-1}_{NT + t - 2}")
                        GE.doi("tensor_tensor", XNb[t % 2][:, :], S0[t % 2][:, :],
                               b2bc[:, :], OP.add, label=f"b_b2_{b}_{t}")
                        SY.w(f"b_b2_{b}_{t}")
                        SY.dma(xns_d[tg * 128:(tg + 1) * 128, :],
                               XNb[t % 2][:, :], label=f"b_xns_{b}_{t}",
                               sem="dso")
                        VE.w(f"b_b2_{b}_{t}")
                        VE.doi("tensor_reduce", amax_cols[:, tg:tg + 1],
                               XNb[t % 2][:, :], AX.X, OP.max,
                               apply_absolute_value=True,
                               label=f"b_am_{b}_{t}")

                    for g in range(NG):
                        qproj(g)
                        for s_ in range(SUBS):
                            t = g * SUBS + s_
                            attnout(t)
                            division(t)
                    for t in range(NT):
                        mixload(t)
                        mixtrans(t)
                        if t >= 1:
                            ln_head(t - 1)
                        proj(t)
                        if t >= 1:
                            lnchain(t - 1)
                    ln_head(NT - 1)
                    lnchain(NT - 1)

                emit_block(body)

        if STOP_AFTER != "prep":
            for b in range(BPC):
                phase_a(b)
                if STOP_AFTER == f"a{b}":
                    break
                phase_b(b)
                if STOP_AFTER == f"b{b}":
                    break

        stab.close()

        stf = ExitStack()
        sg = sbuf(stf, "sg", [128, 8 * D], BF16)
        wfc = [sbuf(stf, f"wfc{i}", [128, D], F32) for i in range(2)]
        ab_buf = sbuf(stf, "ab_buf", [128, D], F32)
        m_buf = sbuf(stf, "m_buf", [128, D], F32)
        sgn_buf = sbuf(stf, "sgn_buf", [128, D], F32)
        trf = sbuf(stf, "trf", [128, D], F32)
        wcols = sbuf(stf, "wcols", [128, 28], F32)
        xnt = [sbuf(stf, f"xnt{i}", [128, D], BF16) for i in range(2)]
        x1t = [sbuf(stf, f"x1t{i}", [128, D], BF16) for i in range(2)]
        Tq = [sbuf(stf, f"Tq{i}", [128, D], F32) for i in range(2)]
        Iq = [sbuf(stf, f"Iq{i}", [128, D], I32) for i in range(2)]
        Fq = sbuf(stf, "Fq", [128, D], F32)
        qb = [sbuf(stf, f"qb{i}", [128, D], BF16) for i in range(2)]
        qbT = [sbuf(stf, f"qbT{i}", [128, D], BF16) for i in range(2)]
        Sf = [sbuf(stf, f"Sf{i}", [128, D], F32) for i in range(2)]
        outb = [sbuf(stf, f"outb{i}", [128, D], F32) for i in range(2)]
        fcol = sbuf(stf, "fcol", [128, 12], F32)

        def cc_phase():
            with ExitStack() as pst:
                pp = psum(pst, "pp_cc", [128, 512], F32)

                def body():
                    VE.doi("tensor_reduce", fcol[:, 0:1], amax_cols[:, :],
                           AX.X, OP.max, label="c_lm")
                    TE.w("c_lm")
                    TE.w("p_idn32")
                    TE.doi("matmul", pp[0:1, 32:160], fcol[:, 0:1], idn32[:, :],
                           start=True, stop=True, label="c_row")
                    VE.w("c_row")
                    VE.doi("tensor_reduce", cc_sc[0:1, 0:1], pp[0:1, 32:160],
                           AX.X, OP.max, label="c_m11")
                    GE.w("c_m11")
                    GED.dma(ccin_d[:, :], cc_sc[0:1, 0:8], label="c_cin")
                    GE.w("c_cin")
                    GE.doi("collective_compute", "AllReduce", OP.max,
                           replica_groups=[list(range(NCORES))],
                           ins=[ccin_d.ap().opt()],
                           outs=[ccout_d.ap().opt()],
                           label="c_cc")
                    GE.w("c_cc")
                    GED.dma(cc_sc[0:1, 0:8], ccout_d[:, :], label="c_cout")
                    VE.w("c_cout")
                    VE.doi("tensor_scalar", cc_sc[0:1, 1:2], cc_sc[0:1, 0:1],
                           1e-8, 1.0 / QMAX, OP.max, OP.mult, label="c_sc1")
                    VE.w("c_sc1")
                    VE.doi("reciprocal", cc_sc[0:1, 2:3], cc_sc[0:1, 1:2],
                           label="c_inv")
                    VE.w("c_inv")

                    SY.dma(wfc[0][:, :], wffnT_d[0:128, :], label="c_w0")
                    SY.dma(wfc[1][:, :], wffnT_d[128:256, :], label="c_w1")
                    for c in range(8):
                        VE.w(f"c_w{c}")
                        VE.doi("tensor_reduce", wcols[:, c:c + 1],
                               wfc[c % 2][:, :], AX.X, OP.add,
                               apply_absolute_value=True, label=f"c_aw{c}")
                        if c + 2 < 8:
                            SY.w(f"c_aw{c}")
                            SY.dma(wfc[c % 2][:, :],
                                   wffnT_d[(c + 2) * 128:(c + 3) * 128, :],
                                   label=f"c_w{c+2}")
                    VE.do("drain")
                    VE.doi("tensor_reduce", wcols[:, 8:9], wcols[:, 0:8],
                           AX.X, OP.add, label="c_awsum")
                    TE.w("c_awsum")
                    TE.doi("matmul", pp[0:1, 0:1], ones_col_f[:, :],
                           wcols[:, 8:9], start=True, stop=True, label="c_tot")
                    VE.w("c_tot")
                    VE.doi("tensor_scalar", cc_sc[0:1, 3:4], pp[0:1, 0:1],
                           THRESH / float(D * D), None, OP.mult, label="c_del")
                    TE.w("c_del")
                    TE.doi("matmul", pp[:, 4:5], ones_row_f[:, :],
                           cc_sc[0:1, 3:4], start=True, stop=True, label="c_delb")
                    VE.w("c_delb")
                    VE.doi("tensor_copy", fcol[:, 1:2], pp[:, 4:5],
                           label="c_delc")
                    SY.w("c_aw6")
                    SY.dma(wfc[0][:, :], wffnT_d[0:128, :], label="c_v0")
                    SY.w("c_aw7")
                    SY.dma(wfc[1][:, :], wffnT_d[128:256, :], label="c_v1")
                    for c in range(8):
                        SE.w(f"c_v{c}")
                        SE.w("c_delc")
                        if c >= 1:
                            SE.w(f"c_awm{c-1}")
                            SE.w(f"c_S{c-1}")
                        SE.doi("activation", ab_buf[:, :], wfc[c % 2][:, :],
                               AF.Abs, label=f"c_ab{c}")
                        SE.doi("activation", sgn_buf[:, :], wfc[c % 2][:, :],
                               AF.Sign, label=f"c_sg{c}")
                        if c + 2 < 8:
                            SY.w(f"c_sg{c}")
                            SY.dma(wfc[c % 2][:, :],
                                   wffnT_d[(c + 2) * 128:(c + 3) * 128, :],
                                   label=f"c_v{c+2}")
                        VE.w(f"c_ab{c}")
                        VE.doi("tensor_scalar", m_buf[:, :], ab_buf[:, :],
                               fcol[:, 1:2], None, OP.is_gt, label=f"c_mk{c}")
                        VE.do("tensor_tensor", trf[:, :], ab_buf[:, :],
                              m_buf[:, :], OP.mult)
                        VE.doi("tensor_reduce", wcols[:, 9 + c:10 + c],
                               trf[:, :], AX.X, OP.add, label=f"c_awm{c}")
                        VE.doi("tensor_reduce", wcols[:, 17 + c:18 + c],
                               m_buf[:, :], AX.X, OP.add, label=f"c_sm{c}")
                        VE.w(f"c_sg{c}")
                        VE.doi("tensor_tensor", sg[:, c * D:(c + 1) * D],
                               sgn_buf[:, :], m_buf[:, :], OP.mult,
                               label=f"c_S{c}")
                    VE.do("drain")
                    VE.doi("tensor_reduce", wcols[:, 25:26], wcols[:, 9:17],
                           AX.X, OP.add, label="c_awm_sum")
                    VE.doi("tensor_reduce", wcols[:, 26:27], wcols[:, 17:25],
                           AX.X, OP.add, label="c_sm_sum")
                    TE.w("c_awm_sum")
                    TE.w("c_delc")
                    TE.doi("matmul", pp[0:1, 8:9], ones_col_f[:, :],
                           wcols[:, 25:26], start=True, stop=True, label="c_t1")
                    TE.w("c_sm_sum")
                    TE.doi("matmul", pp[0:1, 12:13], ones_col_f[:, :],
                           wcols[:, 26:27], start=True, stop=True, label="c_t2")
                    VE.w("c_t1")
                    VE.w("c_t2")
                    VE.doi("tensor_scalar", cc_sc[0:1, 4:5], pp[0:1, 12:13],
                           1.0, None, OP.max, label="c_sc2")
                    VE.w("c_sc2")
                    VE.doi("reciprocal", cc_sc[0:1, 5:6], cc_sc[0:1, 4:5],
                           label="c_rc2")
                    VE.w("c_rc2")
                    VE.do("tensor_tensor", alpha11[0:1, 0:1], pp[0:1, 8:9],
                          cc_sc[0:1, 5:6], OP.mult)
                    VE.do("drain")
                    VE.doi("tensor_tensor", cc_sc[0:1, 6:7], cc_sc[0:1, 1:2],
                           alpha11[0:1, 0:1], OP.mult, label="c_fs")
                    TE.w("c_inv")
                    TE.w("c_fs")
                    TE.doi("matmul", pp[:, 16:17], ones_row_f[:, :],
                           cc_sc[0:1, 2:3], start=True, stop=True, label="c_bi")
                    TE.doi("matmul", pp[:, 20:21], ones_row_f[:, :],
                           cc_sc[0:1, 6:7], start=True, stop=True, label="c_bf")
                    VE.w("c_bi")
                    VE.w("c_bf")
                    VE.doi("tensor_copy", inv_col[:, :], pp[:, 16:17],
                           label="c_invc")
                    VE.doi("tensor_copy", fs_col[:, :], pp[:, 20:21],
                           label="c_fsc")
                    if DEBUG_DUMP == "ffn":
                        GE.w("c_fsc")
                        GED.dma(dbg_d[0:128, 0:4096], sg[:, 0:4096],
                                label="sgd1", sem="dso")
                        GED.dma(dbg_d[128:256, 0:4096], sg[:, 4096:8192],
                                label="sgd2", sem="dso")
                    if DEBUG_DUMP == "cc":
                        GE.w("c_fsc")
                        GED.dma(dbg_d[0:1, 0:8], cc_sc[0:1, :], label="ccd1",
                                sem="dso")
                        GED.dma(dbg_d[1:2, 0:1], alpha11[0:1, :], label="ccd2",
                                sem="dso")
                        GED.dma(dbg_d[2:130, 0:28], wcols[:, :], label="ccd3",
                                sem="dso")

                emit_block(body)

        if STOP_AFTER is None or STOP_AFTER == "cc":
            cc_phase()

        def ffn_phase():
            NTT = BPC * NT
            with ExitStack() as pst:
                pT3 = psum(pst, "pT3", [128, 1024], BF16)
                pF = [psum(pst, f"pF{i}", [128, 1024], F32) for i in range(2)]

                def body():
                    def loads(t):
                        if t >= 2:
                            SY.w(f"f_t2_{t-2}")
                        SY.w(f"b_xns_{BPC-1}_{NT-1}")
                        SY.dma(xnt[t % 2][:, :],
                               xns_d[t * 128:(t + 1) * 128, :],
                               label=f"f_xn_{t}", sem=f"dh{t % 2}")
                        if t >= 2:
                            SCD.w(f"f_s_{t-2}")
                        SCD.w(f"b_x1s_{BPC-1}_{NT-1}")
                        SCD.dma(x1t[t % 2][:, :],
                                x1s_d[t * 128:(t + 1) * 128, :],
                                label=f"f_x1_{t}", sem=f"ds{t % 2}")

                    def quant(t):
                        VE.w(f"f_xn_{t}")
                        VE.w("c_invc")
                        if t >= 2:
                            VE.w(f"f_tr_{t-2}")
                            if DEBUG_DUMP == "ffn" and (t - 2) in (4, 5):
                                VE.w(f"qbd_{t-2}")
                            VE.w(f"f_i_{t-2}")
                        VE.do("tensor_scalar", Fq[:, :], xnt[t % 2][:, :],
                              inv_col[:, 0:1], -QMAX, OP.mult, OP.max)
                        VE.doi("tensor_scalar", Tq[t % 2][:, :], Fq[:, :],
                               QMAX, 128.0, OP.min, OP.add, label=f"f_t2_{t}")
                        SE.w(f"f_t2_{t}")
                        if t >= 2:
                            SE.w(f"f_qb_{t-2}")
                        SE.doi("copy", Iq[t % 2][:, :], Tq[t % 2][:, :],
                               label=f"f_i_{t}")
                        VE.w(f"f_i_{t}")
                        VE.do("scalar_tensor_tensor", Fq[:, :],
                              Iq[t % 2][:, :], 0.5, Tq[t % 2][:, :],
                              OP.add, OP.is_le)
                        VE.doi("scalar_tensor_tensor", qb[t % 2][:, :],
                               Iq[t % 2][:, :], -128.0, Fq[:, :],
                               OP.add, OP.add, label=f"f_qb_{t}")
                        if DEBUG_DUMP == "ffn" and t in (4, 5):
                            GE.w(f"f_qb_{t}")
                            GED.dma(dbg_d[256 + (t - 4) * 128:
                                          256 + (t - 3) * 128, 0:1024],
                                    qb[t % 2][:, :], label=f"qbd_{t}",
                                    sem="dso")

                    def trq(t):
                        TE.w(f"f_qb_{t}")
                        TE.w("p_idnb")
                        if t >= 1:
                            TE.w(f"f_qev_{t-1}_0")
                            TE.w(f"f_qev_{t-1}_1")
                        for c in range(8):
                            TE.doi("transpose", pT3[:, c * 128:(c + 1) * 128],
                                   qb[t % 2][:, c * 128:(c + 1) * 128],
                                   idnbf[:, :],
                                   label=(f"f_tr_{t}" if c == 7 else None))
                        for r, px in ((0, VE), (1, SE)):
                            px.w(f"f_tr_{t}")
                            if r == 1:
                                px.w(f"f_qev_{t}_0")
                            px.doi("tensor_copy" if r == 0 else "copy",
                                   qbT[t % 2][:, r * 512:(r + 1) * 512],
                                   pT3[:, r * 512:(r + 1) * 512],
                                   label=f"f_qev_{t}_{r}")

                    def ffnmm(t):
                        TE.w(f"f_qev_{t}_0")
                        TE.w(f"f_qev_{t}_1")
                        if t >= 2:
                            TE.w(f"f_s_{t-2}")
                        pp = pF[t % 2]
                        for d_ in range(8):
                            lw = qbT[t % 2][:, d_ * 128:(d_ + 1) * 128]
                            for g_ in range(2):
                                TE.doi("matmul",
                                       pp[:, g_ * 512:(g_ + 1) * 512], lw,
                                       sg[:, d_ * D + g_ * 512:
                                          d_ * D + (g_ + 1) * 512],
                                       start=(d_ == 0), stop=(d_ == 7),
                                       label=(f"f_mm_{t}"
                                              if d_ == 7 and g_ == 1 else None))

                    def final(t):
                        pp = pF[t % 2]
                        VE.w(f"f_mm_{t}")
                        VE.w(f"f_x1_{t}")
                        VE.w("c_fsc")
                        if t >= 2:
                            VE.w(f"f_o_{t-2}")
                        VE.doi("scalar_tensor_tensor", Sf[t % 2][:, :], pp[:, :],
                               fs_col[:, 0:1], x1t[t % 2][:, :], OP.mult, OP.add,
                               label=f"f_s_{t}")
                        SE.w(f"f_s_{t}")
                        SE.do("activation", trf[:, :], Sf[t % 2][:, :],
                              AF.Identity, accum_out=fcol[:, 2:3])
                        SE.doi("activation", trf[:, :], Sf[t % 2][:, :],
                               AF.Square, accum_out=fcol[:, 3:4],
                               label=f"f_sq_{t}")
                        VE.w(f"f_sq_{t}")
                        VE.doi("tensor_scalar", fcol[:, 6:7], fcol[:, 3:4],
                               1.0 / D, EPS, OP.mult, OP.add,
                               label=f"f_v0_{t}")
                        VE.doi("tensor_scalar", fcol[:, 4:5], fcol[:, 2:3],
                               1.0 / D, None, OP.mult, label=f"f_u_{t}")
                        SE.w(f"f_u_{t}")
                        SE.doi("activation", fcol[:, 5:6], fcol[:, 4:5],
                               AF.Square, label=f"f_q2_{t}")
                        VE.w(f"f_q2_{t}")
                        VE.w(f"f_v0_{t}")
                        VE.doi("tensor_tensor", fcol[:, 6:7], fcol[:, 6:7],
                               fcol[:, 5:6], OP.subtract, label=f"f_var_{t}")
                        SE.w(f"f_var_{t}")
                        SE.doi("sqrt", fcol[:, 7:8], fcol[:, 6:7],
                               label=f"f_sd_{t}")
                        VE.w(f"f_sd_{t}")
                        VE.doi("reciprocal", fcol[:, 7:8], fcol[:, 7:8],
                               label=f"f_rc_{t}")
                        VE.w(f"f_rc_{t}")
                        VE.doi("scalar_tensor_tensor", fcol[:, 8:9],
                               fcol[:, 4:5], -1.0, fcol[:, 7:8],
                               OP.mult, OP.mult, label=f"f_mb_{t}")
                        SE.w(f"f_mb_{t}")
                        if t >= 2:
                            SE.w(f"f_out_{t-2}")
                        SE.doi("activation", outb[t % 2][:, :], Sf[t % 2][:, :],
                               AF.Identity, scale=fcol[:, 7:8], bias=fcol[:, 8:9],
                               label=f"f_o_{t}")
                        b_, tt = divmod(t, NT)
                        SY.w(f"f_o_{t}")
                        SY.dma(out_d[b_, tt * 128:(tt + 1) * 128, :],
                               outb[t % 2][:, :], label=f"f_out_{t}",
                               sem="dso")

                    loads(0)
                    loads(1)
                    quant(0)
                    for t in range(NTT):
                        trq(t)
                        if t + 1 < NTT:
                            quant(t + 1)
                        ffnmm(t)
                        final(t)
                        if t + 2 < NTT:
                            loads(t + 2)
                    SY.w(f"f_out_{NTT-1}")
                    SY.w(f"f_out_{NTT-2}")

                emit_block(body)

        if STOP_AFTER is None:
            ffn_phase()
        stf.close()

    return nc


def kernel(L0_attn_H, L0_attn_W_qkv, L0_attn_b_qkv, L0_attn_W_proj,
           L0_attn_b_proj, L0_attn_gamma, L0_attn_beta, L0_ffn_weight,
           L0_ffn_gamma, L0_ffn_beta):
    B, S, D_ = L0_attn_H.shape
    nc = build_nc(S=S)
    shared = {
        "wqkvT": np.ascontiguousarray(np.asarray(L0_attn_W_qkv, np.float32).T),
        "bqkv": np.asarray(L0_attn_b_qkv, np.float32).reshape(1, -1),
        "wprojT": np.ascontiguousarray(np.asarray(L0_attn_W_proj, np.float32).T),
        "bproj": np.asarray(L0_attn_b_proj, np.float32).reshape(1, -1),
        "g1": np.asarray(L0_attn_gamma, np.float32).reshape(1, -1),
        "be1": np.asarray(L0_attn_beta, np.float32).reshape(1, -1),
        "wffnT": np.ascontiguousarray(np.asarray(L0_ffn_weight, np.float32).T),
        "g2": np.asarray(L0_ffn_gamma, np.float32).reshape(1, -1),
        "be2": np.asarray(L0_ffn_beta, np.float32).reshape(1, -1),
        "idn": np.eye(128, dtype=np.float32),
    }
    H = np.ascontiguousarray(np.asarray(L0_attn_H, np.float32))
    in_maps = []
    for c in range(NCORES):
        m = dict(shared)
        m["h"] = H[c * BPC:(c + 1) * BPC]
        in_maps.append(m)
    res = run_bass_kernel_spmd(nc, in_maps, core_ids=list(range(NCORES)))
    out = np.concatenate([res.results[c]["out"] for c in range(NCORES)], axis=0)
    return np.ascontiguousarray(out.astype(np.float32))
